# version 14
# baseline (speedup 1.0000x reference)
import numpy as np
import ml_dtypes
import concourse.bass as bass
import concourse.mybir as mybir
from concourse.bass_utils import run_bass_kernel_spmd

F32 = mybir.dt.float32
BF16 = mybir.dt.bfloat16
AF = mybir.ActivationFunctionType
OP = mybir.AluOpType
AX = mybir.AxisListType

E = 2048
ECN = 16
C = 4096
CCN = 32
HD = 128
H = 8192
HCN = 64
GU = 16384
D = 2048
DCN = 16
T = 1024
W = 128
TW = T + W
NCORES = 8
TPS = [(0, W, False), (W, 512, True), (W + 512, 512, True)]

BF16_EPS8 = float(np.float32(ml_dtypes.bfloat16(1e-8)))
THR = 0.50048828125
LN_EPS = 1e-5
ATTN_EPS = 1e-6
FFN_EPS = 1.1920928955078125e-07

ENGS = ("sync", "scalar", "vector", "tensor", "gpsimd")


class _DummyInstr:
    def then_inc(self, *a, **k):
        return self


class _DryEng:
    def __getattr__(self, name):
        return lambda *a, **k: _DummyInstr()


class Ctx:

    def __init__(self, nc, block):
        self.nc = nc
        self.block = block
        self.sems = {}
        self.cnt = {}
        self.lastw = {}
        self._live = []
        self.dry = False

    def sem(self, name):
        cm = self.nc.semaphore(name)
        self._live.append(cm)
        self.sems[name] = cm.__enter__()
        self.cnt[name] = 0

    def close_sems(self):
        for cm in reversed(self._live):
            cm.__exit__(None, None, None)

    def wait(self, eng, engname, name, val):
        if val is None:
            return
        v = val
        if v <= 0:
            return
        key = (engname, name)
        if self.lastw.get(key, -1) >= v:
            return
        if not self.dry:
            eng.wait_ge(self.sems[name], v)
        self.lastw[key] = v

    def sw(self, eng, engname, name):
        self.wait(eng, engname, name, self.cnt[name])

    def inc(self, name, instr, k=1):
        if not self.dry:
            instr.then_inc(self.sems[name], k)
        self.cnt[name] += k
        return self.cnt[name]

    def phase(self, bodies):
        snap_cnt = dict(self.cnt)
        snap_w = dict(self.lastw)
        self.dry = True
        de = _DryEng()
        for name, fn in bodies:
            fn(self, de)
        dry_cnt = dict(self.cnt)
        self.cnt = snap_cnt
        self.lastw = snap_w
        self.dry = False
        tagn = getattr(self, "_tag", None)
        for name, fn in bodies:
            if tagn:
                with self.nc.named_scope(tagn):
                    getattr(self.block, name)(lambda eng, fn=fn: fn(self, eng))
            else:
                getattr(self.block, name)(lambda eng, fn=fn: fn(self, eng))
        assert self.cnt == dry_cnt, (self.cnt, dry_cnt)

    def barrier(self, tag, extra=None, engines=ENGS):
        tgt = self.cnt["ph"] + len(engines)
        extra = extra or {}
        def bfn(nm):
            def f(cx, e):
                for s in extra.get(nm, ()):
                    cx.wait(e, nm, s, cx.cnt[s])
                if not cx.dry:
                    if nm in ("scalar", "vector", "tensor"):
                        e.drain()
                    e.sem_inc(cx.sems["ph"], 1)
                    e.wait_ge(cx.sems["ph"], tgt)
                cx.lastw[(nm, "ph")] = tgt
            return f
        for nm in engines:
            getattr(self.block, nm)(lambda eng, nm=nm: bfn(nm)(self, eng))
            self.lastw[(nm, "ph")] = tgt
        self.cnt["ph"] = tgt


class Alloc:
    def __init__(self, nc):
        self.nc = nc
        self.cms = {}
        self.order = {}
        self.n = 0

    def sb(self, name, shape, dtype, side="left"):
        cm = self.nc.sbuf_tensor(name, shape, dtype, side=side)
        h = cm.__enter__()
        self.cms[name] = cm
        self.order[name] = self.n
        self.n += 1
        return h

    def ps(self, name, shape, dtype):
        cm = self.nc.psum_tensor(name, shape, dtype)
        h = cm.__enter__()
        self.cms[name] = cm
        self.order[name] = self.n
        self.n += 1
        return h

    def free(self, *names):
        for n in sorted(names, key=lambda x: -self.order[x]):
            self.cms.pop(n).__exit__(None, None, None)


def build_program(stage="full"):
    lvl = {"xn": 1, "attn": 2, "partial": 3, "full": 5}[stage]
    nc = bass.Bass("TRN2", target_bir_lowering=False, debug=False)

    x_tm = nc.dram_tensor("x_tm", [TW, E], F32, kind="ExternalInput").ap()
    x_fm = nc.dram_tensor("x_fm", [E, T], F32, kind="ExternalInput").ap()
    wq1 = nc.dram_tensor("wq1", [E, HD], F32, kind="ExternalInput").ap()
    wq2 = nc.dram_tensor("wq2", [HD, C], F32, kind="ExternalInput").ap()
    wk1 = nc.dram_tensor("wk1", [E, HD], F32, kind="ExternalInput").ap()
    wk2 = nc.dram_tensor("wk2", [HD, C], F32, kind="ExternalInput").ap()
    wv = nc.dram_tensor("wv", [E, C], F32, kind="ExternalInput").ap()
    wo = nc.dram_tensor("wo", [C, D], F32, kind="ExternalInput").ap()
    nrm = nc.dram_tensor("nrm", [C], F32, kind="ExternalInput").ap()
    wgu_sl = nc.dram_tensor("wgu_sl", [GU // NCORES, E], F32, kind="ExternalInput").ap()
    wp_sl = nc.dram_tensor("wp_sl", [D // NCORES, H], F32, kind="ExternalInput").ap()

    if lvl >= 5:
        y = nc.dram_tensor("y", [D, T], F32, kind="ExternalOutput").ap()
    if stage == "xn":
        dbg_xn = nc.dram_tensor("dbg_xn", [E, TW], BF16, kind="ExternalOutput").ap()
    if stage == "attn":
        dbg_ow = nc.dram_tensor("dbg_ow", [C, T], BF16, kind="ExternalOutput").ap()
        dbg_st = nc.dram_tensor("dbg_st", [1, T], F32, kind="ExternalOutput").ap()
    if stage == "partial":
        dbg_pt = nc.dram_tensor("dbg_pt", [D, T], F32, kind="ExternalOutput").ap()

    ag_gu_in = [nc.dram_tensor(f"ag_gu_in{i}", [512, E], BF16) for i in range(4)]
    ag_gu_out = [nc.dram_tensor(f"ag_gu_out{i}", [NCORES * 512, E], BF16)
                 for i in range(4)]
    ag_wp_in = nc.dram_tensor("ag_wp_in", [D // NCORES, H], BF16)
    ag_wp_out = nc.dram_tensor("ag_wp_out", [NCORES * (D // NCORES), H], BF16)
    partial = nc.dram_tensor("partial", [D, T], F32)
    wv_b = nc.dram_tensor("wv_b", [CCN * 128, E], BF16)
    wo_b = nc.dram_tensor("wo_b", [DCN * 128, C], BF16)

    al = Alloc(nc)
    nrm_sb = al.sb("nrm_sb", [128, CCN], F32, side="right")
    ones_c = al.sb("ones_c", [128, 1], BF16, side="right")
    ones_r = al.sb("ones_r", [1, 128], F32, side="right")
    epsc = al.sb("epsc", [128, 1], F32, side="right")
    cst = al.sb("cst", [1, 4], F32, side="right")
    xn_mn = al.sb("xn_mn", [128, ECN, T], BF16)
    xn_wm = al.sb("xn_wm", [128, ECN, W], BF16)
    wq1_sb = al.sb("wq1_sb", [128, ECN, HD], BF16)
    wk1_sb = al.sb("wk1_sb", [128, ECN, HD], BF16)
    wq2_sb = al.sb("wq2_sb", [128, C], BF16)
    wk2_sb = al.sb("wk2_sb", [128, C], BF16)
    t1_sb = al.sb("t1_sb", [128, T], BF16)
    t2_sb = al.sb("t2_sb", [128, TW], BF16)

    with nc.Block() as block:
        cx = Ctx(nc, block)
        for s in ("ph", "pe", "sc", "ve", "ldsm", "st_y", "cvb0", "cvb1", "cvb2", "cvb3", "cob",
                  "st_ag0", "st_ag1", "st_wp0", "st_wp1", "st_y0", "st_y1",
                  "ccg0", "ccg1", "ccg2", "ccg3", "ccwp",
                  "trs_q0", "trs_q1", "trs_w0", "trs_w1"):
            cx.sem(s)
        for s in ("ld_x", "cast", "cwp", "trs_ln", "ld_wv", "ld_wo",
                  "ld_wgu", "ld_wp", "ld_xf", "ld_pt", "st_py"):
            cx.sem(s + "0")
            cx.sem(s + "1")

        NGQ, NPQ = 8, 4
        wgu_buf = al.sb("wgu_buf", [128, 2, 2, E], BF16)
        wp_buf = al.sb("wp_buf", [128, 2, 4096], BF16)
        pos_sb = al.sb("pos_sb", [128, 4096], BF16)
        neg_sb = al.sb("neg_sb", [128, 4096], BF16)
        red_sb = al.sb("red_sb", [128, 64], F32)
        scl_sb = al.sb("scl_sb", [128, 64], BF16)
        thr_sb = al.sb("thr_sb", [128, 64], F32)
        nthr_sb = al.sb("nthr_sb", [128, 64], F32)
        tt_sb = al.sb("tt_sb", [128, 2, 16, 128], BF16)
        tt_wp = al.sb("tt_wp", [128, 2, 32, 128], BF16)
        xt_sb = al.sb("xt_sb", [128, 2, E], F32)
        xnt_sb = al.sb("xnt_sb", [128, 2, E], BF16)
        sum_sb = al.sb("sum_sb", [128, 2, 1], F32)
        ssq_sb = al.sb("ssq_sb", [128, 2, 1], F32)
        mean_sb = al.sb("mean_sb", [128, 2, 1], F32)
        var_sb = al.sb("var_sb", [128, 2, 1], F32)
        sd_sb = al.sb("sd_sb", [128, 2, 1], F32)
        inv_sb = al.sb("inv_sb", [128, 2, 1], F32)

        def xn_sl(ec, off, sz):
            if off < W:
                assert off + sz <= W
                return xn_wm[:, ec, off:off + sz]
            return xn_mn[:, ec, off - W:off - W + sz]

        dd = {}
        NTS = TW // 128

        def quant_ops(cx, v, wbuf, G):
            view = wbuf.rearrange("p (g k) -> p g k", k=64)
            rd = red_sb[:, 0:G]
            sc_ = scl_sb[:, 0:G]
            th = thr_sb[:, 0:G]
            nth = nthr_sb[:, 0:G]
            pv = pos_sb[:, 0:G * 64].rearrange("p (g k) -> p g k", k=64)
            nv = neg_sb[:, 0:G * 64].rearrange("p (g k) -> p g k", k=64)
            cx.inc("ve", v.tensor_reduce(rd, view, AX.X, OP.add,
                                          apply_absolute_value=True))
            cx.sw(v, "vector", "ve")
            cx.inc("ve", v.tensor_scalar(sc_, rd, 1.0 / 64.0, None, OP.mult))
            cx.sw(v, "vector", "ve")
            cx.inc("ve", v.tensor_scalar_max(sc_, sc_, BF16_EPS8))
            cx.sw(v, "vector", "ve")
            v.tensor_scalar(th, sc_, THR, None, OP.mult)
            cx.inc("ve", v.tensor_scalar(nth, sc_, -THR, None, OP.mult))
            cx.sw(v, "vector", "ve")
            v.tensor_tensor(pv, view, th.unsqueeze(2).broadcast_to([128, G, 64]),
                            OP.is_gt)
            cx.inc("ve", v.tensor_tensor(
                nv, view, nth.unsqueeze(2).broadcast_to([128, G, 64]), OP.is_lt))
            cx.sw(v, "vector", "ve")
            cx.inc("ve", v.tensor_tensor(pv, pv, nv, OP.subtract))
            cx.sw(v, "vector", "ve")
            return v.tensor_tensor(view, pv,
                                   sc_.unsqueeze(2).broadcast_to([128, G, 64]),
                                   OP.mult)

        def g_prep(cx, g):
            i = cx.inc
            i("ldsm", g.dma_start(out=wq1_sb[:, :, :],
                                  in_=wq1.rearrange("(a p) m -> p a m", p=128)), 16)
            i("ldsm", g.dma_start(out=wk1_sb[:, :, :],
                                  in_=wk1.rearrange("(a p) m -> p a m", p=128)), 16)
            i("ldsm", g.dma_start(out=wq2_sb[:, :], in_=wq2), 16)
            i("ldsm", g.dma_start(out=wk2_sb[:, :], in_=wk2), 16)
            i("ldsm", g.dma_start(out=nrm_sb[:, :],
                                  in_=nrm.rearrange("(a p) -> p a", p=128),
                                  allow_slow_non_contiguous=True), 16)
            dd["ldsm"] = cx.cnt["ldsm"]
            for q in range(NGQ):
                p = q % 2
                if q >= 2:
                    cx.wait(g, "gpsimd", "trs_q0", dd.get(("gu_trs0", q - 2)))
                    cx.wait(g, "gpsimd", "trs_q1", dd.get(("gu_trs1", q - 2)))
                i(f"cast{p}", g.dma_start(
                    out=wgu_buf[:, p, :, :],
                    in_=wgu_sl[q * 256:(q + 1) * 256, :]
                        .rearrange("(hc p) d -> p hc d", p=128)), 16)
            for q in range(4):
                p = q % 2
                if q >= 2:
                    cx.wait(g, "gpsimd", "trs_w0", dd.get(("wp_trs0", q - 2)))
                    cx.wait(g, "gpsimd", "trs_w1", dd.get(("wp_trs1", q - 2)))
                dcl, half = q // 2, q % 2
                i(f"cwp{p}", g.dma_start(
                    out=wp_buf[:, p, :],
                    in_=wp_sl[dcl * 128:(dcl + 1) * 128,
                              half * 4096:(half + 1) * 4096]), 16)


        def v_memset(cx, v):
            if not cx.dry:
                v.memset(ones_c[:, :], 1.0)
                v.memset(ones_r[:, :], 1.0)
                v.memset(epsc[:, :], LN_EPS)
                v.memset(cst[0:1, 0:1], ATTN_EPS)
                v.memset(cst[0:1, 1:2], 1.0 / C)
                v.memset(cst[0:1, 2:3], FFN_EPS)
                v.memset(cst[0:1, 3:4], 1.0 / H)

        def v_prep(cx, v):
            for q in range(NGQ):
                p = q % 2
                cx.wait(v, "vector", f"cast{p}", 16 * (q // 2 + 1))
                dd[("gu_q", q)] = cx.inc(
                    "ve", quant_ops(cx, v, wgu_buf[:, p, :, :]
                                    .rearrange("p a d -> p (a d)"), 64))
            for q in range(4):
                p = q % 2
                cx.wait(v, "vector", f"cwp{p}", 16 * (q // 2 + 1))
                dd[("wp_q", q)] = cx.inc("ve", quant_ops(cx, v, wp_buf[:, p, :], 64))

        def s_prep(cx, s):
            for q in range(NGQ):
                p = q % 2
                cx.wait(s, "sync", "ve", dd[("gu_q", q)])
                for i2 in range(2):
                    hcl = q * 2 + i2
                    P, col = hcl // 4, hcl % 4
                    pt = hcl % 2
                    if hcl >= 2:
                        cx.wait(s, "sync", f"st_ag{pt}", dd.get(("gu_st", hcl - 2)))
                    cx.inc(f"trs_q{pt}", s.dma_start_transpose(
                        tt_sb[:, pt, :, :],
                        wgu_buf[:, p, i2, :].rearrange("p (dc r) -> p dc r", r=128)),
                        16)
                    dd[("gu_trs" + str(pt), q)] = cx.cnt[f"trs_q{pt}"]
                    cx.wait(s, "sync", f"trs_q{pt}", cx.cnt[f"trs_q{pt}"])
                    dd[("gu_st", hcl)] = cx.inc(f"st_ag{pt}", s.dma_start(
                        out=ag_gu_in[P].ap()[col * 128:(col + 1) * 128, :]
                            .rearrange("p (ec h) -> p ec h", h=128),
                        in_=tt_sb[:, pt, :, :]), 16)
                    if hcl % 4 == 3:
                        dd[("gu_ag0", P)] = cx.cnt["st_ag0"]
                        dd[("gu_ag1", P)] = cx.cnt["st_ag1"]
            for q in range(4):
                p = q % 2
                dcl, half = q // 2, q % 2
                pt = q % 2
                cx.wait(s, "sync", "ve", dd[("wp_q", q)])
                if q >= 2:
                    cx.wait(s, "sync", f"st_wp{pt}", dd.get(("wp_st", q - 2)))
                cx.inc(f"trs_w{pt}", s.dma_start_transpose(
                    tt_wp[:, pt, :, :],
                    wp_buf[:, p, :].rearrange("p (hc r) -> p hc r", r=128)), 16)
                dd[("wp_trs" + str(pt), q)] = cx.cnt[f"trs_w{pt}"]
                cx.wait(s, "sync", f"trs_w{pt}", cx.cnt[f"trs_w{pt}"])
                dd[("wp_st", q)] = cx.inc(f"st_wp{pt}", s.dma_start(
                    out=ag_wp_in.ap()[dcl * 128:(dcl + 1) * 128,
                                      half * 4096:(half + 1) * 4096]
                        .rearrange("p (hc d) -> p hc d", d=128),
                    in_=tt_wp[:, pt, :, :]), 16)
                dd[("wp_ag0", q)] = cx.cnt["st_wp0"]
                dd[("wp_ag1", q)] = cx.cnt["st_wp1"]

        def sc_ln(cx, e):
            def process(ts):
                cx.wait(e, "scalar", "ve", dd.get(("ln_var", ts)))
                dd[("ln_sd", ts)] = cx.inc("sc", e.activation(
                    out=sd_sb[:, ts % 2, :], in_=var_sb[:, ts % 2, :],
                    func=AF.Sqrt, bias=epsc[:, 0:1]))
            for ts in range(NTS):
                p = ts % 2
                if ts >= 2:
                    cx.wait(e, "scalar", "ve", dd.get(("ln_xn", ts - 2)))
                dd[("ln_x", ts)] = cx.inc(f"ld_x{p}", e.dma_start(
                    out=xt_sb[:, p, :], in_=x_tm[ts * 128:(ts + 1) * 128, :]), 16)
                if ts >= 1:
                    process(ts - 1)
            process(NTS - 1)

        def v_ln(cx, v):
            for ts in range(NTS):
                p = ts % 2
                cx.wait(v, "vector", f"ld_x{p}", dd[("ln_x", ts)])
                if ts >= 2:
                    cx.wait(v, "vector", f"trs_ln{p}", dd.get(("ln_tr", ts - 2)))
                cx.inc("ve", v.scalar_tensor_tensor(
                    xnt_sb[:, p, :], xt_sb[:, p, :], 1.0, xt_sb[:, p, :],
                    OP.mult, OP.mult, accum_out=ssq_sb[:, p, :]))
                cx.inc("ve", v.reduce_sum(sum_sb[:, p, :], xt_sb[:, p, :], AX.X))
                cx.sw(v, "vector", "ve")
                cx.inc("ve", v.tensor_scalar(mean_sb[:, p, :], sum_sb[:, p, :],
                                             1.0 / E, None, OP.mult))
                v.tensor_scalar(var_sb[:, p, :], ssq_sb[:, p, :], 1.0 / E,
                                None, OP.mult)
                cx.sw(v, "vector", "ve")
                cx.inc("ve", v.tensor_tensor(inv_sb[:, p, :], mean_sb[:, p, :],
                                             mean_sb[:, p, :], OP.mult))
                cx.sw(v, "vector", "ve")
                dd[("ln_var", ts)] = cx.inc("ve", v.tensor_tensor(
                    var_sb[:, p, :], var_sb[:, p, :], inv_sb[:, p, :],
                    OP.subtract))
                cx.wait(v, "vector", "sc", dd[("ln_sd", ts)])
                cx.inc("ve", v.reciprocal(inv_sb[:, p, :], sd_sb[:, p, :]))
                cx.sw(v, "vector", "ve")
                dd[("ln_xn", ts)] = cx.inc("ve", v.tensor_scalar(
                    xnt_sb[:, p, :], xt_sb[:, p, :], mean_sb[:, p, :],
                    inv_sb[:, p, :], OP.subtract, OP.mult))

        def s_lntr(cx, s):
            for ts in range(NTS):
                p = ts % 2
                cx.wait(s, "sync", "ve", dd[("ln_xn", ts)])
                nw = W // 128
                if ts < nw:
                    dst = xn_wm[:, :, ts * 128:(ts + 1) * 128]
                else:
                    dst = xn_mn[:, :, (ts - nw) * 128:(ts - nw + 1) * 128]
                cx.inc(f"trs_ln{p}", s.dma_start_transpose(
                    dst,
                    xnt_sb[:, p, :].rearrange("p (ec r) -> p ec r", r=128)), 16)
                dd[("ln_tr", ts)] = cx.cnt[f"trs_ln{p}"]

        cx._tag = "p0ln"
        cx.phase([("vector", v_memset), ("scalar", sc_ln), ("vector", v_ln),
                  ("sync", s_lntr), ("vector", v_prep), ("sync", s_prep),
                  ("gpsimd", g_prep)])

        xn_rdy0 = cx.cnt["trs_ln0"]
        xn_rdy1 = cx.cnt["trs_ln1"]

        if stage == "xn":
            def s_dump(cx, s):
                cx.wait(s, "sync", "trs_ln0", xn_rdy0)
                cx.wait(s, "sync", "trs_ln1", xn_rdy1)
                for ec in range(ECN):
                    cx.inc("st_y", s.dma_start(
                        out=dbg_xn[ec * 128:(ec + 1) * 128, 0:W],
                        in_=xn_wm[:, ec, :]), 16)
                    cx.inc("st_y", s.dma_start(
                        out=dbg_xn[ec * 128:(ec + 1) * 128, W:TW],
                        in_=xn_mn[:, ec, :]), 16)
                if not cx.dry:
                    s.wait_ge(cx.sems["st_y"], cx.cnt["st_y"])
            cx.phase([("sync", s_dump)])
            cx.close_sems()
            return nc

        cx.barrier("b1", extra={
            "sync": ["trs_ln0", "trs_ln1", "st_ag0", "st_ag1", "st_wp0",
                     "st_wp1", "trs_q0", "trs_q1", "trs_w0", "trs_w1"],
            "gpsimd": ["ldsm", "cast0", "cast1", "cwp0", "cwp1"],
            "scalar": ["ld_x0", "ld_x1"]})
        al.free("wgu_buf", "wp_buf", "pos_sb", "neg_sb", "red_sb", "scl_sb",
                "thr_sb", "nthr_sb", "tt_sb", "tt_wp", "xt_sb", "xnt_sb",
                "sum_sb", "ssq_sb", "mean_sb", "var_sb", "sd_sb", "inv_sb")

        outw = al.sb("outw", [128, CCN, T], BF16)
        wv_t = al.sb("wv_t", [128, 2, ECN, 128], BF16)
        wo_t = al.sb("wo_t", [128, 2, CCN, 128], BF16)
        rowa = al.sb("rowa", [1, T], F32)
        rowb = al.sb("rowb", [1, T], F32)
        rms_a = al.sb("rms_a", [128, T], F32)
        f_sb = al.sb("f_sb", [128, 2, 512], F32)
        a_sb = al.sb("a_sb", [128, 2, 512], F32)
        k_sb = al.sb("k_sb", [128, 2, 512], F32)
        b_sb = al.sb("b_sb", [128, 2, 512], F32)
        h_sb = al.sb("h_sb", [128, 2, 512], F32)
        q_sb = al.sb("q_sb", [128, 2, 512], F32)
        o_sb = al.sb("o_sb", [128, 2, 512], F32)
        sq_sb = al.sb("sq_sb", [128, 2, 512], BF16)

        logf_ps = al.ps("logf_ps", [128, 2, 512], F32)
        v_ps = al.ps("v_ps", [128, 2, 512], F32)
        q_ps = al.ps("q_ps", [128, 2, 512], F32)
        stat_ps = al.ps("stat_ps", [1, T], F32)

        steps = [(cc, tpi) for cc in range(CCN) for tpi in range(3)]
        qsteps = set(s for s in steps if TPS[s[1]][2])

        def pe_t12(cx, t):
            cx.wait(t, "tensor", "ldsm", dd["ldsm"])
            cx.wait(t, "tensor", "trs_ln0", xn_rdy0)
            cx.wait(t, "tensor", "trs_ln1", xn_rdy1)
            gi = 0
            for (name, wsb, pieces) in (
                    ("t2", wk1_sb, TPS),
                    ("t1", wq1_sb, [TPS[1], TPS[2]])):
                for pi, (off, sz, _) in enumerate(pieces):
                    pg = gi % 2
                    if gi >= 2:
                        cx.wait(t, "tensor", "ve", dd.get(("t12ev", gi - 2)))
                    ps = logf_ps[:, pg, 0:sz]
                    for ec in range(ECN):
                        m = t.matmul(ps, wsb[:, ec, :], xn_sl(ec, off, sz),
                                     start=(ec == 0), stop=(ec == ECN - 1))
                    dd[("t12mm", gi)] = cx.inc("pe", m)
                    gi += 1

        def v_t12(cx, v):
            gi = 0
            for (name, dst, pieces) in (
                    ("t2", t2_sb, TPS),
                    ("t1", t1_sb, [TPS[1], TPS[2]])):
                for pi, (off, sz, _) in enumerate(pieces):
                    pg = gi % 2
                    cx.wait(v, "vector", "pe", dd[("t12mm", gi)])
                    doff = off if name == "t2" else off - W
                    dd[("t12ev", gi)] = cx.inc("ve", v.tensor_copy(
                        dst[:, doff:doff + sz], logf_ps[:, pg, 0:sz]))
                    dd[(name + "ev", pi)] = dd[("t12ev", gi)]
                    gi += 1

        def g_post(cx, g):
            i = cx.inc
            def ag(P):
                if P < NPQ:
                    cx.wait(g, "gpsimd", "st_ag0", dd[("gu_ag0", P)])
                    cx.wait(g, "gpsimd", "st_ag1", dd[("gu_ag1", P)])
                    i(f"ccg{P}", g.collective_compute(
                        "AllGather", OP.bypass,
                        replica_groups=[list(range(NCORES))],
                        ins=[ag_gu_in[P].ap().opt()],
                        outs=[ag_gu_out[P].ap().opt()]), 1)
                else:
                    cx.wait(g, "gpsimd", "st_wp0", dd[("wp_ag0", 3)])
                    cx.wait(g, "gpsimd", "st_wp1", dd[("wp_ag1", 3)])
                    i("ccwp", g.collective_compute(
                        "AllGather", OP.bypass,
                        replica_groups=[list(range(NCORES))],
                        ins=[ag_wp_in.ap().opt()],
                        outs=[ag_wp_out.ap().opt()]), 1)
            for P in range(NPQ):
                ag(P)
            for cc in range(CCN):
                i(f"cvb{cc // 8}", g.dma_start(
                    out=wv_b.ap()[cc * 128:(cc + 1) * 128, :]
                        .rearrange("p (ec c) -> p ec c", c=128),
                    in_=wv[:, cc * 128:(cc + 1) * 128]
                        .rearrange("(ec p) c -> p ec c", p=128)), 16)
            for dc in range(DCN):
                i("cob", g.dma_start(
                    out=wo_b.ap()[dc * 128:(dc + 1) * 128, :]
                        .rearrange("p (cc d) -> p cc d", d=128),
                    in_=wo[:, dc * 128:(dc + 1) * 128]
                        .rearrange("(cc p) d -> p cc d", p=128)), 16)
            ag(NPQ)

        def s_wv(cx, s):
            for cc in range(CCN):
                p = cc % 2
                cx.wait(s, "sync", f"cvb{cc // 8}", 128)
                if cc >= 2:
                    cx.wait(s, "sync", "pe", dd.get(("vmm", cc - 2, 2)))
                dd[("wvld", cc)] = cx.inc(f"ld_wv{p}", s.dma_start(
                    out=wv_t[:, p, :, :],
                    in_=wv_b.ap()[cc * 128:(cc + 1) * 128, :]
                        .rearrange("p (ec c) -> p ec c", c=128)), 16)

        def pe_cc(cx, t):
            last_par = {}
            cx.wait(t, "tensor", "ve", dd.get(("t12ev", 4)))
            si = 0
            pend = []
            for cc in range(CCN):
                pwv = cc % 2
                cx.wait(t, "tensor", f"ld_wv{pwv}", dd[("wvld", cc)])
                for tpi, (off, sz, hasq) in enumerate(TPS):
                    s = (cc, tpi)
                    p = si % 2
                    if si >= 2:
                        cx.wait(t, "tensor", "sc", dd.get(("f", steps[si - 2])))
                    cx.wait(t, "tensor", "ve", dd[("t2ev", tpi)])
                    m = t.matmul(logf_ps[:, p, 0:sz],
                                 wk2_sb[:, cc * 128:(cc + 1) * 128],
                                 t2_sb[:, off:off + sz], start=True, stop=True)
                    dd[("logf", s)] = cx.inc("pe", m)
                    if si >= 2:
                        cx.wait(t, "tensor", "ve", dd.get(("b", steps[si - 2])))
                    for ec in range(ECN):
                        m = t.matmul(v_ps[:, p, 0:sz], wv_t[:, pwv, ec, :],
                                     xn_sl(ec, off, sz),
                                     start=(ec == 0), stop=(ec == ECN - 1))
                    dd[("vmm", cc, tpi)] = cx.inc("pe", m)
                    if hasq:
                        prevq = last_par.get(("qps", p))
                        if prevq is not None:
                            cx.wait(t, "tensor", "ve", dd.get(("o", prevq)))
                        last_par[("qps", p)] = s
                        cx.wait(t, "tensor", "ve", dd[("t1ev", tpi - 1)])
                        m = t.matmul(q_ps[:, p, 0:sz],
                                     wq2_sb[:, cc * 128:(cc + 1) * 128],
                                     t1_sb[:, off - W:off - W + sz],
                                     start=True, stop=True)
                        dd[("qmm", s)] = cx.inc("pe", m)
                        pend.append((s, p))
                    if len(pend) > 2:
                        stat_emit(cx, t, *pend.pop(0))
                    si += 1
            for it in pend:
                stat_emit(cx, t, *it)

        def stat_emit(cx, t, s, p):
            cc, tpi = s
            off = TPS[tpi][0] - W
            cx.wait(t, "tensor", "ve", dd.get(("sq", s)))
            m = t.matmul(stat_ps[0:1, off:off + 512], ones_c[:, :],
                         sq_sb[:, p, :], start=(cc == 0), stop=(cc == CCN - 1))
            dd[("stat", s)] = cx.inc("pe", m)

        def sc_cc(cx, e):
            last_par = {}
            si = 0
            pend = []
            for cc in range(CCN):
                for tpi, (off, sz, hasq) in enumerate(TPS):
                    s = (cc, tpi)
                    p = si % 2
                    cx.wait(e, "scalar", "pe", dd[("logf", s)])
                    if si >= 2 and steps[si - 2] in qsteps:
                        cx.wait(e, "scalar", "ve", dd.get(("o", steps[si - 2])))
                    dd[("f", s)] = cx.inc("sc", e.activation(
                        out=f_sb[:, p, 0:sz], in_=logf_ps[:, p, 0:sz],
                        func=AF.Sigmoid))
                    cx.sw(e, "scalar", "sc")
                    if si >= 2:
                        cx.wait(e, "scalar", "ve", dd.get(("scan", steps[si - 2])))
                    dd[("a", s)] = cx.inc("sc", e.activation(
                        out=a_sb[:, p, 0:sz], in_=f_sb[:, p, 0:sz],
                        func=AF.Sigmoid))
                    if hasq:
                        cx.wait(e, "scalar", "pe", dd[("qmm", s)])
                        prevq = last_par.get(("qsb", p))
                        if prevq is not None:
                            cx.wait(e, "scalar", "ve", dd.get(("o1", prevq)))
                        last_par[("qsb", p)] = s
                        dd[("sig", s)] = cx.inc("sc", e.activation(
                            out=q_sb[:, p, 0:sz], in_=q_ps[:, p, 0:sz],
                            func=AF.Sigmoid))
                    si += 1

        def v_cc(cx, v):
            last_par = {}
            si = 0
            for cc in range(CCN):
                for tpi, (off, sz, hasq) in enumerate(TPS):
                    s = (cc, tpi)
                    p = si % 2
                    cx.wait(v, "vector", "sc", dd[("a", s)])
                    cx.inc("ve", v.tensor_scalar(k_sb[:, p, 0:sz],
                                                 a_sb[:, p, 0:sz],
                                                 -1.0, 1.0, OP.mult, OP.add))
                    cx.sw(v, "vector", "ve")
                    cx.wait(v, "vector", "pe", dd[("vmm", cc, tpi)])
                    dd[("b", s)] = cx.inc("ve", v.tensor_tensor(
                        b_sb[:, p, 0:sz], k_sb[:, p, 0:sz], v_ps[:, p, 0:sz],
                        OP.mult))
                    cx.sw(v, "vector", "ve")
                    if tpi == 0:
                        init = 0.0
                    else:
                        psz = TPS[tpi - 1][1]
                        init = h_sb[:, 1 - p, psz - 1:psz]
                    dd[("scan", s)] = cx.inc("ve", v.tensor_tensor_scan(
                        h_sb[:, p, 0:sz], a_sb[:, p, 0:sz], b_sb[:, p, 0:sz],
                        init, OP.mult, OP.add))
                    cx.sw(v, "vector", "ve")
                    if hasq:
                        cx.wait(v, "vector", "sc", dd[("sig", s)])
                        dd[("o1", s)] = cx.inc("ve", v.tensor_tensor(
                            f_sb[:, p, 0:sz], q_sb[:, p, 0:sz],
                            h_sb[:, p, 0:sz], OP.mult))
                        cx.sw(v, "vector", "ve")
                        dd[("o", s)] = cx.inc("ve", v.tensor_tensor(
                            o_sb[:, p, 0:sz], f_sb[:, p, 0:sz],
                            q_ps[:, p, 0:sz], OP.mult))
                        cx.sw(v, "vector", "ve")
                        prevq = last_par.get(("sqb", p))
                        if prevq is not None:
                            cx.wait(v, "vector", "pe", dd.get(("stat", prevq)))
                        last_par[("sqb", p)] = s
                        dd[("sq", s)] = cx.inc("ve", v.tensor_tensor(
                            sq_sb[:, p, :], o_sb[:, p, 0:sz], o_sb[:, p, 0:sz],
                            OP.mult))
                        dd[("outw", s)] = cx.inc("ve", v.tensor_scalar(
                            outw[:, cc, off - W:off - W + sz], o_sb[:, p, 0:sz],
                            nrm_sb[:, cc:cc + 1], None, OP.mult))
                    si += 1

        cx._tag = "attn"
        cx.phase([("gpsimd", g_post),
                  ("tensor", pe_t12), ("vector", v_t12), ("sync", s_wv),
                  ("tensor", pe_cc), ("scalar", sc_cc), ("vector", v_cc)])

        if stage == "attn":
            def v_dump(cx, v):
                cx.wait(v, "vector", "pe", cx.cnt["pe"])
                dd["strow"] = cx.inc("ve", v.tensor_copy(rowa[0:1, :],
                                                         stat_ps[0:1, :]))
            def s_dumpa(cx, s):
                cx.wait(s, "sync", "ve", cx.cnt["ve"])
                for cc in range(CCN):
                    cx.inc("st_y", s.dma_start(
                        out=dbg_ow[cc * 128:(cc + 1) * 128, :],
                        in_=outw[:, cc, :]), 16)
                cx.inc("st_y", s.dma_start(out=dbg_st, in_=rowa[0:1, :]), 16)
                if not cx.dry:
                    s.wait_ge(cx.sems["st_y"], cx.cnt["st_y"])
            cx.phase([("vector", v_dump), ("sync", s_dumpa)])
            cx.close_sems()
            return nc

        def v_rmsa1(cx, v):
            cx.wait(v, "vector", "pe", cx.cnt["pe"])
            dd["rowa"] = cx.inc("ve", v.tensor_copy(rowa[0:1, :],
                                                    stat_ps[0:1, :]))
        def sc_rmsa(cx, e):
            cx.wait(e, "scalar", "ve", dd["rowa"])
            dd["rowsd"] = cx.inc("sc", e.activation(
                out=rowb[0:1, :], in_=rowa[0:1, :], func=AF.Sqrt,
                bias=cst[0:1, 0:1], scale=cst[0:1, 1:2]))
        def v_rmsa2(cx, v):
            cx.wait(v, "vector", "sc", dd["rowsd"])
            dd["rowinv"] = cx.inc("ve", v.reciprocal(rowa[0:1, :], rowb[0:1, :]))
        cx._tag = "rmsa"
        cx.phase([("vector", v_rmsa1), ("scalar", sc_rmsa),
                  ("vector", v_rmsa2)])

        al.free("logf_ps", "v_ps", "q_ps", "stat_ps")
        al.free("f_sb", "a_sb", "k_sb", "b_sb", "h_sb", "q_sb", "o_sb",
                "sq_sb")
        xf_t = al.sb("xf_t", [128, 2, T], F32)
        py_t = al.sb("py_t", [128, 2, T], F32)
        psum_a = al.ps("psum_a", [128, 2, T], F32)
        bc_ps = al.ps("bc_ps", [128, T], F32)

        def pe_bca(cx, t):
            cx.wait(t, "tensor", "ve", cx.cnt["ve"])
            cx.wait(t, "tensor", "sc", cx.cnt["sc"])
            for half in range(2):
                m = t.matmul(bc_ps[:, half * 512:(half + 1) * 512], ones_r[:, :],
                             rowa[0:1, half * 512:(half + 1) * 512],
                             start=True, stop=True)
            dd["bca"] = cx.inc("pe", m)
        def v_bca(cx, v):
            cx.wait(v, "vector", "pe", dd["bca"])
            dd["rmsabc"] = cx.inc("ve", v.tensor_copy(rms_a[:, :], bc_ps[:, :]))
        cx._tag = "rmsa"
        cx.phase([("tensor", pe_bca), ("vector", v_bca)])

        def wo_load(cx, s, dc):
            p = dc % 2
            if dc >= 2:
                cx.wait(s, "sync", "pe", dd.get(("atmm", dc - 2)))
            dd[("wold", dc)] = cx.inc(f"ld_wo{p}", s.dma_start(
                out=wo_t[:, p, :, :],
                in_=wo_b.ap()[dc * 128:(dc + 1) * 128, :]
                    .rearrange("p (cc d) -> p cc d", d=128)), 16)

        def s_atail(cx, s):
            cx.wait(s, "sync", "cob", 16 * DCN)
            wo_load(cx, s, 0)
            wo_load(cx, s, 1)
            cx.wait(s, "sync", "ve", dd.get("rowinv"))
            def load(dc):
                p = dc % 2
                if dc >= 2:
                    cx.wait(s, "sync", "ve", dd.get(("aev", dc - 2)))
                dd[("xfld", dc)] = cx.inc(f"ld_xf{p}", s.dma_start(
                    out=xf_t[:, p, :], in_=x_fm[dc * 128:(dc + 1) * 128, :]), 16)
            load(0)
            load(1)
            for dc in range(DCN):
                cx.wait(s, "sync", "ve", dd.get(("aev", dc)))
                cx.inc(f"st_py{dc % 2}", s.dma_start(
                    out=partial.ap()[dc * 128:(dc + 1) * 128, :],
                    in_=py_t[:, dc % 2, :]), 16)
                if dc + 2 < DCN:
                    wo_load(cx, s, dc + 2)
                    load(dc + 2)

        def pe_at(cx, t):
            for dc in range(DCN):
                p = dc % 2
                cx.wait(t, "tensor", f"ld_wo{p}", dd.get(("wold", dc)))
                if dc >= 2:
                    cx.wait(t, "tensor", "ve", dd.get(("aev", dc - 2)))
                for cc in range(CCN):
                    for half in range(2):
                        m = t.matmul(psum_a[:, p, half * 512:(half + 1) * 512],
                                     wo_t[:, p, cc, :],
                                     outw[:, cc, half * 512:(half + 1) * 512],
                                     start=(cc == 0), stop=(cc == CCN - 1))
                dd[("atmm", dc)] = cx.inc("pe", m)

        def v_at(cx, v):
            cx.wait(v, "vector", "ve", dd.get("rmsabc"))
            for dc in range(DCN):
                p = dc % 2
                cx.wait(v, "vector", "pe", dd[("atmm", dc)])
                cx.wait(v, "vector", f"ld_xf{p}", dd.get(("xfld", dc)))
                if dc >= 2:
                    cx.wait(v, "vector", f"st_py{p}", 16 * (dc // 2))
                cx.inc("ve", v.tensor_tensor(py_t[:, p, :], psum_a[:, p, :],
                                             rms_a[:, :], OP.mult))
                cx.sw(v, "vector", "ve")
                dd[("aev", dc)] = cx.inc("ve", v.tensor_tensor(
                    py_t[:, p, :], py_t[:, p, :], xf_t[:, p, :], OP.add))

        cx._tag = "atail"
        cx.phase([("tensor", pe_at), ("vector", v_at), ("sync", s_atail)])
        pt_done0 = cx.cnt["st_py0"]
        pt_done1 = cx.cnt["st_py1"]

        if stage == "partial":
            def s_dumpp(cx, s):
                cx.wait(s, "sync", "st_py0", pt_done0)
                cx.wait(s, "sync", "st_py1", pt_done1)
                cx.inc("st_y", s.dma_start(out=dbg_pt, in_=partial.ap()), 16)
                if not cx.dry:
                    s.wait_ge(cx.sems["st_y"], cx.cnt["st_y"])
            cx.phase([("sync", s_dumpp)])
            cx.close_sems()
            return nc

        cx.barrier("b2", extra={
            "sync": ["st_py0", "st_py1", "ld_xf0", "ld_xf1", "ld_wv0",
                     "ld_wv1", "ld_wo0", "ld_wo1"]})
        al.free("outw", "wv_t", "wo_t", "xf_t", "py_t",
                "psum_a", "bc_ps", "rowa", "rowb", "rms_a",
                "wq1_sb", "wk1_sb", "wq2_sb", "wk2_sb", "t1_sb", "t2_sb",
                "xn_wm")

        hn = al.sb("hn", [128, HCN, T], BF16, side="right")
        wgu_t = al.sb("wgu_t", [128, 2, 2, ECN, 128], BF16)
        g_sb = al.sb("g_sb", [128, T], F32)
        gate_ps = al.ps("gate_ps", [128, 2, T], F32)
        up_ps = al.ps("up_ps", [128, 2, T], F32)

        pairs = [(P, r, k) for P in range(4) for r in range(4) for k in range(4)]

        def s_gul(cx, s):
            for j, (P, r, k) in enumerate(pairs):
                p = j % 2
                cx.wait(s, "sync", f"ccg{P}", 1)
                if j >= 2:
                    cx.wait(s, "sync", "pe", dd.get(("gumm", j - 2)))
                cx.inc(f"ld_wgu{p}", s.dma_start(
                    out=wgu_t[:, p, 0, :, :],
                    in_=ag_gu_out[P].ap()[r * 512 + k * 128:r * 512 + (k + 1) * 128, :]
                        .rearrange("p (ec h) -> p ec h", h=128)), 16)
                dd[("guld", j)] = cx.inc(f"ld_wgu{p}", s.dma_start(
                    out=wgu_t[:, p, 1, :, :],
                    in_=ag_gu_out[P].ap()[(r + 4) * 512 + k * 128:
                                          (r + 4) * 512 + (k + 1) * 128, :]
                        .rearrange("p (ec h) -> p ec h", h=128)), 16)

        def pe_gu(cx, t):
            for j in range(len(pairs)):
                p = j % 2
                cx.wait(t, "tensor", f"ld_wgu{p}", dd[("guld", j)])
                if j >= 2:
                    cx.wait(t, "tensor", "ve", dd.get(("hn", j - 2)))
                    cx.wait(t, "tensor", "sc", dd.get(("gsilu", j - 2)))
                for ec in range(ECN):
                    for g in range(2):
                        ps = gate_ps if g == 0 else up_ps
                        for half in range(2):
                            m = t.matmul(ps[:, p, half * 512:(half + 1) * 512],
                                         wgu_t[:, p, g, ec, :],
                                         xn_mn[:, ec,
                                               half * 512:(half + 1) * 512],
                                         start=(ec == 0), stop=(ec == ECN - 1))
                dd[("gumm", j)] = cx.inc("pe", m)

        def sc_gu(cx, e):
            for j in range(len(pairs)):
                p = j % 2
                cx.wait(e, "scalar", "pe", dd[("gumm", j)])
                if j >= 1:
                    cx.wait(e, "scalar", "ve", dd.get(("hn", j - 1)))
                dd[("gsilu", j)] = cx.inc("sc", e.activation(
                    out=g_sb[:, :], in_=gate_ps[:, p, :], func=AF.Silu))

        def v_gu(cx, v):
            for j, (P, r, k) in enumerate(pairs):
                p = j % 2
                hc = r * 16 + P * 4 + k
                cx.wait(v, "vector", "sc", dd[("gsilu", j)])
                cx.wait(v, "vector", "pe", dd[("gumm", j)])
                dd[("hn", j)] = cx.inc("ve", v.tensor_tensor(
                    hn[:, hc, :], g_sb[:, :], up_ps[:, p, :], OP.mult))

        cx._tag = "gu"
        cx.phase([("sync", s_gul), ("tensor", pe_gu), ("scalar", sc_gu),
                  ("vector", v_gu)])

        cx.barrier("b3", extra={"sync": ["ld_wgu0", "ld_wgu1"]})
        al.free("wgu_t", "g_sb", "gate_ps", "up_ps", "xn_mn")

        rowa = al.sb("rowa2", [1, T], F32)
        rowb = al.sb("rowb2", [1, T], F32)
        rms_f = al.sb("rms_f", [128, T], F32)
        sqf_sb = al.sb("sqf_sb", [128, 2, T], BF16)
        statf_ps = al.ps("statf_ps", [1, T], F32)

        def v_sqf(cx, v):
            cx.sw(v, "vector", "ve")
            for hc in range(HCN):
                p = hc % 2
                if hc >= 2:
                    cx.wait(v, "vector", "pe", dd.get(("sfmm", hc - 2)))
                dd[("sqf", hc)] = cx.inc("ve", v.tensor_tensor(
                    sqf_sb[:, p, :], hn[:, hc, :], hn[:, hc, :], OP.mult))

        def pe_sqf(cx, t):
            for hc in range(HCN):
                p = hc % 2
                cx.wait(t, "tensor", "ve", dd[("sqf", hc)])
                for half in range(2):
                    m = t.matmul(statf_ps[0:1, half * 512:(half + 1) * 512],
                                 ones_c[:, :],
                                 sqf_sb[:, p, half * 512:(half + 1) * 512],
                                 start=(hc == 0), stop=(hc == HCN - 1))
                dd[("sfmm", hc)] = cx.inc("pe", m)

        def v_rmsf1(cx, v):
            cx.wait(v, "vector", "pe", cx.cnt["pe"])
            dd["rowf"] = cx.inc("ve", v.tensor_copy(rowa[0:1, :],
                                                    statf_ps[0:1, :]))
        def sc_rmsf(cx, e):
            cx.wait(e, "scalar", "ve", dd["rowf"])
            dd["rowfsd"] = cx.inc("sc", e.activation(
                out=rowb[0:1, :], in_=rowa[0:1, :], func=AF.Sqrt,
                bias=cst[0:1, 2:3], scale=cst[0:1, 3:4]))
        def v_rmsf2(cx, v):
            cx.wait(v, "vector", "sc", dd["rowfsd"])
            cx.inc("ve", v.reciprocal(rowa[0:1, :], rowb[0:1, :]))
        cx._tag = "statf"
        cx.phase([("vector", v_sqf), ("tensor", pe_sqf),
                  ("vector", v_rmsf1), ("scalar", sc_rmsf),
                  ("vector", v_rmsf2)])

        al.free("statf_ps", "sqf_sb")
        psum_f = al.ps("psum_f", [128, 2, T], F32)
        bcf_ps = al.ps("bcf_ps", [128, T], F32)

        def pe_bcf(cx, t):
            cx.wait(t, "tensor", "ve", cx.cnt["ve"])
            for half in range(2):
                m = t.matmul(bcf_ps[:, half * 512:(half + 1) * 512], ones_r[:, :],
                             rowa[0:1, half * 512:(half + 1) * 512],
                             start=True, stop=True)
            dd["bcf"] = cx.inc("pe", m)
        def v_bcf(cx, v):
            cx.wait(v, "vector", "pe", dd["bcf"])
            cx.inc("ve", v.tensor_copy(rms_f[:, :], bcf_ps[:, :]))
        cx._tag = "statf"
        cx.phase([("tensor", pe_bcf), ("vector", v_bcf)])
        rmsf_rdy = cx.cnt["ve"]

        wp_t = al.sb("wp_t", [128, 2, HCN, 128], BF16)
        pt_t = al.sb("pt_t", [128, 2, T], F32)
        yv_t = al.sb("yv_t", [128, 2, T], F32)

        def s_ftail(cx, s):
            cx.wait(s, "sync", "ccwp", 1)
            cx.wait(s, "sync", "st_py0", pt_done0)
            cx.wait(s, "sync", "st_py1", pt_done1)
            cx.wait(s, "sync", "pe", dd.get("bcf"))
            cx.wait(s, "sync", "ve", rmsf_rdy)
            def load(dc):
                p = dc % 2
                if dc >= 2:
                    cx.wait(s, "sync", "pe", dd.get(("ftmm", dc - 2)))
                rr, dcl = dc // 2, dc % 2
                dd[("wpld", dc)] = cx.inc(f"ld_wp{p}", s.dma_start(
                    out=wp_t[:, p, :, :],
                    in_=ag_wp_out.ap()[rr * 256 + dcl * 128:
                                       rr * 256 + (dcl + 1) * 128, :]
                        .rearrange("p (hc d) -> p hc d", d=128)), 16)
                if dc >= 2:
                    cx.wait(s, "sync", "ve", dd.get(("fev", dc - 2)))
                dd[("ptld", dc)] = cx.inc(f"ld_pt{p}", s.dma_start(
                    out=pt_t[:, p, :],
                    in_=partial.ap()[dc * 128:(dc + 1) * 128, :]), 16)
            load(0)
            load(1)
            for dc in range(DCN):
                cx.wait(s, "sync", "ve", dd.get(("fev", dc)))
                cx.inc(f"st_y{dc % 2}", s.dma_start(
                    out=y[dc * 128:(dc + 1) * 128, :],
                    in_=yv_t[:, dc % 2, :]), 16)
                if dc + 2 < DCN:
                    load(dc + 2)
            if not cx.dry:
                s.wait_ge(cx.sems["st_y0"], cx.cnt["st_y0"])
                s.wait_ge(cx.sems["st_y1"], cx.cnt["st_y1"])

        def pe_ft(cx, t):
            for dc in range(DCN):
                p = dc % 2
                cx.wait(t, "tensor", f"ld_wp{p}", dd.get(("wpld", dc)))
                if dc >= 2:
                    cx.wait(t, "tensor", "ve", dd.get(("fev", dc - 2)))
                for hc in range(HCN):
                    for half in range(2):
                        m = t.matmul(psum_f[:, p, half * 512:(half + 1) * 512],
                                     wp_t[:, p, hc, :],
                                     hn[:, hc, half * 512:(half + 1) * 512],
                                     start=(hc == 0), stop=(hc == HCN - 1))
                dd[("ftmm", dc)] = cx.inc("pe", m)

        def v_ft(cx, v):
            cx.sw(v, "vector", "ve")
            for dc in range(DCN):
                p = dc % 2
                cx.wait(v, "vector", "pe", dd[("ftmm", dc)])
                cx.wait(v, "vector", f"ld_pt{p}", dd.get(("ptld", dc)))
                if dc >= 2:
                    cx.wait(v, "vector", f"st_y{p}", 16 * (dc // 2))
                cx.inc("ve", v.tensor_tensor(yv_t[:, p, :], psum_f[:, p, :],
                                             rms_f[:, :], OP.mult))
                cx.sw(v, "vector", "ve")
                dd[("fev", dc)] = cx.inc("ve", v.tensor_tensor(
                    yv_t[:, p, :], yv_t[:, p, :], pt_t[:, p, :], OP.add))

        cx._tag = "ftail"
        cx.phase([("tensor", pe_ft), ("vector", v_ft),
                  ("sync", s_ftail)])
        cx.close_sems()
    return nc


def make_in_maps(inputs):
    X = np.asarray(inputs["L0_attn_x"], dtype=np.float32)
    ws = {k: np.asarray(inputs["L0_attn_" + k], dtype=np.float32)
          for k in ("Wq1", "Wq2", "Wk1", "Wk2", "Wv", "Wo", "norm_w")}
    Wgu = np.asarray(inputs["L0_ffn_w_gate_up"], dtype=np.float32)
    Wp = np.asarray(inputs["L0_ffn_w_proj"], dtype=np.float32)
    in_maps = []
    for c in range(NCORES):
        b, hf = c // 2, c % 2
        seq = X[b]
        main = seq[hf * T:(hf + 1) * T]
        warm = np.zeros((W, E), np.float32) if hf == 0 else seq[T - W:T]
        in_maps.append({
            "x_tm": np.ascontiguousarray(np.concatenate([warm, main], axis=0)),
            "x_fm": np.ascontiguousarray(main.T),
            "wq1": ws["Wq1"], "wq2": ws["Wq2"], "wk1": ws["Wk1"],
            "wk2": ws["Wk2"], "wv": ws["Wv"], "wo": ws["Wo"],
            "nrm": ws["norm_w"],
            "wgu_sl": np.ascontiguousarray(
                Wgu[c * (GU // NCORES):(c + 1) * (GU // NCORES)]),
            "wp_sl": np.ascontiguousarray(
                Wp[c * (D // NCORES):(c + 1) * (D // NCORES)]),
        })
    return in_maps


def kernel(**inputs):
    nc = build_program("full")
    in_maps = make_in_maps(inputs)
    res = run_bass_kernel_spmd(nc, in_maps, core_ids=list(range(NCORES)))
    out = np.empty((4, 2048, E), np.float32)
    for c in range(NCORES):
        b, hf = c // 2, c % 2
        out[b, hf * T:(hf + 1) * T, :] = res.results[c]["y"].T
    return out


# version 15
# speedup vs baseline: 1.0313x; 1.0313x over previous
import numpy as np
import ml_dtypes
import concourse.bass as bass
import concourse.mybir as mybir
from concourse.bass_utils import run_bass_kernel_spmd

F32 = mybir.dt.float32
BF16 = mybir.dt.bfloat16
AF = mybir.ActivationFunctionType
OP = mybir.AluOpType
AX = mybir.AxisListType

E = 2048
ECN = 16
C = 4096
CCN = 32
HD = 128
H = 8192
HCN = 64
GU = 16384
D = 2048
DCN = 16
T = 1024
W = 128
TW = T + W
NCORES = 8
TPS = [(0, W, False), (W, 512, True), (W + 512, 512, True)]

BF16_EPS8 = float(np.float32(ml_dtypes.bfloat16(1e-8)))
THR = 0.50048828125
LN_EPS = 1e-5
ATTN_EPS = 1e-6
FFN_EPS = 1.1920928955078125e-07

ENGS = ("sync", "scalar", "vector", "tensor", "gpsimd")


class _DummyInstr:
    def then_inc(self, *a, **k):
        return self


class _DryEng:
    def __getattr__(self, name):
        return lambda *a, **k: _DummyInstr()


class Ctx:

    def __init__(self, nc, block):
        self.nc = nc
        self.block = block
        self.sems = {}
        self.cnt = {}
        self.lastw = {}
        self._live = []
        self.dry = False

    def sem(self, name):
        cm = self.nc.semaphore(name)
        self._live.append(cm)
        self.sems[name] = cm.__enter__()
        self.cnt[name] = 0

    def close_sems(self):
        for cm in reversed(self._live):
            cm.__exit__(None, None, None)

    def wait(self, eng, engname, name, val):
        if val is None:
            return
        v = val
        if v <= 0:
            return
        key = (engname, name)
        if self.lastw.get(key, -1) >= v:
            return
        if not self.dry:
            eng.wait_ge(self.sems[name], v)
        self.lastw[key] = v

    def sw(self, eng, engname, name):
        self.wait(eng, engname, name, self.cnt[name])

    def inc(self, name, instr, k=1):
        if not self.dry:
            instr.then_inc(self.sems[name], k)
        self.cnt[name] += k
        return self.cnt[name]

    def phase(self, bodies):
        snap_cnt = dict(self.cnt)
        snap_w = dict(self.lastw)
        self.dry = True
        de = _DryEng()
        for name, fn in bodies:
            fn(self, de)
        dry_cnt = dict(self.cnt)
        self.cnt = snap_cnt
        self.lastw = snap_w
        self.dry = False
        tagn = getattr(self, "_tag", None)
        for name, fn in bodies:
            if tagn:
                with self.nc.named_scope(tagn):
                    getattr(self.block, name)(lambda eng, fn=fn: fn(self, eng))
            else:
                getattr(self.block, name)(lambda eng, fn=fn: fn(self, eng))
        assert self.cnt == dry_cnt, (self.cnt, dry_cnt)

    def barrier(self, tag, extra=None, engines=ENGS):
        tgt = self.cnt["ph"] + len(engines)
        extra = extra or {}
        def bfn(nm):
            def f(cx, e):
                for s in extra.get(nm, ()):
                    cx.wait(e, nm, s, cx.cnt[s])
                if not cx.dry:
                    if nm in ("scalar", "vector", "tensor"):
                        e.drain()
                    e.sem_inc(cx.sems["ph"], 1)
                    e.wait_ge(cx.sems["ph"], tgt)
                cx.lastw[(nm, "ph")] = tgt
            return f
        for nm in engines:
            getattr(self.block, nm)(lambda eng, nm=nm: bfn(nm)(self, eng))
            self.lastw[(nm, "ph")] = tgt
        self.cnt["ph"] = tgt


class Alloc:
    def __init__(self, nc):
        self.nc = nc
        self.cms = {}
        self.order = {}
        self.n = 0

    def sb(self, name, shape, dtype, side="left"):
        cm = self.nc.sbuf_tensor(name, shape, dtype, side=side)
        h = cm.__enter__()
        self.cms[name] = cm
        self.order[name] = self.n
        self.n += 1
        return h

    def ps(self, name, shape, dtype):
        cm = self.nc.psum_tensor(name, shape, dtype)
        h = cm.__enter__()
        self.cms[name] = cm
        self.order[name] = self.n
        self.n += 1
        return h

    def free(self, *names):
        for n in sorted(names, key=lambda x: -self.order[x]):
            self.cms.pop(n).__exit__(None, None, None)


def build_program(stage="full"):
    lvl = {"xn": 1, "attn": 2, "partial": 3, "full": 5}[stage]
    nc = bass.Bass("TRN2", target_bir_lowering=False, debug=False)

    x_tm = nc.dram_tensor("x_tm", [TW, E], F32, kind="ExternalInput").ap()
    x_fm = nc.dram_tensor("x_fm", [E, T], F32, kind="ExternalInput").ap()
    wq1 = nc.dram_tensor("wq1", [E, HD], F32, kind="ExternalInput").ap()
    wq2 = nc.dram_tensor("wq2", [HD, C], F32, kind="ExternalInput").ap()
    wk1 = nc.dram_tensor("wk1", [E, HD], F32, kind="ExternalInput").ap()
    wk2 = nc.dram_tensor("wk2", [HD, C], F32, kind="ExternalInput").ap()
    wv = nc.dram_tensor("wv", [E, C], F32, kind="ExternalInput").ap()
    wo = nc.dram_tensor("wo", [C, D], F32, kind="ExternalInput").ap()
    nrm = nc.dram_tensor("nrm", [C], F32, kind="ExternalInput").ap()
    wgu_sl = nc.dram_tensor("wgu_sl", [GU // NCORES, E], F32, kind="ExternalInput").ap()
    wp_sl = nc.dram_tensor("wp_sl", [D // NCORES, H], F32, kind="ExternalInput").ap()

    if lvl >= 5:
        y = nc.dram_tensor("y", [D, T], F32, kind="ExternalOutput").ap()
    if stage == "xn":
        dbg_xn = nc.dram_tensor("dbg_xn", [E, TW], BF16, kind="ExternalOutput").ap()
    if stage == "attn":
        dbg_ow = nc.dram_tensor("dbg_ow", [C, T], BF16, kind="ExternalOutput").ap()
        dbg_st = nc.dram_tensor("dbg_st", [1, T], F32, kind="ExternalOutput").ap()
    if stage == "partial":
        dbg_pt = nc.dram_tensor("dbg_pt", [D, T], F32, kind="ExternalOutput").ap()

    ag_gu_in = [nc.dram_tensor(f"ag_gu_in{i}", [512, E], BF16) for i in range(4)]
    ag_gu_out = [nc.dram_tensor(f"ag_gu_out{i}", [NCORES * 512, E], BF16)
                 for i in range(4)]
    ag_wp_in = nc.dram_tensor("ag_wp_in", [D // NCORES, H], BF16)
    ag_wp_out = nc.dram_tensor("ag_wp_out", [NCORES * (D // NCORES), H], BF16)
    partial = nc.dram_tensor("partial", [D, T], F32)
    wv_b = nc.dram_tensor("wv_b", [CCN * 128, E], BF16)
    wo_b = nc.dram_tensor("wo_b", [DCN * 128, C], BF16)

    al = Alloc(nc)
    nrm_sb = al.sb("nrm_sb", [128, CCN], F32, side="right")
    ones_c = al.sb("ones_c", [128, 1], BF16, side="right")
    ones_r = al.sb("ones_r", [1, 128], F32, side="right")
    epsc = al.sb("epsc", [128, 1], F32, side="right")
    cst = al.sb("cst", [1, 4], F32, side="right")
    xn_mn = al.sb("xn_mn", [128, ECN, T], BF16)
    xn_wm = al.sb("xn_wm", [128, ECN, W], BF16)
    wq1_sb = al.sb("wq1_sb", [128, ECN, HD], BF16)
    wk1_sb = al.sb("wk1_sb", [128, ECN, HD], BF16)
    wq2_sb = al.sb("wq2_sb", [128, C], BF16)
    wk2_sb = al.sb("wk2_sb", [128, C], BF16)
    t1_sb = al.sb("t1_sb", [128, T], BF16)
    t2_sb = al.sb("t2_sb", [128, TW], BF16)

    with nc.Block() as block:
        cx = Ctx(nc, block)
        for s in ("ph", "pe", "sc", "ve", "ldsm", "st_y", "cvb0", "cvb1", "cvb2", "cvb3", "cob",
                  "st_ag0", "st_ag1", "st_wp0", "st_wp1", "st_y0", "st_y1",
                  "ccg0", "ccg1", "ccg2", "ccg3", "ccwp",
                  "trs_q0", "trs_q1", "trs_w0", "trs_w1"):
            cx.sem(s)
        for s in ("ld_x", "cast", "cwp", "trs_ln", "ld_wv", "ld_wo",
                  "ld_wgu", "ld_wp", "ld_xf", "ld_pt", "st_py"):
            cx.sem(s + "0")
            cx.sem(s + "1")

        NGQ, NPQ = 8, 4
        wgu_buf = al.sb("wgu_buf", [128, 2, 2, E], BF16)
        wp_buf = al.sb("wp_buf", [128, 2, 4096], BF16)
        pos_sb = al.sb("pos_sb", [128, 4096], BF16)
        neg_sb = al.sb("neg_sb", [128, 4096], BF16)
        red_sb = al.sb("red_sb", [128, 64], F32)
        scl_sb = al.sb("scl_sb", [128, 64], BF16)
        thr_sb = al.sb("thr_sb", [128, 64], F32)
        nthr_sb = al.sb("nthr_sb", [128, 64], F32)
        tt_sb = al.sb("tt_sb", [128, 2, 16, 128], BF16)
        tt_wp = al.sb("tt_wp", [128, 2, 32, 128], BF16)
        xt_sb = al.sb("xt_sb", [128, 2, E], F32)
        xnt_sb = al.sb("xnt_sb", [128, 2, E], BF16)
        sum_sb = al.sb("sum_sb", [128, 2, 1], F32)
        ssq_sb = al.sb("ssq_sb", [128, 2, 1], F32)
        mean_sb = al.sb("mean_sb", [128, 2, 1], F32)
        var_sb = al.sb("var_sb", [128, 2, 1], F32)
        sd_sb = al.sb("sd_sb", [128, 2, 1], F32)
        inv_sb = al.sb("inv_sb", [128, 2, 1], F32)

        def xn_sl(ec, off, sz):
            if off < W:
                assert off + sz <= W
                return xn_wm[:, ec, off:off + sz]
            return xn_mn[:, ec, off - W:off - W + sz]

        dd = {}
        NTS = TW // 128

        def quant_ops(cx, v, wbuf, G):
            view = wbuf.rearrange("p (g k) -> p g k", k=64)
            rd = red_sb[:, 0:G]
            sc_ = scl_sb[:, 0:G]
            th = thr_sb[:, 0:G]
            nth = nthr_sb[:, 0:G]
            pv = pos_sb[:, 0:G * 64].rearrange("p (g k) -> p g k", k=64)
            nv = neg_sb[:, 0:G * 64].rearrange("p (g k) -> p g k", k=64)
            cx.inc("ve", v.tensor_reduce(rd, view, AX.X, OP.add,
                                          apply_absolute_value=True))
            cx.sw(v, "vector", "ve")
            cx.inc("ve", v.tensor_scalar(sc_, rd, 1.0 / 64.0, None, OP.mult))
            cx.sw(v, "vector", "ve")
            cx.inc("ve", v.tensor_scalar_max(sc_, sc_, BF16_EPS8))
            cx.sw(v, "vector", "ve")
            v.tensor_scalar(th, sc_, THR, None, OP.mult)
            cx.inc("ve", v.tensor_scalar(nth, sc_, -THR, None, OP.mult))
            cx.sw(v, "vector", "ve")
            v.tensor_tensor(pv, view, th.unsqueeze(2).broadcast_to([128, G, 64]),
                            OP.is_gt)
            cx.inc("ve", v.tensor_tensor(
                nv, view, nth.unsqueeze(2).broadcast_to([128, G, 64]), OP.is_lt))
            cx.sw(v, "vector", "ve")
            cx.inc("ve", v.tensor_tensor(pv, pv, nv, OP.subtract))
            cx.sw(v, "vector", "ve")
            return v.tensor_tensor(view, pv,
                                   sc_.unsqueeze(2).broadcast_to([128, G, 64]),
                                   OP.mult)

        def g_prep(cx, g):
            i = cx.inc
            i("ldsm", g.dma_start(out=wq1_sb[:, :, :],
                                  in_=wq1.rearrange("(a p) m -> p a m", p=128)), 16)
            i("ldsm", g.dma_start(out=wk1_sb[:, :, :],
                                  in_=wk1.rearrange("(a p) m -> p a m", p=128)), 16)
            i("ldsm", g.dma_start(out=wq2_sb[:, :], in_=wq2), 16)
            i("ldsm", g.dma_start(out=wk2_sb[:, :], in_=wk2), 16)
            i("ldsm", g.dma_start(out=nrm_sb[:, :],
                                  in_=nrm.rearrange("(a p) -> p a", p=128),
                                  allow_slow_non_contiguous=True), 16)
            dd["ldsm"] = cx.cnt["ldsm"]
            for q in range(NGQ):
                p = q % 2
                if q >= 2:
                    cx.wait(g, "gpsimd", "trs_q0", dd.get(("gu_trs0", q - 2)))
                    cx.wait(g, "gpsimd", "trs_q1", dd.get(("gu_trs1", q - 2)))
                i(f"cast{p}", g.dma_start(
                    out=wgu_buf[:, p, :, :],
                    in_=wgu_sl[q * 256:(q + 1) * 256, :]
                        .rearrange("(hc p) d -> p hc d", p=128)), 16)
            for q in range(4):
                p = q % 2
                if q >= 2:
                    cx.wait(g, "gpsimd", "trs_w0", dd.get(("wp_trs0", q - 2)))
                    cx.wait(g, "gpsimd", "trs_w1", dd.get(("wp_trs1", q - 2)))
                dcl, half = q // 2, q % 2
                i(f"cwp{p}", g.dma_start(
                    out=wp_buf[:, p, :],
                    in_=wp_sl[dcl * 128:(dcl + 1) * 128,
                              half * 4096:(half + 1) * 4096]), 16)


        def v_memset(cx, v):
            if not cx.dry:
                v.memset(ones_c[:, :], 1.0)
                v.memset(ones_r[:, :], 1.0)
                v.memset(epsc[:, :], LN_EPS)
                v.memset(cst[0:1, 0:1], ATTN_EPS)
                v.memset(cst[0:1, 1:2], 1.0 / C)
                v.memset(cst[0:1, 2:3], FFN_EPS)
                v.memset(cst[0:1, 3:4], 1.0 / H)

        def v_prep(cx, v):
            for q in range(NGQ):
                p = q % 2
                cx.wait(v, "vector", f"cast{p}", 16 * (q // 2 + 1))
                dd[("gu_q", q)] = cx.inc(
                    "ve", quant_ops(cx, v, wgu_buf[:, p, :, :]
                                    .rearrange("p a d -> p (a d)"), 64))
            for q in range(4):
                p = q % 2
                cx.wait(v, "vector", f"cwp{p}", 16 * (q // 2 + 1))
                dd[("wp_q", q)] = cx.inc("ve", quant_ops(cx, v, wp_buf[:, p, :], 64))

        def s_prep(cx, s):
            for q in range(NGQ):
                p = q % 2
                cx.wait(s, "sync", "ve", dd[("gu_q", q)])
                for i2 in range(2):
                    hcl = q * 2 + i2
                    P, col = hcl // 4, hcl % 4
                    pt = hcl % 2
                    if hcl >= 2:
                        cx.wait(s, "sync", f"st_ag{pt}", dd.get(("gu_st", hcl - 2)))
                    cx.inc(f"trs_q{pt}", s.dma_start_transpose(
                        tt_sb[:, pt, :, :],
                        wgu_buf[:, p, i2, :].rearrange("p (dc r) -> p dc r", r=128)),
                        16)
                    dd[("gu_trs" + str(pt), q)] = cx.cnt[f"trs_q{pt}"]
                    cx.wait(s, "sync", f"trs_q{pt}", cx.cnt[f"trs_q{pt}"])
                    dd[("gu_st", hcl)] = cx.inc(f"st_ag{pt}", s.dma_start(
                        out=ag_gu_in[P].ap()[col * 128:(col + 1) * 128, :]
                            .rearrange("p (ec h) -> p ec h", h=128),
                        in_=tt_sb[:, pt, :, :]), 16)
                    if hcl % 4 == 3:
                        dd[("gu_ag0", P)] = cx.cnt["st_ag0"]
                        dd[("gu_ag1", P)] = cx.cnt["st_ag1"]
            for q in range(4):
                p = q % 2
                dcl, half = q // 2, q % 2
                pt = q % 2
                cx.wait(s, "sync", "ve", dd[("wp_q", q)])
                if q >= 2:
                    cx.wait(s, "sync", f"st_wp{pt}", dd.get(("wp_st", q - 2)))
                cx.inc(f"trs_w{pt}", s.dma_start_transpose(
                    tt_wp[:, pt, :, :],
                    wp_buf[:, p, :].rearrange("p (hc r) -> p hc r", r=128)), 16)
                dd[("wp_trs" + str(pt), q)] = cx.cnt[f"trs_w{pt}"]
                cx.wait(s, "sync", f"trs_w{pt}", cx.cnt[f"trs_w{pt}"])
                dd[("wp_st", q)] = cx.inc(f"st_wp{pt}", s.dma_start(
                    out=ag_wp_in.ap()[dcl * 128:(dcl + 1) * 128,
                                      half * 4096:(half + 1) * 4096]
                        .rearrange("p (hc d) -> p hc d", d=128),
                    in_=tt_wp[:, pt, :, :]), 16)
                dd[("wp_ag0", q)] = cx.cnt["st_wp0"]
                dd[("wp_ag1", q)] = cx.cnt["st_wp1"]

        def sc_ln(cx, e):
            def process(ts):
                cx.wait(e, "scalar", "ve", dd.get(("ln_var", ts)))
                dd[("ln_sd", ts)] = cx.inc("sc", e.activation(
                    out=sd_sb[:, ts % 2, :], in_=var_sb[:, ts % 2, :],
                    func=AF.Sqrt, bias=epsc[:, 0:1]))
            for ts in range(NTS):
                p = ts % 2
                if ts >= 2:
                    cx.wait(e, "scalar", "ve", dd.get(("ln_xn", ts - 2)))
                dd[("ln_x", ts)] = cx.inc(f"ld_x{p}", e.dma_start(
                    out=xt_sb[:, p, :], in_=x_tm[ts * 128:(ts + 1) * 128, :]), 16)
                if ts >= 1:
                    process(ts - 1)
            process(NTS - 1)

        def v_ln(cx, v):
            for ts in range(NTS):
                p = ts % 2
                cx.wait(v, "vector", f"ld_x{p}", dd[("ln_x", ts)])
                if ts >= 2:
                    cx.wait(v, "vector", f"trs_ln{p}", dd.get(("ln_tr", ts - 2)))
                cx.inc("ve", v.scalar_tensor_tensor(
                    xnt_sb[:, p, :], xt_sb[:, p, :], 1.0, xt_sb[:, p, :],
                    OP.mult, OP.mult, accum_out=ssq_sb[:, p, :]))
                cx.inc("ve", v.reduce_sum(sum_sb[:, p, :], xt_sb[:, p, :], AX.X))
                cx.sw(v, "vector", "ve")
                cx.inc("ve", v.tensor_scalar(mean_sb[:, p, :], sum_sb[:, p, :],
                                             1.0 / E, None, OP.mult))
                v.tensor_scalar(var_sb[:, p, :], ssq_sb[:, p, :], 1.0 / E,
                                None, OP.mult)
                cx.sw(v, "vector", "ve")
                cx.inc("ve", v.tensor_tensor(inv_sb[:, p, :], mean_sb[:, p, :],
                                             mean_sb[:, p, :], OP.mult))
                cx.sw(v, "vector", "ve")
                dd[("ln_var", ts)] = cx.inc("ve", v.tensor_tensor(
                    var_sb[:, p, :], var_sb[:, p, :], inv_sb[:, p, :],
                    OP.subtract))
                cx.wait(v, "vector", "sc", dd[("ln_sd", ts)])
                cx.inc("ve", v.reciprocal(inv_sb[:, p, :], sd_sb[:, p, :]))
                cx.sw(v, "vector", "ve")
                dd[("ln_xn", ts)] = cx.inc("ve", v.tensor_scalar(
                    xnt_sb[:, p, :], xt_sb[:, p, :], mean_sb[:, p, :],
                    inv_sb[:, p, :], OP.subtract, OP.mult))

        def s_lntr(cx, s):
            for ts in range(NTS):
                p = ts % 2
                cx.wait(s, "sync", "ve", dd[("ln_xn", ts)])
                nw = W // 128
                if ts < nw:
                    dst = xn_wm[:, :, ts * 128:(ts + 1) * 128]
                else:
                    dst = xn_mn[:, :, (ts - nw) * 128:(ts - nw + 1) * 128]
                cx.inc(f"trs_ln{p}", s.dma_start_transpose(
                    dst,
                    xnt_sb[:, p, :].rearrange("p (ec r) -> p ec r", r=128)), 16)
                dd[("ln_tr", ts)] = cx.cnt[f"trs_ln{p}"]

        cx._tag = "p0ln"
        cx.phase([("vector", v_memset), ("scalar", sc_ln), ("vector", v_ln),
                  ("sync", s_lntr), ("vector", v_prep), ("sync", s_prep),
                  ("gpsimd", g_prep)])

        xn_rdy0 = cx.cnt["trs_ln0"]
        xn_rdy1 = cx.cnt["trs_ln1"]

        if stage == "xn":
            def s_dump(cx, s):
                cx.wait(s, "sync", "trs_ln0", xn_rdy0)
                cx.wait(s, "sync", "trs_ln1", xn_rdy1)
                for ec in range(ECN):
                    cx.inc("st_y", s.dma_start(
                        out=dbg_xn[ec * 128:(ec + 1) * 128, 0:W],
                        in_=xn_wm[:, ec, :]), 16)
                    cx.inc("st_y", s.dma_start(
                        out=dbg_xn[ec * 128:(ec + 1) * 128, W:TW],
                        in_=xn_mn[:, ec, :]), 16)
                if not cx.dry:
                    s.wait_ge(cx.sems["st_y"], cx.cnt["st_y"])
            cx.phase([("sync", s_dump)])
            cx.close_sems()
            return nc

        cx.barrier("b1", extra={
            "sync": ["trs_ln0", "trs_ln1", "st_ag0", "st_ag1", "st_wp0",
                     "st_wp1", "trs_q0", "trs_q1", "trs_w0", "trs_w1"],
            "gpsimd": ["ldsm", "cast0", "cast1", "cwp0", "cwp1"],
            "scalar": ["ld_x0", "ld_x1"]})
        al.free("wgu_buf", "wp_buf", "pos_sb", "neg_sb", "red_sb", "scl_sb",
                "thr_sb", "nthr_sb", "tt_sb", "tt_wp", "xt_sb", "xnt_sb",
                "sum_sb", "ssq_sb", "mean_sb", "var_sb", "sd_sb", "inv_sb")

        outw = al.sb("outw", [128, CCN, T], BF16)
        wv_t = al.sb("wv_t", [128, 2, ECN, 128], BF16)
        wo_t = al.sb("wo_t", [128, 2, CCN, 128], BF16)
        rowa = al.sb("rowa", [1, T], F32)
        rowb = al.sb("rowb", [1, T], F32)
        rms_a = al.sb("rms_a", [128, T], F32)
        f_sb = al.sb("f_sb", [128, 2, 512], F32)
        a_sb = al.sb("a_sb", [128, 2, 512], F32)
        k_sb = al.sb("k_sb", [128, 2, 512], F32)
        b_sb = al.sb("b_sb", [128, 2, 512], F32)
        h_sb = al.sb("h_sb", [128, 2, 512], F32)
        q_sb = al.sb("q_sb", [128, 2, 512], F32)
        o_sb = al.sb("o_sb", [128, 2, 512], F32)
        sq_sb = al.sb("sq_sb", [128, 2, 512], BF16)

        logf_ps = al.ps("logf_ps", [128, 2, 512], F32)
        v_ps = al.ps("v_ps", [128, 2, 512], F32)
        q_ps = al.ps("q_ps", [128, 2, 512], F32)
        stat_ps = al.ps("stat_ps", [1, T], F32)

        steps = [(cc, tpi) for cc in range(CCN) for tpi in range(3)]
        qsteps = set(s for s in steps if TPS[s[1]][2])

        def pe_t12(cx, t):
            cx.wait(t, "tensor", "ldsm", dd["ldsm"])
            cx.wait(t, "tensor", "trs_ln0", xn_rdy0)
            cx.wait(t, "tensor", "trs_ln1", xn_rdy1)
            gi = 0
            for (name, wsb, pieces) in (
                    ("t2", wk1_sb, TPS),
                    ("t1", wq1_sb, [TPS[1], TPS[2]])):
                for pi, (off, sz, _) in enumerate(pieces):
                    pg = gi % 2
                    if gi >= 2:
                        cx.wait(t, "tensor", "ve", dd.get(("t12ev", gi - 2)))
                    ps = logf_ps[:, pg, 0:sz]
                    for ec in range(ECN):
                        m = t.matmul(ps, wsb[:, ec, :], xn_sl(ec, off, sz),
                                     start=(ec == 0), stop=(ec == ECN - 1))
                    dd[("t12mm", gi)] = cx.inc("pe", m)
                    gi += 1

        def v_t12(cx, v):
            gi = 0
            for (name, dst, pieces) in (
                    ("t2", t2_sb, TPS),
                    ("t1", t1_sb, [TPS[1], TPS[2]])):
                for pi, (off, sz, _) in enumerate(pieces):
                    pg = gi % 2
                    cx.wait(v, "vector", "pe", dd[("t12mm", gi)])
                    doff = off if name == "t2" else off - W
                    dd[("t12ev", gi)] = cx.inc("ve", v.tensor_copy(
                        dst[:, doff:doff + sz], logf_ps[:, pg, 0:sz]))
                    dd[(name + "ev", pi)] = dd[("t12ev", gi)]
                    gi += 1

        def g_post(cx, g):
            i = cx.inc
            def ag(P):
                if P < NPQ:
                    cx.wait(g, "gpsimd", "st_ag0", dd[("gu_ag0", P)])
                    cx.wait(g, "gpsimd", "st_ag1", dd[("gu_ag1", P)])
                    i(f"ccg{P}", g.collective_compute(
                        "AllGather", OP.bypass,
                        replica_groups=[list(range(NCORES))],
                        ins=[ag_gu_in[P].ap().opt()],
                        outs=[ag_gu_out[P].ap().opt()]), 1)
                else:
                    cx.wait(g, "gpsimd", "st_wp0", dd[("wp_ag0", 3)])
                    cx.wait(g, "gpsimd", "st_wp1", dd[("wp_ag1", 3)])
                    i("ccwp", g.collective_compute(
                        "AllGather", OP.bypass,
                        replica_groups=[list(range(NCORES))],
                        ins=[ag_wp_in.ap().opt()],
                        outs=[ag_wp_out.ap().opt()]), 1)
            ag(0)
            for cc in range(CCN):
                i(f"cvb{cc // 8}", g.dma_start(
                    out=wv_b.ap()[cc * 128:(cc + 1) * 128, :]
                        .rearrange("p (ec c) -> p ec c", c=128),
                    in_=wv[:, cc * 128:(cc + 1) * 128]
                        .rearrange("(ec p) c -> p ec c", p=128)), 16)
                if cc == 15:
                    ag(1)
            for dc in range(DCN):
                i("cob", g.dma_start(
                    out=wo_b.ap()[dc * 128:(dc + 1) * 128, :]
                        .rearrange("p (cc d) -> p cc d", d=128),
                    in_=wo[:, dc * 128:(dc + 1) * 128]
                        .rearrange("(cc p) d -> p cc d", p=128)), 16)
            for P in range(2, NPQ + 1):
                ag(P)

        def s_wv(cx, s):
            for cc in range(CCN):
                p = cc % 2
                cx.wait(s, "sync", f"cvb{cc // 8}", 128)
                if cc >= 2:
                    cx.wait(s, "sync", "pe", dd.get(("vmm", cc - 2, 2)))
                dd[("wvld", cc)] = cx.inc(f"ld_wv{p}", s.dma_start(
                    out=wv_t[:, p, :, :],
                    in_=wv_b.ap()[cc * 128:(cc + 1) * 128, :]
                        .rearrange("p (ec c) -> p ec c", c=128)), 16)

        def pe_cc(cx, t):
            last_par = {}
            cx.wait(t, "tensor", "ve", dd.get(("t12ev", 4)))
            si = 0
            pend = []
            for cc in range(CCN):
                pwv = cc % 2
                cx.wait(t, "tensor", f"ld_wv{pwv}", dd[("wvld", cc)])
                for tpi, (off, sz, hasq) in enumerate(TPS):
                    s = (cc, tpi)
                    p = si % 2
                    if si >= 2:
                        cx.wait(t, "tensor", "sc", dd.get(("f", steps[si - 2])))
                    cx.wait(t, "tensor", "ve", dd[("t2ev", tpi)])
                    m = t.matmul(logf_ps[:, p, 0:sz],
                                 wk2_sb[:, cc * 128:(cc + 1) * 128],
                                 t2_sb[:, off:off + sz], start=True, stop=True)
                    dd[("logf", s)] = cx.inc("pe", m)
                    if si >= 2:
                        cx.wait(t, "tensor", "ve", dd.get(("b", steps[si - 2])))
                    for ec in range(ECN):
                        m = t.matmul(v_ps[:, p, 0:sz], wv_t[:, pwv, ec, :],
                                     xn_sl(ec, off, sz),
                                     start=(ec == 0), stop=(ec == ECN - 1))
                    dd[("vmm", cc, tpi)] = cx.inc("pe", m)
                    if hasq:
                        prevq = last_par.get(("qps", p))
                        if prevq is not None:
                            cx.wait(t, "tensor", "ve", dd.get(("o", prevq)))
                        last_par[("qps", p)] = s
                        cx.wait(t, "tensor", "ve", dd[("t1ev", tpi - 1)])
                        m = t.matmul(q_ps[:, p, 0:sz],
                                     wq2_sb[:, cc * 128:(cc + 1) * 128],
                                     t1_sb[:, off - W:off - W + sz],
                                     start=True, stop=True)
                        dd[("qmm", s)] = cx.inc("pe", m)
                        pend.append((s, p))
                    if len(pend) > 2:
                        stat_emit(cx, t, *pend.pop(0))
                    si += 1
            for it in pend:
                stat_emit(cx, t, *it)

        def stat_emit(cx, t, s, p):
            cc, tpi = s
            off = TPS[tpi][0] - W
            cx.wait(t, "tensor", "ve", dd.get(("sq", s)))
            m = t.matmul(stat_ps[0:1, off:off + 512], ones_c[:, :],
                         sq_sb[:, p, :], start=(cc == 0), stop=(cc == CCN - 1))
            dd[("stat", s)] = cx.inc("pe", m)

        def sc_cc(cx, e):
            last_par = {}
            si = 0
            pend = []
            for cc in range(CCN):
                for tpi, (off, sz, hasq) in enumerate(TPS):
                    s = (cc, tpi)
                    p = si % 2
                    cx.wait(e, "scalar", "pe", dd[("logf", s)])
                    if si >= 2 and steps[si - 2] in qsteps:
                        cx.wait(e, "scalar", "ve", dd.get(("o", steps[si - 2])))
                    dd[("f", s)] = cx.inc("sc", e.activation(
                        out=f_sb[:, p, 0:sz], in_=logf_ps[:, p, 0:sz],
                        func=AF.Sigmoid))
                    cx.sw(e, "scalar", "sc")
                    if si >= 2:
                        cx.wait(e, "scalar", "ve", dd.get(("scan", steps[si - 2])))
                    dd[("a", s)] = cx.inc("sc", e.activation(
                        out=a_sb[:, p, 0:sz], in_=f_sb[:, p, 0:sz],
                        func=AF.Sigmoid))
                    if hasq:
                        cx.wait(e, "scalar", "pe", dd[("qmm", s)])
                        prevq = last_par.get(("qsb", p))
                        if prevq is not None:
                            cx.wait(e, "scalar", "ve", dd.get(("o1", prevq)))
                        last_par[("qsb", p)] = s
                        dd[("sig", s)] = cx.inc("sc", e.activation(
                            out=q_sb[:, p, 0:sz], in_=q_ps[:, p, 0:sz],
                            func=AF.Sigmoid))
                    si += 1

        def v_cc(cx, v):
            last_par = {}
            si = 0
            for cc in range(CCN):
                for tpi, (off, sz, hasq) in enumerate(TPS):
                    s = (cc, tpi)
                    p = si % 2
                    cx.wait(v, "vector", "sc", dd[("a", s)])
                    cx.inc("ve", v.tensor_scalar(k_sb[:, p, 0:sz],
                                                 a_sb[:, p, 0:sz],
                                                 -1.0, 1.0, OP.mult, OP.add))
                    cx.sw(v, "vector", "ve")
                    cx.wait(v, "vector", "pe", dd[("vmm", cc, tpi)])
                    dd[("b", s)] = cx.inc("ve", v.tensor_tensor(
                        b_sb[:, p, 0:sz], k_sb[:, p, 0:sz], v_ps[:, p, 0:sz],
                        OP.mult))
                    cx.sw(v, "vector", "ve")
                    if tpi == 0:
                        init = 0.0
                    else:
                        psz = TPS[tpi - 1][1]
                        init = h_sb[:, 1 - p, psz - 1:psz]
                    dd[("scan", s)] = cx.inc("ve", v.tensor_tensor_scan(
                        h_sb[:, p, 0:sz], a_sb[:, p, 0:sz], b_sb[:, p, 0:sz],
                        init, OP.mult, OP.add))
                    cx.sw(v, "vector", "ve")
                    if hasq:
                        cx.wait(v, "vector", "sc", dd[("sig", s)])
                        dd[("o1", s)] = cx.inc("ve", v.tensor_tensor(
                            f_sb[:, p, 0:sz], q_sb[:, p, 0:sz],
                            h_sb[:, p, 0:sz], OP.mult))
                        cx.sw(v, "vector", "ve")
                        dd[("o", s)] = cx.inc("ve", v.tensor_tensor(
                            o_sb[:, p, 0:sz], f_sb[:, p, 0:sz],
                            q_ps[:, p, 0:sz], OP.mult))
                        cx.sw(v, "vector", "ve")
                        prevq = last_par.get(("sqb", p))
                        if prevq is not None:
                            cx.wait(v, "vector", "pe", dd.get(("stat", prevq)))
                        last_par[("sqb", p)] = s
                        dd[("sq", s)] = cx.inc("ve", v.tensor_tensor(
                            sq_sb[:, p, :], o_sb[:, p, 0:sz], o_sb[:, p, 0:sz],
                            OP.mult))
                        dd[("outw", s)] = cx.inc("ve", v.tensor_scalar(
                            outw[:, cc, off - W:off - W + sz], o_sb[:, p, 0:sz],
                            nrm_sb[:, cc:cc + 1], None, OP.mult))
                    si += 1

        cx._tag = "attn"
        cx.phase([("gpsimd", g_post),
                  ("tensor", pe_t12), ("vector", v_t12), ("sync", s_wv),
                  ("tensor", pe_cc), ("scalar", sc_cc), ("vector", v_cc)])

        if stage == "attn":
            def v_dump(cx, v):
                cx.wait(v, "vector", "pe", cx.cnt["pe"])
                dd["strow"] = cx.inc("ve", v.tensor_copy(rowa[0:1, :],
                                                         stat_ps[0:1, :]))
            def s_dumpa(cx, s):
                cx.wait(s, "sync", "ve", cx.cnt["ve"])
                for cc in range(CCN):
                    cx.inc("st_y", s.dma_start(
                        out=dbg_ow[cc * 128:(cc + 1) * 128, :],
                        in_=outw[:, cc, :]), 16)
                cx.inc("st_y", s.dma_start(out=dbg_st, in_=rowa[0:1, :]), 16)
                if not cx.dry:
                    s.wait_ge(cx.sems["st_y"], cx.cnt["st_y"])
            cx.phase([("vector", v_dump), ("sync", s_dumpa)])
            cx.close_sems()
            return nc

        def v_rmsa1(cx, v):
            cx.wait(v, "vector", "pe", cx.cnt["pe"])
            dd["rowa"] = cx.inc("ve", v.tensor_copy(rowa[0:1, :],
                                                    stat_ps[0:1, :]))
        def sc_rmsa(cx, e):
            cx.wait(e, "scalar", "ve", dd["rowa"])
            dd["rowsd"] = cx.inc("sc", e.activation(
                out=rowb[0:1, :], in_=rowa[0:1, :], func=AF.Sqrt,
                bias=cst[0:1, 0:1], scale=cst[0:1, 1:2]))
        def v_rmsa2(cx, v):
            cx.wait(v, "vector", "sc", dd["rowsd"])
            dd["rowinv"] = cx.inc("ve", v.reciprocal(rowa[0:1, :], rowb[0:1, :]))
        cx._tag = "rmsa"
        cx.phase([("vector", v_rmsa1), ("scalar", sc_rmsa),
                  ("vector", v_rmsa2)])

        al.free("logf_ps", "v_ps", "q_ps", "stat_ps")
        al.free("f_sb", "a_sb", "k_sb", "b_sb", "h_sb", "q_sb", "o_sb",
                "sq_sb")
        xf_t = al.sb("xf_t", [128, 2, T], F32)
        py_t = al.sb("py_t", [128, 2, T], F32)
        psum_a = al.ps("psum_a", [128, 2, T], F32)
        bc_ps = al.ps("bc_ps", [128, T], F32)

        def pe_bca(cx, t):
            cx.wait(t, "tensor", "ve", cx.cnt["ve"])
            cx.wait(t, "tensor", "sc", cx.cnt["sc"])
            for half in range(2):
                m = t.matmul(bc_ps[:, half * 512:(half + 1) * 512], ones_r[:, :],
                             rowa[0:1, half * 512:(half + 1) * 512],
                             start=True, stop=True)
            dd["bca"] = cx.inc("pe", m)
        def v_bca(cx, v):
            cx.wait(v, "vector", "pe", dd["bca"])
            dd["rmsabc"] = cx.inc("ve", v.tensor_copy(rms_a[:, :], bc_ps[:, :]))
        cx._tag = "rmsa"
        cx.phase([("tensor", pe_bca), ("vector", v_bca)])

        def wo_load(cx, s, dc):
            p = dc % 2
            if dc >= 2:
                cx.wait(s, "sync", "pe", dd.get(("atmm", dc - 2)))
            dd[("wold", dc)] = cx.inc(f"ld_wo{p}", s.dma_start(
                out=wo_t[:, p, :, :],
                in_=wo_b.ap()[dc * 128:(dc + 1) * 128, :]
                    .rearrange("p (cc d) -> p cc d", d=128)), 16)

        def s_atail(cx, s):
            cx.wait(s, "sync", "cob", 16 * DCN)
            wo_load(cx, s, 0)
            wo_load(cx, s, 1)
            cx.wait(s, "sync", "ve", dd.get("rowinv"))
            def load(dc):
                p = dc % 2
                if dc >= 2:
                    cx.wait(s, "sync", "ve", dd.get(("aev", dc - 2)))
                dd[("xfld", dc)] = cx.inc(f"ld_xf{p}", s.dma_start(
                    out=xf_t[:, p, :], in_=x_fm[dc * 128:(dc + 1) * 128, :]), 16)
            load(0)
            load(1)
            for dc in range(DCN):
                cx.wait(s, "sync", "ve", dd.get(("aev", dc)))
                cx.inc(f"st_py{dc % 2}", s.dma_start(
                    out=partial.ap()[dc * 128:(dc + 1) * 128, :],
                    in_=py_t[:, dc % 2, :]), 16)
                if dc + 2 < DCN:
                    wo_load(cx, s, dc + 2)
                    load(dc + 2)

        def pe_at(cx, t):
            for dc in range(DCN):
                p = dc % 2
                cx.wait(t, "tensor", f"ld_wo{p}", dd.get(("wold", dc)))
                if dc >= 2:
                    cx.wait(t, "tensor", "ve", dd.get(("aev", dc - 2)))
                for cc in range(CCN):
                    for half in range(2):
                        m = t.matmul(psum_a[:, p, half * 512:(half + 1) * 512],
                                     wo_t[:, p, cc, :],
                                     outw[:, cc, half * 512:(half + 1) * 512],
                                     start=(cc == 0), stop=(cc == CCN - 1))
                dd[("atmm", dc)] = cx.inc("pe", m)

        def v_at(cx, v):
            cx.wait(v, "vector", "ve", dd.get("rmsabc"))
            for dc in range(DCN):
                p = dc % 2
                cx.wait(v, "vector", "pe", dd[("atmm", dc)])
                cx.wait(v, "vector", f"ld_xf{p}", dd.get(("xfld", dc)))
                if dc >= 2:
                    cx.wait(v, "vector", f"st_py{p}", 16 * (dc // 2))
                cx.inc("ve", v.tensor_tensor(py_t[:, p, :], psum_a[:, p, :],
                                             rms_a[:, :], OP.mult))
                cx.sw(v, "vector", "ve")
                dd[("aev", dc)] = cx.inc("ve", v.tensor_tensor(
                    py_t[:, p, :], py_t[:, p, :], xf_t[:, p, :], OP.add))

        cx._tag = "atail"
        cx.phase([("tensor", pe_at), ("vector", v_at), ("sync", s_atail)])
        pt_done0 = cx.cnt["st_py0"]
        pt_done1 = cx.cnt["st_py1"]

        if stage == "partial":
            def s_dumpp(cx, s):
                cx.wait(s, "sync", "st_py0", pt_done0)
                cx.wait(s, "sync", "st_py1", pt_done1)
                cx.inc("st_y", s.dma_start(out=dbg_pt, in_=partial.ap()), 16)
                if not cx.dry:
                    s.wait_ge(cx.sems["st_y"], cx.cnt["st_y"])
            cx.phase([("sync", s_dumpp)])
            cx.close_sems()
            return nc

        cx.barrier("b2", extra={
            "sync": ["st_py0", "st_py1", "ld_xf0", "ld_xf1", "ld_wv0",
                     "ld_wv1", "ld_wo0", "ld_wo1"]})
        al.free("outw", "wv_t", "wo_t", "xf_t", "py_t",
                "psum_a", "bc_ps", "rowa", "rowb", "rms_a",
                "wq1_sb", "wk1_sb", "wq2_sb", "wk2_sb", "t1_sb", "t2_sb",
                "xn_wm")

        hn = al.sb("hn", [128, HCN, T], BF16, side="right")
        wgu_t = al.sb("wgu_t", [128, 2, 2, ECN, 128], BF16)
        g_sb = al.sb("g_sb", [128, T], F32)
        gate_ps = al.ps("gate_ps", [128, 2, T], F32)
        up_ps = al.ps("up_ps", [128, 2, T], F32)

        pairs = [(P, r, k) for P in range(4) for r in range(4) for k in range(4)]

        def s_gul(cx, s):
            for j, (P, r, k) in enumerate(pairs):
                p = j % 2
                cx.wait(s, "sync", f"ccg{P}", 1)
                if j >= 2:
                    cx.wait(s, "sync", "pe", dd.get(("gumm", j - 2)))
                cx.inc(f"ld_wgu{p}", s.dma_start(
                    out=wgu_t[:, p, 0, :, :],
                    in_=ag_gu_out[P].ap()[r * 512 + k * 128:r * 512 + (k + 1) * 128, :]
                        .rearrange("p (ec h) -> p ec h", h=128)), 16)
                dd[("guld", j)] = cx.inc(f"ld_wgu{p}", s.dma_start(
                    out=wgu_t[:, p, 1, :, :],
                    in_=ag_gu_out[P].ap()[(r + 4) * 512 + k * 128:
                                          (r + 4) * 512 + (k + 1) * 128, :]
                        .rearrange("p (ec h) -> p ec h", h=128)), 16)

        def pe_gu(cx, t):
            for j in range(len(pairs)):
                p = j % 2
                cx.wait(t, "tensor", f"ld_wgu{p}", dd[("guld", j)])
                if j >= 2:
                    cx.wait(t, "tensor", "ve", dd.get(("hn", j - 2)))
                    cx.wait(t, "tensor", "sc", dd.get(("gsilu", j - 2)))
                for ec in range(ECN):
                    for g in range(2):
                        ps = gate_ps if g == 0 else up_ps
                        for half in range(2):
                            m = t.matmul(ps[:, p, half * 512:(half + 1) * 512],
                                         wgu_t[:, p, g, ec, :],
                                         xn_mn[:, ec,
                                               half * 512:(half + 1) * 512],
                                         start=(ec == 0), stop=(ec == ECN - 1))
                dd[("gumm", j)] = cx.inc("pe", m)

        def sc_gu(cx, e):
            for j in range(len(pairs)):
                p = j % 2
                cx.wait(e, "scalar", "pe", dd[("gumm", j)])
                if j >= 1:
                    cx.wait(e, "scalar", "ve", dd.get(("hn", j - 1)))
                dd[("gsilu", j)] = cx.inc("sc", e.activation(
                    out=g_sb[:, :], in_=gate_ps[:, p, :], func=AF.Silu))

        def v_gu(cx, v):
            for j, (P, r, k) in enumerate(pairs):
                p = j % 2
                hc = r * 16 + P * 4 + k
                cx.wait(v, "vector", "sc", dd[("gsilu", j)])
                cx.wait(v, "vector", "pe", dd[("gumm", j)])
                dd[("hn", j)] = cx.inc("ve", v.tensor_tensor(
                    hn[:, hc, :], g_sb[:, :], up_ps[:, p, :], OP.mult))

        cx._tag = "gu"
        cx.phase([("sync", s_gul), ("tensor", pe_gu), ("scalar", sc_gu),
                  ("vector", v_gu)])

        cx.barrier("b3", extra={"sync": ["ld_wgu0", "ld_wgu1"]})
        al.free("wgu_t", "g_sb", "gate_ps", "up_ps", "xn_mn")

        rowa = al.sb("rowa2", [1, T], F32)
        rowb = al.sb("rowb2", [1, T], F32)
        rms_f = al.sb("rms_f", [128, T], F32)
        sqf_sb = al.sb("sqf_sb", [128, 2, T], BF16)
        statf_ps = al.ps("statf_ps", [1, T], F32)

        def v_sqf(cx, v):
            cx.sw(v, "vector", "ve")
            for hc in range(HCN):
                p = hc % 2
                if hc >= 2:
                    cx.wait(v, "vector", "pe", dd.get(("sfmm", hc - 2)))
                dd[("sqf", hc)] = cx.inc("ve", v.tensor_tensor(
                    sqf_sb[:, p, :], hn[:, hc, :], hn[:, hc, :], OP.mult))

        def pe_sqf(cx, t):
            for hc in range(HCN):
                p = hc % 2
                cx.wait(t, "tensor", "ve", dd[("sqf", hc)])
                for half in range(2):
                    m = t.matmul(statf_ps[0:1, half * 512:(half + 1) * 512],
                                 ones_c[:, :],
                                 sqf_sb[:, p, half * 512:(half + 1) * 512],
                                 start=(hc == 0), stop=(hc == HCN - 1))
                dd[("sfmm", hc)] = cx.inc("pe", m)

        def v_rmsf1(cx, v):
            cx.wait(v, "vector", "pe", cx.cnt["pe"])
            dd["rowf"] = cx.inc("ve", v.tensor_copy(rowa[0:1, :],
                                                    statf_ps[0:1, :]))
        def sc_rmsf(cx, e):
            cx.wait(e, "scalar", "ve", dd["rowf"])
            dd["rowfsd"] = cx.inc("sc", e.activation(
                out=rowb[0:1, :], in_=rowa[0:1, :], func=AF.Sqrt,
                bias=cst[0:1, 2:3], scale=cst[0:1, 3:4]))
        def v_rmsf2(cx, v):
            cx.wait(v, "vector", "sc", dd["rowfsd"])
            cx.inc("ve", v.reciprocal(rowa[0:1, :], rowb[0:1, :]))
        cx._tag = "statf"
        cx.phase([("vector", v_sqf), ("tensor", pe_sqf),
                  ("vector", v_rmsf1), ("scalar", sc_rmsf),
                  ("vector", v_rmsf2)])

        al.free("statf_ps", "sqf_sb")
        psum_f = al.ps("psum_f", [128, 2, T], F32)
        bcf_ps = al.ps("bcf_ps", [128, T], F32)

        def pe_bcf(cx, t):
            cx.wait(t, "tensor", "ve", cx.cnt["ve"])
            for half in range(2):
                m = t.matmul(bcf_ps[:, half * 512:(half + 1) * 512], ones_r[:, :],
                             rowa[0:1, half * 512:(half + 1) * 512],
                             start=True, stop=True)
            dd["bcf"] = cx.inc("pe", m)
        def v_bcf(cx, v):
            cx.wait(v, "vector", "pe", dd["bcf"])
            cx.inc("ve", v.tensor_copy(rms_f[:, :], bcf_ps[:, :]))
        cx._tag = "statf"
        cx.phase([("tensor", pe_bcf), ("vector", v_bcf)])
        rmsf_rdy = cx.cnt["ve"]

        wp_t = al.sb("wp_t", [128, 2, HCN, 128], BF16)
        pt_t = al.sb("pt_t", [128, 2, T], F32)
        yv_t = al.sb("yv_t", [128, 2, T], F32)

        def s_ftail(cx, s):
            cx.wait(s, "sync", "ccwp", 1)
            cx.wait(s, "sync", "st_py0", pt_done0)
            cx.wait(s, "sync", "st_py1", pt_done1)
            cx.wait(s, "sync", "pe", dd.get("bcf"))
            cx.wait(s, "sync", "ve", rmsf_rdy)
            def load(dc):
                p = dc % 2
                if dc >= 2:
                    cx.wait(s, "sync", "pe", dd.get(("ftmm", dc - 2)))
                rr, dcl = dc // 2, dc % 2
                dd[("wpld", dc)] = cx.inc(f"ld_wp{p}", s.dma_start(
                    out=wp_t[:, p, :, :],
                    in_=ag_wp_out.ap()[rr * 256 + dcl * 128:
                                       rr * 256 + (dcl + 1) * 128, :]
                        .rearrange("p (hc d) -> p hc d", d=128)), 16)
                if dc >= 2:
                    cx.wait(s, "sync", "ve", dd.get(("fev", dc - 2)))
                dd[("ptld", dc)] = cx.inc(f"ld_pt{p}", s.dma_start(
                    out=pt_t[:, p, :],
                    in_=partial.ap()[dc * 128:(dc + 1) * 128, :]), 16)
            load(0)
            load(1)
            for dc in range(DCN):
                cx.wait(s, "sync", "ve", dd.get(("fev", dc)))
                cx.inc(f"st_y{dc % 2}", s.dma_start(
                    out=y[dc * 128:(dc + 1) * 128, :],
                    in_=yv_t[:, dc % 2, :]), 16)
                if dc + 2 < DCN:
                    load(dc + 2)
            if not cx.dry:
                s.wait_ge(cx.sems["st_y0"], cx.cnt["st_y0"])
                s.wait_ge(cx.sems["st_y1"], cx.cnt["st_y1"])

        def pe_ft(cx, t):
            for dc in range(DCN):
                p = dc % 2
                cx.wait(t, "tensor", f"ld_wp{p}", dd.get(("wpld", dc)))
                if dc >= 2:
                    cx.wait(t, "tensor", "ve", dd.get(("fev", dc - 2)))
                for hc in range(HCN):
                    for half in range(2):
                        m = t.matmul(psum_f[:, p, half * 512:(half + 1) * 512],
                                     wp_t[:, p, hc, :],
                                     hn[:, hc, half * 512:(half + 1) * 512],
                                     start=(hc == 0), stop=(hc == HCN - 1))
                dd[("ftmm", dc)] = cx.inc("pe", m)

        def v_ft(cx, v):
            cx.sw(v, "vector", "ve")
            for dc in range(DCN):
                p = dc % 2
                cx.wait(v, "vector", "pe", dd[("ftmm", dc)])
                cx.wait(v, "vector", f"ld_pt{p}", dd.get(("ptld", dc)))
                if dc >= 2:
                    cx.wait(v, "vector", f"st_y{p}", 16 * (dc // 2))
                cx.inc("ve", v.tensor_tensor(yv_t[:, p, :], psum_f[:, p, :],
                                             rms_f[:, :], OP.mult))
                cx.sw(v, "vector", "ve")
                dd[("fev", dc)] = cx.inc("ve", v.tensor_tensor(
                    yv_t[:, p, :], yv_t[:, p, :], pt_t[:, p, :], OP.add))

        cx._tag = "ftail"
        cx.phase([("tensor", pe_ft), ("vector", v_ft),
                  ("sync", s_ftail)])
        cx.close_sems()
    return nc


def make_in_maps(inputs):
    X = np.asarray(inputs["L0_attn_x"], dtype=np.float32)
    ws = {k: np.asarray(inputs["L0_attn_" + k], dtype=np.float32)
          for k in ("Wq1", "Wq2", "Wk1", "Wk2", "Wv", "Wo", "norm_w")}
    Wgu = np.asarray(inputs["L0_ffn_w_gate_up"], dtype=np.float32)
    Wp = np.asarray(inputs["L0_ffn_w_proj"], dtype=np.float32)
    in_maps = []
    for c in range(NCORES):
        b, hf = c // 2, c % 2
        seq = X[b]
        main = seq[hf * T:(hf + 1) * T]
        warm = np.zeros((W, E), np.float32) if hf == 0 else seq[T - W:T]
        in_maps.append({
            "x_tm": np.ascontiguousarray(np.concatenate([warm, main], axis=0)),
            "x_fm": np.ascontiguousarray(main.T),
            "wq1": ws["Wq1"], "wq2": ws["Wq2"], "wk1": ws["Wk1"],
            "wk2": ws["Wk2"], "wv": ws["Wv"], "wo": ws["Wo"],
            "nrm": ws["norm_w"],
            "wgu_sl": np.ascontiguousarray(
                Wgu[c * (GU // NCORES):(c + 1) * (GU // NCORES)]),
            "wp_sl": np.ascontiguousarray(
                Wp[c * (D // NCORES):(c + 1) * (D // NCORES)]),
        })
    return in_maps


def kernel(**inputs):
    nc = build_program("full")
    in_maps = make_in_maps(inputs)
    res = run_bass_kernel_spmd(nc, in_maps, core_ids=list(range(NCORES)))
    out = np.empty((4, 2048, E), np.float32)
    for c in range(NCORES):
        b, hf = c // 2, c % 2
        out[b, hf * T:(hf + 1) * T, :] = res.results[c]["y"].T
    return out


# version 16
# speedup vs baseline: 1.1488x; 1.1139x over previous
import numpy as np
import ml_dtypes
import concourse.bass as bass
import concourse.mybir as mybir
from concourse.bass_utils import run_bass_kernel_spmd

F32 = mybir.dt.float32
BF16 = mybir.dt.bfloat16
AF = mybir.ActivationFunctionType
OP = mybir.AluOpType
AX = mybir.AxisListType

E = 2048
ECN = 16
C = 4096
CCN = 32
HD = 128
H = 8192
HCN = 64
GU = 16384
D = 2048
DCN = 16
T = 1024
W = 128
TW = T + W
NCORES = 8
TPS = [(0, W, False), (W, 512, True), (W + 512, 512, True)]

BF16_EPS8 = float(np.float32(ml_dtypes.bfloat16(1e-8)))
THR = 0.50048828125
LN_EPS = 1e-5
ATTN_EPS = 1e-6
FFN_EPS = 1.1920928955078125e-07

ENGS = ("sync", "scalar", "vector", "tensor", "gpsimd")


class _DummyInstr:
    def then_inc(self, *a, **k):
        return self


class _DryEng:
    def __getattr__(self, name):
        return lambda *a, **k: _DummyInstr()


class Ctx:

    def __init__(self, nc, block):
        self.nc = nc
        self.block = block
        self.sems = {}
        self.cnt = {}
        self.lastw = {}
        self._live = []
        self.dry = False

    def sem(self, name):
        cm = self.nc.semaphore(name)
        self._live.append(cm)
        self.sems[name] = cm.__enter__()
        self.cnt[name] = 0

    def close_sems(self):
        for cm in reversed(self._live):
            cm.__exit__(None, None, None)

    def wait(self, eng, engname, name, val):
        if val is None:
            return
        v = val
        if v <= 0:
            return
        key = (engname, name)
        if self.lastw.get(key, -1) >= v:
            return
        if not self.dry:
            eng.wait_ge(self.sems[name], v)
        self.lastw[key] = v

    def sw(self, eng, engname, name):
        self.wait(eng, engname, name, self.cnt[name])

    def inc(self, name, instr, k=1):
        if not self.dry:
            instr.then_inc(self.sems[name], k)
        self.cnt[name] += k
        return self.cnt[name]

    def phase(self, bodies):
        snap_cnt = dict(self.cnt)
        snap_w = dict(self.lastw)
        self.dry = True
        de = _DryEng()
        for name, fn in bodies:
            fn(self, de)
        dry_cnt = dict(self.cnt)
        self.cnt = snap_cnt
        self.lastw = snap_w
        self.dry = False
        tagn = getattr(self, "_tag", None)
        for name, fn in bodies:
            if tagn:
                with self.nc.named_scope(tagn):
                    getattr(self.block, name)(lambda eng, fn=fn: fn(self, eng))
            else:
                getattr(self.block, name)(lambda eng, fn=fn: fn(self, eng))
        assert self.cnt == dry_cnt, (self.cnt, dry_cnt)

    def barrier(self, tag, extra=None, engines=ENGS):
        tgt = self.cnt["ph"] + len(engines)
        extra = extra or {}
        def bfn(nm):
            def f(cx, e):
                for s in extra.get(nm, ()):
                    cx.wait(e, nm, s, cx.cnt[s])
                if not cx.dry:
                    if nm in ("scalar", "vector", "tensor"):
                        e.drain()
                    e.sem_inc(cx.sems["ph"], 1)
                    e.wait_ge(cx.sems["ph"], tgt)
                cx.lastw[(nm, "ph")] = tgt
            return f
        for nm in engines:
            getattr(self.block, nm)(lambda eng, nm=nm: bfn(nm)(self, eng))
            self.lastw[(nm, "ph")] = tgt
        self.cnt["ph"] = tgt


class Alloc:
    def __init__(self, nc):
        self.nc = nc
        self.cms = {}
        self.order = {}
        self.n = 0

    def sb(self, name, shape, dtype, side="left"):
        cm = self.nc.sbuf_tensor(name, shape, dtype, side=side)
        h = cm.__enter__()
        self.cms[name] = cm
        self.order[name] = self.n
        self.n += 1
        return h

    def ps(self, name, shape, dtype):
        cm = self.nc.psum_tensor(name, shape, dtype)
        h = cm.__enter__()
        self.cms[name] = cm
        self.order[name] = self.n
        self.n += 1
        return h

    def free(self, *names):
        for n in sorted(names, key=lambda x: -self.order[x]):
            self.cms.pop(n).__exit__(None, None, None)


def build_program(stage="full"):
    lvl = {"xn": 1, "attn": 2, "partial": 3, "full": 5}[stage]
    nc = bass.Bass("TRN2", target_bir_lowering=False, debug=False)

    x_tm = nc.dram_tensor("x_tm", [TW, E], F32, kind="ExternalInput").ap()
    x_fm = nc.dram_tensor("x_fm", [E, T], F32, kind="ExternalInput").ap()
    wq1 = nc.dram_tensor("wq1", [E, HD], F32, kind="ExternalInput").ap()
    wq2 = nc.dram_tensor("wq2", [HD, C], F32, kind="ExternalInput").ap()
    wk1 = nc.dram_tensor("wk1", [E, HD], F32, kind="ExternalInput").ap()
    wk2 = nc.dram_tensor("wk2", [HD, C], F32, kind="ExternalInput").ap()
    wv = nc.dram_tensor("wv", [E, C], F32, kind="ExternalInput").ap()
    wo = nc.dram_tensor("wo", [C, D], F32, kind="ExternalInput").ap()
    nrm = nc.dram_tensor("nrm", [C], F32, kind="ExternalInput").ap()
    wgu_sl = nc.dram_tensor("wgu_sl", [GU // NCORES, E], F32, kind="ExternalInput").ap()
    wp_sl = nc.dram_tensor("wp_sl", [D // NCORES, H], F32, kind="ExternalInput").ap()

    if lvl >= 5:
        y = nc.dram_tensor("y", [D, T], F32, kind="ExternalOutput").ap()
    if stage == "xn":
        dbg_xn = nc.dram_tensor("dbg_xn", [E, TW], BF16, kind="ExternalOutput").ap()
    if stage == "attn":
        dbg_ow = nc.dram_tensor("dbg_ow", [C, T], BF16, kind="ExternalOutput").ap()
        dbg_st = nc.dram_tensor("dbg_st", [1, T], F32, kind="ExternalOutput").ap()
    if stage == "partial":
        dbg_pt = nc.dram_tensor("dbg_pt", [D, T], F32, kind="ExternalOutput").ap()

    ag_gu_in = [nc.dram_tensor(f"ag_gu_in{i}", [512, E], BF16) for i in range(4)]
    ag_gu_out = [nc.dram_tensor(f"ag_gu_out{i}", [NCORES * 512, E], BF16,
                                addr_space="Shared")
                 for i in range(4)]
    ag_wp_in = nc.dram_tensor("ag_wp_in", [D // NCORES, H], BF16)
    ag_wp_out = nc.dram_tensor("ag_wp_out", [NCORES * (D // NCORES), H], BF16,
                               addr_space="Shared")
    partial = nc.dram_tensor("partial", [D, T], F32)
    wv_b = nc.dram_tensor("wv_b", [CCN * 128, E], BF16)
    wo_b = nc.dram_tensor("wo_b", [DCN * 128, C], BF16)

    al = Alloc(nc)
    nrm_sb = al.sb("nrm_sb", [128, CCN], F32, side="right")
    ones_c = al.sb("ones_c", [128, 1], BF16, side="right")
    ones_r = al.sb("ones_r", [1, 128], F32, side="right")
    epsc = al.sb("epsc", [128, 1], F32, side="right")
    cst = al.sb("cst", [1, 4], F32, side="right")
    xn_mn = al.sb("xn_mn", [128, ECN, T], BF16)
    xn_wm = al.sb("xn_wm", [128, ECN, W], BF16)
    wq1_sb = al.sb("wq1_sb", [128, ECN, HD], BF16)
    wk1_sb = al.sb("wk1_sb", [128, ECN, HD], BF16)
    wq2_sb = al.sb("wq2_sb", [128, C], BF16)
    wk2_sb = al.sb("wk2_sb", [128, C], BF16)
    t1_sb = al.sb("t1_sb", [128, T], BF16)
    t2_sb = al.sb("t2_sb", [128, TW], BF16)

    with nc.Block() as block:
        cx = Ctx(nc, block)
        for s in ("ph", "pe", "sc", "ve", "ldsm", "st_y", "cvb0", "cvb1", "cvb2", "cvb3", "cob",
                  "st_ag0", "st_ag1", "st_wp0", "st_wp1", "st_y0", "st_y1",
                  "ccg0", "ccg1", "ccg2", "ccg3", "ccwp",
                  "trs_q0", "trs_q1", "trs_w0", "trs_w1"):
            cx.sem(s)
        for s in ("ld_x", "cast", "cwp", "trs_ln", "ld_wv", "ld_wo",
                  "ld_wgu", "ld_wp", "ld_xf", "ld_pt", "st_py"):
            cx.sem(s + "0")
            cx.sem(s + "1")

        NGQ, NPQ = 8, 4
        wgu_buf = al.sb("wgu_buf", [128, 2, 2, E], BF16)
        wp_buf = al.sb("wp_buf", [128, 2, 4096], BF16)
        pos_sb = al.sb("pos_sb", [128, 4096], BF16)
        neg_sb = al.sb("neg_sb", [128, 4096], BF16)
        red_sb = al.sb("red_sb", [128, 64], F32)
        scl_sb = al.sb("scl_sb", [128, 64], BF16)
        thr_sb = al.sb("thr_sb", [128, 64], F32)
        nthr_sb = al.sb("nthr_sb", [128, 64], F32)
        tt_sb = al.sb("tt_sb", [128, 2, 16, 128], BF16)
        tt_wp = al.sb("tt_wp", [128, 2, 32, 128], BF16)
        xt_sb = al.sb("xt_sb", [128, 2, E], F32)
        xnt_sb = al.sb("xnt_sb", [128, 2, E], BF16)
        sum_sb = al.sb("sum_sb", [128, 2, 1], F32)
        ssq_sb = al.sb("ssq_sb", [128, 2, 1], F32)
        mean_sb = al.sb("mean_sb", [128, 2, 1], F32)
        var_sb = al.sb("var_sb", [128, 2, 1], F32)
        sd_sb = al.sb("sd_sb", [128, 2, 1], F32)
        inv_sb = al.sb("inv_sb", [128, 2, 1], F32)

        def xn_sl(ec, off, sz):
            if off < W:
                assert off + sz <= W
                return xn_wm[:, ec, off:off + sz]
            return xn_mn[:, ec, off - W:off - W + sz]

        dd = {}
        NTS = TW // 128

        def quant_ops(cx, v, wbuf, G):
            view = wbuf.rearrange("p (g k) -> p g k", k=64)
            rd = red_sb[:, 0:G]
            sc_ = scl_sb[:, 0:G]
            th = thr_sb[:, 0:G]
            nth = nthr_sb[:, 0:G]
            pv = pos_sb[:, 0:G * 64].rearrange("p (g k) -> p g k", k=64)
            nv = neg_sb[:, 0:G * 64].rearrange("p (g k) -> p g k", k=64)
            cx.inc("ve", v.tensor_reduce(rd, view, AX.X, OP.add,
                                          apply_absolute_value=True))
            cx.sw(v, "vector", "ve")
            cx.inc("ve", v.tensor_scalar(sc_, rd, 1.0 / 64.0, None, OP.mult))
            cx.sw(v, "vector", "ve")
            cx.inc("ve", v.tensor_scalar_max(sc_, sc_, BF16_EPS8))
            cx.sw(v, "vector", "ve")
            v.tensor_scalar(th, sc_, THR, None, OP.mult)
            cx.inc("ve", v.tensor_scalar(nth, sc_, -THR, None, OP.mult))
            cx.sw(v, "vector", "ve")
            v.tensor_tensor(pv, view, th.unsqueeze(2).broadcast_to([128, G, 64]),
                            OP.is_gt)
            cx.inc("ve", v.tensor_tensor(
                nv, view, nth.unsqueeze(2).broadcast_to([128, G, 64]), OP.is_lt))
            cx.sw(v, "vector", "ve")
            cx.inc("ve", v.tensor_tensor(pv, pv, nv, OP.subtract))
            cx.sw(v, "vector", "ve")
            return v.tensor_tensor(view, pv,
                                   sc_.unsqueeze(2).broadcast_to([128, G, 64]),
                                   OP.mult)

        def g_prep(cx, g):
            i = cx.inc
            i("ldsm", g.dma_start(out=wq1_sb[:, :, :],
                                  in_=wq1.rearrange("(a p) m -> p a m", p=128)), 16)
            i("ldsm", g.dma_start(out=wk1_sb[:, :, :],
                                  in_=wk1.rearrange("(a p) m -> p a m", p=128)), 16)
            i("ldsm", g.dma_start(out=wq2_sb[:, :], in_=wq2), 16)
            i("ldsm", g.dma_start(out=wk2_sb[:, :], in_=wk2), 16)
            i("ldsm", g.dma_start(out=nrm_sb[:, :],
                                  in_=nrm.rearrange("(a p) -> p a", p=128),
                                  allow_slow_non_contiguous=True), 16)
            dd["ldsm"] = cx.cnt["ldsm"]
            for q in range(NGQ):
                p = q % 2
                if q >= 2:
                    cx.wait(g, "gpsimd", "trs_q0", dd.get(("gu_trs0", q - 2)))
                    cx.wait(g, "gpsimd", "trs_q1", dd.get(("gu_trs1", q - 2)))
                i(f"cast{p}", g.dma_start(
                    out=wgu_buf[:, p, :, :],
                    in_=wgu_sl[q * 256:(q + 1) * 256, :]
                        .rearrange("(hc p) d -> p hc d", p=128)), 16)
            for q in range(4):
                p = q % 2
                if q >= 2:
                    cx.wait(g, "gpsimd", "trs_w0", dd.get(("wp_trs0", q - 2)))
                    cx.wait(g, "gpsimd", "trs_w1", dd.get(("wp_trs1", q - 2)))
                dcl, half = q // 2, q % 2
                i(f"cwp{p}", g.dma_start(
                    out=wp_buf[:, p, :],
                    in_=wp_sl[dcl * 128:(dcl + 1) * 128,
                              half * 4096:(half + 1) * 4096]), 16)


        def v_memset(cx, v):
            if not cx.dry:
                v.memset(ones_c[:, :], 1.0)
                v.memset(ones_r[:, :], 1.0)
                v.memset(epsc[:, :], LN_EPS)
                v.memset(cst[0:1, 0:1], ATTN_EPS)
                v.memset(cst[0:1, 1:2], 1.0 / C)
                v.memset(cst[0:1, 2:3], FFN_EPS)
                v.memset(cst[0:1, 3:4], 1.0 / H)

        def v_prep(cx, v):
            for q in range(NGQ):
                p = q % 2
                cx.wait(v, "vector", f"cast{p}", 16 * (q // 2 + 1))
                dd[("gu_q", q)] = cx.inc(
                    "ve", quant_ops(cx, v, wgu_buf[:, p, :, :]
                                    .rearrange("p a d -> p (a d)"), 64))
            for q in range(4):
                p = q % 2
                cx.wait(v, "vector", f"cwp{p}", 16 * (q // 2 + 1))
                dd[("wp_q", q)] = cx.inc("ve", quant_ops(cx, v, wp_buf[:, p, :], 64))

        def s_prep(cx, s):
            for q in range(NGQ):
                p = q % 2
                cx.wait(s, "sync", "ve", dd[("gu_q", q)])
                for i2 in range(2):
                    hcl = q * 2 + i2
                    P, col = hcl // 4, hcl % 4
                    pt = hcl % 2
                    if hcl >= 2:
                        cx.wait(s, "sync", f"st_ag{pt}", dd.get(("gu_st", hcl - 2)))
                    cx.inc(f"trs_q{pt}", s.dma_start_transpose(
                        tt_sb[:, pt, :, :],
                        wgu_buf[:, p, i2, :].rearrange("p (dc r) -> p dc r", r=128)),
                        16)
                    dd[("gu_trs" + str(pt), q)] = cx.cnt[f"trs_q{pt}"]
                    cx.wait(s, "sync", f"trs_q{pt}", cx.cnt[f"trs_q{pt}"])
                    dd[("gu_st", hcl)] = cx.inc(f"st_ag{pt}", s.dma_start(
                        out=ag_gu_in[P].ap()[col * 128:(col + 1) * 128, :]
                            .rearrange("p (ec h) -> p ec h", h=128),
                        in_=tt_sb[:, pt, :, :]), 16)
                    if hcl % 4 == 3:
                        dd[("gu_ag0", P)] = cx.cnt["st_ag0"]
                        dd[("gu_ag1", P)] = cx.cnt["st_ag1"]
            for q in range(4):
                p = q % 2
                dcl, half = q // 2, q % 2
                pt = q % 2
                cx.wait(s, "sync", "ve", dd[("wp_q", q)])
                if q >= 2:
                    cx.wait(s, "sync", f"st_wp{pt}", dd.get(("wp_st", q - 2)))
                cx.inc(f"trs_w{pt}", s.dma_start_transpose(
                    tt_wp[:, pt, :, :],
                    wp_buf[:, p, :].rearrange("p (hc r) -> p hc r", r=128)), 16)
                dd[("wp_trs" + str(pt), q)] = cx.cnt[f"trs_w{pt}"]
                cx.wait(s, "sync", f"trs_w{pt}", cx.cnt[f"trs_w{pt}"])
                dd[("wp_st", q)] = cx.inc(f"st_wp{pt}", s.dma_start(
                    out=ag_wp_in.ap()[dcl * 128:(dcl + 1) * 128,
                                      half * 4096:(half + 1) * 4096]
                        .rearrange("p (hc d) -> p hc d", d=128),
                    in_=tt_wp[:, pt, :, :]), 16)
                dd[("wp_ag0", q)] = cx.cnt["st_wp0"]
                dd[("wp_ag1", q)] = cx.cnt["st_wp1"]

        def sc_ln(cx, e):
            def process(ts):
                cx.wait(e, "scalar", "ve", dd.get(("ln_var", ts)))
                dd[("ln_sd", ts)] = cx.inc("sc", e.activation(
                    out=sd_sb[:, ts % 2, :], in_=var_sb[:, ts % 2, :],
                    func=AF.Sqrt, bias=epsc[:, 0:1]))
            for ts in range(NTS):
                p = ts % 2
                if ts >= 2:
                    cx.wait(e, "scalar", "ve", dd.get(("ln_xn", ts - 2)))
                dd[("ln_x", ts)] = cx.inc(f"ld_x{p}", e.dma_start(
                    out=xt_sb[:, p, :], in_=x_tm[ts * 128:(ts + 1) * 128, :]), 16)
                if ts >= 1:
                    process(ts - 1)
            process(NTS - 1)

        def v_ln(cx, v):
            for ts in range(NTS):
                p = ts % 2
                cx.wait(v, "vector", f"ld_x{p}", dd[("ln_x", ts)])
                if ts >= 2:
                    cx.wait(v, "vector", f"trs_ln{p}", dd.get(("ln_tr", ts - 2)))
                cx.inc("ve", v.scalar_tensor_tensor(
                    xnt_sb[:, p, :], xt_sb[:, p, :], 1.0, xt_sb[:, p, :],
                    OP.mult, OP.mult, accum_out=ssq_sb[:, p, :]))
                cx.inc("ve", v.reduce_sum(sum_sb[:, p, :], xt_sb[:, p, :], AX.X))
                cx.sw(v, "vector", "ve")
                cx.inc("ve", v.tensor_scalar(mean_sb[:, p, :], sum_sb[:, p, :],
                                             1.0 / E, None, OP.mult))
                v.tensor_scalar(var_sb[:, p, :], ssq_sb[:, p, :], 1.0 / E,
                                None, OP.mult)
                cx.sw(v, "vector", "ve")
                cx.inc("ve", v.tensor_tensor(inv_sb[:, p, :], mean_sb[:, p, :],
                                             mean_sb[:, p, :], OP.mult))
                cx.sw(v, "vector", "ve")
                dd[("ln_var", ts)] = cx.inc("ve", v.tensor_tensor(
                    var_sb[:, p, :], var_sb[:, p, :], inv_sb[:, p, :],
                    OP.subtract))
                cx.wait(v, "vector", "sc", dd[("ln_sd", ts)])
                cx.inc("ve", v.reciprocal(inv_sb[:, p, :], sd_sb[:, p, :]))
                cx.sw(v, "vector", "ve")
                dd[("ln_xn", ts)] = cx.inc("ve", v.tensor_scalar(
                    xnt_sb[:, p, :], xt_sb[:, p, :], mean_sb[:, p, :],
                    inv_sb[:, p, :], OP.subtract, OP.mult))

        def s_lntr(cx, s):
            for ts in range(NTS):
                p = ts % 2
                cx.wait(s, "sync", "ve", dd[("ln_xn", ts)])
                nw = W // 128
                if ts < nw:
                    dst = xn_wm[:, :, ts * 128:(ts + 1) * 128]
                else:
                    dst = xn_mn[:, :, (ts - nw) * 128:(ts - nw + 1) * 128]
                cx.inc(f"trs_ln{p}", s.dma_start_transpose(
                    dst,
                    xnt_sb[:, p, :].rearrange("p (ec r) -> p ec r", r=128)), 16)
                dd[("ln_tr", ts)] = cx.cnt[f"trs_ln{p}"]

        cx._tag = "p0ln"
        cx.phase([("vector", v_memset), ("scalar", sc_ln), ("vector", v_ln),
                  ("sync", s_lntr), ("vector", v_prep), ("sync", s_prep),
                  ("gpsimd", g_prep)])

        xn_rdy0 = cx.cnt["trs_ln0"]
        xn_rdy1 = cx.cnt["trs_ln1"]

        if stage == "xn":
            def s_dump(cx, s):
                cx.wait(s, "sync", "trs_ln0", xn_rdy0)
                cx.wait(s, "sync", "trs_ln1", xn_rdy1)
                for ec in range(ECN):
                    cx.inc("st_y", s.dma_start(
                        out=dbg_xn[ec * 128:(ec + 1) * 128, 0:W],
                        in_=xn_wm[:, ec, :]), 16)
                    cx.inc("st_y", s.dma_start(
                        out=dbg_xn[ec * 128:(ec + 1) * 128, W:TW],
                        in_=xn_mn[:, ec, :]), 16)
                if not cx.dry:
                    s.wait_ge(cx.sems["st_y"], cx.cnt["st_y"])
            cx.phase([("sync", s_dump)])
            cx.close_sems()
            return nc

        cx.barrier("b1", extra={
            "sync": ["trs_ln0", "trs_ln1", "st_ag0", "st_ag1", "st_wp0",
                     "st_wp1", "trs_q0", "trs_q1", "trs_w0", "trs_w1"],
            "gpsimd": ["ldsm", "cast0", "cast1", "cwp0", "cwp1"],
            "scalar": ["ld_x0", "ld_x1"]})
        al.free("wgu_buf", "wp_buf", "pos_sb", "neg_sb", "red_sb", "scl_sb",
                "thr_sb", "nthr_sb", "tt_sb", "tt_wp", "xt_sb", "xnt_sb",
                "sum_sb", "ssq_sb", "mean_sb", "var_sb", "sd_sb", "inv_sb")

        outw = al.sb("outw", [128, CCN, T], BF16)
        wv_t = al.sb("wv_t", [128, 2, ECN, 128], BF16)
        wo_t = al.sb("wo_t", [128, 2, CCN, 128], BF16)
        rowa = al.sb("rowa", [1, T], F32)
        rowb = al.sb("rowb", [1, T], F32)
        rms_a = al.sb("rms_a", [128, T], F32)
        f_sb = al.sb("f_sb", [128, 2, 512], F32)
        a_sb = al.sb("a_sb", [128, 2, 512], F32)
        k_sb = al.sb("k_sb", [128, 2, 512], F32)
        b_sb = al.sb("b_sb", [128, 2, 512], F32)
        h_sb = al.sb("h_sb", [128, 2, 512], F32)
        q_sb = al.sb("q_sb", [128, 2, 512], F32)
        o_sb = al.sb("o_sb", [128, 2, 512], F32)
        sq_sb = al.sb("sq_sb", [128, 2, 512], BF16)

        logf_ps = al.ps("logf_ps", [128, 2, 512], F32)
        v_ps = al.ps("v_ps", [128, 2, 512], F32)
        q_ps = al.ps("q_ps", [128, 2, 512], F32)
        stat_ps = al.ps("stat_ps", [1, T], F32)

        steps = [(cc, tpi) for cc in range(CCN) for tpi in range(3)]
        qsteps = set(s for s in steps if TPS[s[1]][2])

        def pe_t12(cx, t):
            cx.wait(t, "tensor", "ldsm", dd["ldsm"])
            cx.wait(t, "tensor", "trs_ln0", xn_rdy0)
            cx.wait(t, "tensor", "trs_ln1", xn_rdy1)
            gi = 0
            for (name, wsb, pieces) in (
                    ("t2", wk1_sb, TPS),
                    ("t1", wq1_sb, [TPS[1], TPS[2]])):
                for pi, (off, sz, _) in enumerate(pieces):
                    pg = gi % 2
                    if gi >= 2:
                        cx.wait(t, "tensor", "ve", dd.get(("t12ev", gi - 2)))
                    ps = logf_ps[:, pg, 0:sz]
                    for ec in range(ECN):
                        m = t.matmul(ps, wsb[:, ec, :], xn_sl(ec, off, sz),
                                     start=(ec == 0), stop=(ec == ECN - 1))
                    dd[("t12mm", gi)] = cx.inc("pe", m)
                    gi += 1

        def v_t12(cx, v):
            gi = 0
            for (name, dst, pieces) in (
                    ("t2", t2_sb, TPS),
                    ("t1", t1_sb, [TPS[1], TPS[2]])):
                for pi, (off, sz, _) in enumerate(pieces):
                    pg = gi % 2
                    cx.wait(v, "vector", "pe", dd[("t12mm", gi)])
                    doff = off if name == "t2" else off - W
                    dd[("t12ev", gi)] = cx.inc("ve", v.tensor_copy(
                        dst[:, doff:doff + sz], logf_ps[:, pg, 0:sz]))
                    dd[(name + "ev", pi)] = dd[("t12ev", gi)]
                    gi += 1

        def g_post(cx, g):
            i = cx.inc
            def ag(P):
                if P < NPQ:
                    cx.wait(g, "gpsimd", "st_ag0", dd[("gu_ag0", P)])
                    cx.wait(g, "gpsimd", "st_ag1", dd[("gu_ag1", P)])
                    i(f"ccg{P}", g.collective_compute(
                        "AllGather", OP.bypass,
                        replica_groups=[list(range(NCORES))],
                        ins=[ag_gu_in[P].ap().opt()],
                        outs=[ag_gu_out[P].ap().opt()]), 1)
                else:
                    cx.wait(g, "gpsimd", "st_wp0", dd[("wp_ag0", 3)])
                    cx.wait(g, "gpsimd", "st_wp1", dd[("wp_ag1", 3)])
                    i("ccwp", g.collective_compute(
                        "AllGather", OP.bypass,
                        replica_groups=[list(range(NCORES))],
                        ins=[ag_wp_in.ap().opt()],
                        outs=[ag_wp_out.ap().opt()]), 1)
            for cc in range(CCN):
                i(f"cvb{cc // 8}", g.dma_start(
                    out=wv_b.ap()[cc * 128:(cc + 1) * 128, :]
                        .rearrange("p (ec c) -> p ec c", c=128),
                    in_=wv[:, cc * 128:(cc + 1) * 128]
                        .rearrange("(ec p) c -> p ec c", p=128)), 16)
            for dc in range(DCN):
                i("cob", g.dma_start(
                    out=wo_b.ap()[dc * 128:(dc + 1) * 128, :]
                        .rearrange("p (cc d) -> p cc d", d=128),
                    in_=wo[:, dc * 128:(dc + 1) * 128]
                        .rearrange("(cc p) d -> p cc d", p=128)), 16)
            for P in range(NPQ + 1):
                ag(P)

        def s_wv(cx, s):
            for cc in range(CCN):
                p = cc % 2
                cx.wait(s, "sync", f"cvb{cc // 8}", 128)
                if cc >= 2:
                    cx.wait(s, "sync", "pe", dd.get(("vmm", cc - 2, 2)))
                dd[("wvld", cc)] = cx.inc(f"ld_wv{p}", s.dma_start(
                    out=wv_t[:, p, :, :],
                    in_=wv_b.ap()[cc * 128:(cc + 1) * 128, :]
                        .rearrange("p (ec c) -> p ec c", c=128)), 16)

        def pe_cc(cx, t):
            last_par = {}
            cx.wait(t, "tensor", "ve", dd.get(("t12ev", 4)))
            si = 0
            pend = []
            for cc in range(CCN):
                pwv = cc % 2
                cx.wait(t, "tensor", f"ld_wv{pwv}", dd[("wvld", cc)])
                for tpi, (off, sz, hasq) in enumerate(TPS):
                    s = (cc, tpi)
                    p = si % 2
                    if si >= 2:
                        cx.wait(t, "tensor", "sc", dd.get(("f", steps[si - 2])))
                    cx.wait(t, "tensor", "ve", dd[("t2ev", tpi)])
                    m = t.matmul(logf_ps[:, p, 0:sz],
                                 wk2_sb[:, cc * 128:(cc + 1) * 128],
                                 t2_sb[:, off:off + sz], start=True, stop=True)
                    dd[("logf", s)] = cx.inc("pe", m)
                    if si >= 2:
                        cx.wait(t, "tensor", "ve", dd.get(("b", steps[si - 2])))
                    for ec in range(ECN):
                        m = t.matmul(v_ps[:, p, 0:sz], wv_t[:, pwv, ec, :],
                                     xn_sl(ec, off, sz),
                                     start=(ec == 0), stop=(ec == ECN - 1))
                    dd[("vmm", cc, tpi)] = cx.inc("pe", m)
                    if hasq:
                        prevq = last_par.get(("qps", p))
                        if prevq is not None:
                            cx.wait(t, "tensor", "ve", dd.get(("o", prevq)))
                        last_par[("qps", p)] = s
                        cx.wait(t, "tensor", "ve", dd[("t1ev", tpi - 1)])
                        m = t.matmul(q_ps[:, p, 0:sz],
                                     wq2_sb[:, cc * 128:(cc + 1) * 128],
                                     t1_sb[:, off - W:off - W + sz],
                                     start=True, stop=True)
                        dd[("qmm", s)] = cx.inc("pe", m)
                        pend.append((s, p))
                    if len(pend) > 2:
                        stat_emit(cx, t, *pend.pop(0))
                    si += 1
            for it in pend:
                stat_emit(cx, t, *it)

        def stat_emit(cx, t, s, p):
            cc, tpi = s
            off = TPS[tpi][0] - W
            cx.wait(t, "tensor", "ve", dd.get(("sq", s)))
            m = t.matmul(stat_ps[0:1, off:off + 512], ones_c[:, :],
                         sq_sb[:, p, :], start=(cc == 0), stop=(cc == CCN - 1))
            dd[("stat", s)] = cx.inc("pe", m)

        def sc_cc(cx, e):
            last_par = {}
            si = 0
            pend = []
            for cc in range(CCN):
                for tpi, (off, sz, hasq) in enumerate(TPS):
                    s = (cc, tpi)
                    p = si % 2
                    cx.wait(e, "scalar", "pe", dd[("logf", s)])
                    if si >= 2 and steps[si - 2] in qsteps:
                        cx.wait(e, "scalar", "ve", dd.get(("o", steps[si - 2])))
                    dd[("f", s)] = cx.inc("sc", e.activation(
                        out=f_sb[:, p, 0:sz], in_=logf_ps[:, p, 0:sz],
                        func=AF.Sigmoid))
                    cx.sw(e, "scalar", "sc")
                    if si >= 2:
                        cx.wait(e, "scalar", "ve", dd.get(("scan", steps[si - 2])))
                    dd[("a", s)] = cx.inc("sc", e.activation(
                        out=a_sb[:, p, 0:sz], in_=f_sb[:, p, 0:sz],
                        func=AF.Sigmoid))
                    if hasq:
                        cx.wait(e, "scalar", "pe", dd[("qmm", s)])
                        prevq = last_par.get(("qsb", p))
                        if prevq is not None:
                            cx.wait(e, "scalar", "ve", dd.get(("o1", prevq)))
                        last_par[("qsb", p)] = s
                        dd[("sig", s)] = cx.inc("sc", e.activation(
                            out=q_sb[:, p, 0:sz], in_=q_ps[:, p, 0:sz],
                            func=AF.Sigmoid))
                    si += 1

        def v_cc(cx, v):
            last_par = {}
            si = 0
            for cc in range(CCN):
                for tpi, (off, sz, hasq) in enumerate(TPS):
                    s = (cc, tpi)
                    p = si % 2
                    cx.wait(v, "vector", "sc", dd[("a", s)])
                    cx.inc("ve", v.tensor_scalar(k_sb[:, p, 0:sz],
                                                 a_sb[:, p, 0:sz],
                                                 -1.0, 1.0, OP.mult, OP.add))
                    cx.sw(v, "vector", "ve")
                    cx.wait(v, "vector", "pe", dd[("vmm", cc, tpi)])
                    dd[("b", s)] = cx.inc("ve", v.tensor_tensor(
                        b_sb[:, p, 0:sz], k_sb[:, p, 0:sz], v_ps[:, p, 0:sz],
                        OP.mult))
                    cx.sw(v, "vector", "ve")
                    if tpi == 0:
                        init = 0.0
                    else:
                        psz = TPS[tpi - 1][1]
                        init = h_sb[:, 1 - p, psz - 1:psz]
                    dd[("scan", s)] = cx.inc("ve", v.tensor_tensor_scan(
                        h_sb[:, p, 0:sz], a_sb[:, p, 0:sz], b_sb[:, p, 0:sz],
                        init, OP.mult, OP.add))
                    cx.sw(v, "vector", "ve")
                    if hasq:
                        cx.wait(v, "vector", "sc", dd[("sig", s)])
                        dd[("o1", s)] = cx.inc("ve", v.tensor_tensor(
                            f_sb[:, p, 0:sz], q_sb[:, p, 0:sz],
                            h_sb[:, p, 0:sz], OP.mult))
                        cx.sw(v, "vector", "ve")
                        dd[("o", s)] = cx.inc("ve", v.tensor_tensor(
                            o_sb[:, p, 0:sz], f_sb[:, p, 0:sz],
                            q_ps[:, p, 0:sz], OP.mult))
                        cx.sw(v, "vector", "ve")
                        prevq = last_par.get(("sqb", p))
                        if prevq is not None:
                            cx.wait(v, "vector", "pe", dd.get(("stat", prevq)))
                        last_par[("sqb", p)] = s
                        dd[("sq", s)] = cx.inc("ve", v.tensor_tensor(
                            sq_sb[:, p, :], o_sb[:, p, 0:sz], o_sb[:, p, 0:sz],
                            OP.mult))
                        dd[("outw", s)] = cx.inc("ve", v.tensor_scalar(
                            outw[:, cc, off - W:off - W + sz], o_sb[:, p, 0:sz],
                            nrm_sb[:, cc:cc + 1], None, OP.mult))
                    si += 1

        cx._tag = "attn"
        cx.phase([("gpsimd", g_post),
                  ("tensor", pe_t12), ("vector", v_t12), ("sync", s_wv),
                  ("tensor", pe_cc), ("scalar", sc_cc), ("vector", v_cc)])

        if stage == "attn":
            def v_dump(cx, v):
                cx.wait(v, "vector", "pe", cx.cnt["pe"])
                dd["strow"] = cx.inc("ve", v.tensor_copy(rowa[0:1, :],
                                                         stat_ps[0:1, :]))
            def s_dumpa(cx, s):
                cx.wait(s, "sync", "ve", cx.cnt["ve"])
                for cc in range(CCN):
                    cx.inc("st_y", s.dma_start(
                        out=dbg_ow[cc * 128:(cc + 1) * 128, :],
                        in_=outw[:, cc, :]), 16)
                cx.inc("st_y", s.dma_start(out=dbg_st, in_=rowa[0:1, :]), 16)
                if not cx.dry:
                    s.wait_ge(cx.sems["st_y"], cx.cnt["st_y"])
            cx.phase([("vector", v_dump), ("sync", s_dumpa)])
            cx.close_sems()
            return nc

        def v_rmsa1(cx, v):
            cx.wait(v, "vector", "pe", cx.cnt["pe"])
            dd["rowa"] = cx.inc("ve", v.tensor_copy(rowa[0:1, :],
                                                    stat_ps[0:1, :]))
        def sc_rmsa(cx, e):
            cx.wait(e, "scalar", "ve", dd["rowa"])
            dd["rowsd"] = cx.inc("sc", e.activation(
                out=rowb[0:1, :], in_=rowa[0:1, :], func=AF.Sqrt,
                bias=cst[0:1, 0:1], scale=cst[0:1, 1:2]))
        def v_rmsa2(cx, v):
            cx.wait(v, "vector", "sc", dd["rowsd"])
            dd["rowinv"] = cx.inc("ve", v.reciprocal(rowa[0:1, :], rowb[0:1, :]))
        cx._tag = "rmsa"
        cx.phase([("vector", v_rmsa1), ("scalar", sc_rmsa),
                  ("vector", v_rmsa2)])

        al.free("logf_ps", "v_ps", "q_ps", "stat_ps")
        al.free("f_sb", "a_sb", "k_sb", "b_sb", "h_sb", "q_sb", "o_sb",
                "sq_sb")
        xf_t = al.sb("xf_t", [128, 2, T], F32)
        py_t = al.sb("py_t", [128, 2, T], F32)
        psum_a = al.ps("psum_a", [128, 2, T], F32)
        bc_ps = al.ps("bc_ps", [128, T], F32)

        def pe_bca(cx, t):
            cx.wait(t, "tensor", "ve", cx.cnt["ve"])
            cx.wait(t, "tensor", "sc", cx.cnt["sc"])
            for half in range(2):
                m = t.matmul(bc_ps[:, half * 512:(half + 1) * 512], ones_r[:, :],
                             rowa[0:1, half * 512:(half + 1) * 512],
                             start=True, stop=True)
            dd["bca"] = cx.inc("pe", m)
        def v_bca(cx, v):
            cx.wait(v, "vector", "pe", dd["bca"])
            dd["rmsabc"] = cx.inc("ve", v.tensor_copy(rms_a[:, :], bc_ps[:, :]))
        cx._tag = "rmsa"
        cx.phase([("tensor", pe_bca), ("vector", v_bca)])

        def wo_load(cx, s, dc):
            p = dc % 2
            if dc >= 2:
                cx.wait(s, "sync", "pe", dd.get(("atmm", dc - 2)))
            dd[("wold", dc)] = cx.inc(f"ld_wo{p}", s.dma_start(
                out=wo_t[:, p, :, :],
                in_=wo_b.ap()[dc * 128:(dc + 1) * 128, :]
                    .rearrange("p (cc d) -> p cc d", d=128)), 16)

        def s_atail(cx, s):
            cx.wait(s, "sync", "cob", 16 * DCN)
            wo_load(cx, s, 0)
            wo_load(cx, s, 1)
            cx.wait(s, "sync", "ve", dd.get("rowinv"))
            def load(dc):
                p = dc % 2
                if dc >= 2:
                    cx.wait(s, "sync", "ve", dd.get(("aev", dc - 2)))
                dd[("xfld", dc)] = cx.inc(f"ld_xf{p}", s.dma_start(
                    out=xf_t[:, p, :], in_=x_fm[dc * 128:(dc + 1) * 128, :]), 16)
            load(0)
            load(1)
            for dc in range(DCN):
                cx.wait(s, "sync", "ve", dd.get(("aev", dc)))
                cx.inc(f"st_py{dc % 2}", s.dma_start(
                    out=partial.ap()[dc * 128:(dc + 1) * 128, :],
                    in_=py_t[:, dc % 2, :]), 16)
                if dc + 2 < DCN:
                    wo_load(cx, s, dc + 2)
                    load(dc + 2)

        def pe_at(cx, t):
            for dc in range(DCN):
                p = dc % 2
                cx.wait(t, "tensor", f"ld_wo{p}", dd.get(("wold", dc)))
                if dc >= 2:
                    cx.wait(t, "tensor", "ve", dd.get(("aev", dc - 2)))
                for cc in range(CCN):
                    for half in range(2):
                        m = t.matmul(psum_a[:, p, half * 512:(half + 1) * 512],
                                     wo_t[:, p, cc, :],
                                     outw[:, cc, half * 512:(half + 1) * 512],
                                     start=(cc == 0), stop=(cc == CCN - 1))
                dd[("atmm", dc)] = cx.inc("pe", m)

        def v_at(cx, v):
            cx.wait(v, "vector", "ve", dd.get("rmsabc"))
            for dc in range(DCN):
                p = dc % 2
                cx.wait(v, "vector", "pe", dd[("atmm", dc)])
                cx.wait(v, "vector", f"ld_xf{p}", dd.get(("xfld", dc)))
                if dc >= 2:
                    cx.wait(v, "vector", f"st_py{p}", 16 * (dc // 2))
                cx.inc("ve", v.tensor_tensor(py_t[:, p, :], psum_a[:, p, :],
                                             rms_a[:, :], OP.mult))
                cx.sw(v, "vector", "ve")
                dd[("aev", dc)] = cx.inc("ve", v.tensor_tensor(
                    py_t[:, p, :], py_t[:, p, :], xf_t[:, p, :], OP.add))

        cx._tag = "atail"
        cx.phase([("tensor", pe_at), ("vector", v_at), ("sync", s_atail)])
        pt_done0 = cx.cnt["st_py0"]
        pt_done1 = cx.cnt["st_py1"]

        if stage == "partial":
            def s_dumpp(cx, s):
                cx.wait(s, "sync", "st_py0", pt_done0)
                cx.wait(s, "sync", "st_py1", pt_done1)
                cx.inc("st_y", s.dma_start(out=dbg_pt, in_=partial.ap()), 16)
                if not cx.dry:
                    s.wait_ge(cx.sems["st_y"], cx.cnt["st_y"])
            cx.phase([("sync", s_dumpp)])
            cx.close_sems()
            return nc

        cx.barrier("b2", extra={
            "sync": ["st_py0", "st_py1", "ld_xf0", "ld_xf1", "ld_wv0",
                     "ld_wv1", "ld_wo0", "ld_wo1"]})
        al.free("outw", "wv_t", "wo_t", "xf_t", "py_t",
                "psum_a", "bc_ps", "rowa", "rowb", "rms_a",
                "wq1_sb", "wk1_sb", "wq2_sb", "wk2_sb", "t1_sb", "t2_sb",
                "xn_wm")

        hn = al.sb("hn", [128, HCN, T], BF16, side="right")
        wgu_t = al.sb("wgu_t", [128, 2, 2, ECN, 128], BF16)
        g_sb = al.sb("g_sb", [128, T], F32)
        gate_ps = al.ps("gate_ps", [128, 2, T], F32)
        up_ps = al.ps("up_ps", [128, 2, T], F32)

        pairs = [(P, r, k) for P in range(4) for r in range(4) for k in range(4)]

        def s_gul(cx, s):
            for j, (P, r, k) in enumerate(pairs):
                p = j % 2
                cx.wait(s, "sync", f"ccg{P}", 1)
                if j >= 2:
                    cx.wait(s, "sync", "pe", dd.get(("gumm", j - 2)))
                cx.inc(f"ld_wgu{p}", s.dma_start(
                    out=wgu_t[:, p, 0, :, :],
                    in_=ag_gu_out[P].ap()[r * 512 + k * 128:r * 512 + (k + 1) * 128, :]
                        .rearrange("p (ec h) -> p ec h", h=128)), 16)
                dd[("guld", j)] = cx.inc(f"ld_wgu{p}", s.dma_start(
                    out=wgu_t[:, p, 1, :, :],
                    in_=ag_gu_out[P].ap()[(r + 4) * 512 + k * 128:
                                          (r + 4) * 512 + (k + 1) * 128, :]
                        .rearrange("p (ec h) -> p ec h", h=128)), 16)

        def pe_gu(cx, t):
            for j in range(len(pairs)):
                p = j % 2
                cx.wait(t, "tensor", f"ld_wgu{p}", dd[("guld", j)])
                if j >= 2:
                    cx.wait(t, "tensor", "ve", dd.get(("hn", j - 2)))
                    cx.wait(t, "tensor", "sc", dd.get(("gsilu", j - 2)))
                for ec in range(ECN):
                    for g in range(2):
                        ps = gate_ps if g == 0 else up_ps
                        for half in range(2):
                            m = t.matmul(ps[:, p, half * 512:(half + 1) * 512],
                                         wgu_t[:, p, g, ec, :],
                                         xn_mn[:, ec,
                                               half * 512:(half + 1) * 512],
                                         start=(ec == 0), stop=(ec == ECN - 1))
                dd[("gumm", j)] = cx.inc("pe", m)

        def sc_gu(cx, e):
            for j in range(len(pairs)):
                p = j % 2
                cx.wait(e, "scalar", "pe", dd[("gumm", j)])
                if j >= 1:
                    cx.wait(e, "scalar", "ve", dd.get(("hn", j - 1)))
                dd[("gsilu", j)] = cx.inc("sc", e.activation(
                    out=g_sb[:, :], in_=gate_ps[:, p, :], func=AF.Silu))

        def v_gu(cx, v):
            for j, (P, r, k) in enumerate(pairs):
                p = j % 2
                hc = r * 16 + P * 4 + k
                cx.wait(v, "vector", "sc", dd[("gsilu", j)])
                cx.wait(v, "vector", "pe", dd[("gumm", j)])
                dd[("hn", j)] = cx.inc("ve", v.tensor_tensor(
                    hn[:, hc, :], g_sb[:, :], up_ps[:, p, :], OP.mult))

        cx._tag = "gu"
        cx.phase([("sync", s_gul), ("tensor", pe_gu), ("scalar", sc_gu),
                  ("vector", v_gu)])

        cx.barrier("b3", extra={"sync": ["ld_wgu0", "ld_wgu1"]})
        al.free("wgu_t", "g_sb", "gate_ps", "up_ps", "xn_mn")

        rowa = al.sb("rowa2", [1, T], F32)
        rowb = al.sb("rowb2", [1, T], F32)
        rms_f = al.sb("rms_f", [128, T], F32)
        sqf_sb = al.sb("sqf_sb", [128, 2, T], BF16)
        statf_ps = al.ps("statf_ps", [1, T], F32)

        def v_sqf(cx, v):
            cx.sw(v, "vector", "ve")
            for hc in range(HCN):
                p = hc % 2
                if hc >= 2:
                    cx.wait(v, "vector", "pe", dd.get(("sfmm", hc - 2)))
                dd[("sqf", hc)] = cx.inc("ve", v.tensor_tensor(
                    sqf_sb[:, p, :], hn[:, hc, :], hn[:, hc, :], OP.mult))

        def pe_sqf(cx, t):
            for hc in range(HCN):
                p = hc % 2
                cx.wait(t, "tensor", "ve", dd[("sqf", hc)])
                for half in range(2):
                    m = t.matmul(statf_ps[0:1, half * 512:(half + 1) * 512],
                                 ones_c[:, :],
                                 sqf_sb[:, p, half * 512:(half + 1) * 512],
                                 start=(hc == 0), stop=(hc == HCN - 1))
                dd[("sfmm", hc)] = cx.inc("pe", m)

        def v_rmsf1(cx, v):
            cx.wait(v, "vector", "pe", cx.cnt["pe"])
            dd["rowf"] = cx.inc("ve", v.tensor_copy(rowa[0:1, :],
                                                    statf_ps[0:1, :]))
        def sc_rmsf(cx, e):
            cx.wait(e, "scalar", "ve", dd["rowf"])
            dd["rowfsd"] = cx.inc("sc", e.activation(
                out=rowb[0:1, :], in_=rowa[0:1, :], func=AF.Sqrt,
                bias=cst[0:1, 2:3], scale=cst[0:1, 3:4]))
        def v_rmsf2(cx, v):
            cx.wait(v, "vector", "sc", dd["rowfsd"])
            cx.inc("ve", v.reciprocal(rowa[0:1, :], rowb[0:1, :]))
        cx._tag = "statf"
        cx.phase([("vector", v_sqf), ("tensor", pe_sqf),
                  ("vector", v_rmsf1), ("scalar", sc_rmsf),
                  ("vector", v_rmsf2)])

        al.free("statf_ps", "sqf_sb")
        psum_f = al.ps("psum_f", [128, 2, T], F32)
        bcf_ps = al.ps("bcf_ps", [128, T], F32)

        def pe_bcf(cx, t):
            cx.wait(t, "tensor", "ve", cx.cnt["ve"])
            for half in range(2):
                m = t.matmul(bcf_ps[:, half * 512:(half + 1) * 512], ones_r[:, :],
                             rowa[0:1, half * 512:(half + 1) * 512],
                             start=True, stop=True)
            dd["bcf"] = cx.inc("pe", m)
        def v_bcf(cx, v):
            cx.wait(v, "vector", "pe", dd["bcf"])
            cx.inc("ve", v.tensor_copy(rms_f[:, :], bcf_ps[:, :]))
        cx._tag = "statf"
        cx.phase([("tensor", pe_bcf), ("vector", v_bcf)])
        rmsf_rdy = cx.cnt["ve"]

        wp_t = al.sb("wp_t", [128, 2, HCN, 128], BF16)
        pt_t = al.sb("pt_t", [128, 2, T], F32)
        yv_t = al.sb("yv_t", [128, 2, T], F32)

        def s_ftail(cx, s):
            cx.wait(s, "sync", "ccwp", 1)
            cx.wait(s, "sync", "st_py0", pt_done0)
            cx.wait(s, "sync", "st_py1", pt_done1)
            cx.wait(s, "sync", "pe", dd.get("bcf"))
            cx.wait(s, "sync", "ve", rmsf_rdy)
            def load(dc):
                p = dc % 2
                if dc >= 2:
                    cx.wait(s, "sync", "pe", dd.get(("ftmm", dc - 2)))
                rr, dcl = dc // 2, dc % 2
                dd[("wpld", dc)] = cx.inc(f"ld_wp{p}", s.dma_start(
                    out=wp_t[:, p, :, :],
                    in_=ag_wp_out.ap()[rr * 256 + dcl * 128:
                                       rr * 256 + (dcl + 1) * 128, :]
                        .rearrange("p (hc d) -> p hc d", d=128)), 16)
                if dc >= 2:
                    cx.wait(s, "sync", "ve", dd.get(("fev", dc - 2)))
                dd[("ptld", dc)] = cx.inc(f"ld_pt{p}", s.dma_start(
                    out=pt_t[:, p, :],
                    in_=partial.ap()[dc * 128:(dc + 1) * 128, :]), 16)
            load(0)
            load(1)
            for dc in range(DCN):
                cx.wait(s, "sync", "ve", dd.get(("fev", dc)))
                cx.inc(f"st_y{dc % 2}", s.dma_start(
                    out=y[dc * 128:(dc + 1) * 128, :],
                    in_=yv_t[:, dc % 2, :]), 16)
                if dc + 2 < DCN:
                    load(dc + 2)
            if not cx.dry:
                s.wait_ge(cx.sems["st_y0"], cx.cnt["st_y0"])
                s.wait_ge(cx.sems["st_y1"], cx.cnt["st_y1"])

        def pe_ft(cx, t):
            for dc in range(DCN):
                p = dc % 2
                cx.wait(t, "tensor", f"ld_wp{p}", dd.get(("wpld", dc)))
                if dc >= 2:
                    cx.wait(t, "tensor", "ve", dd.get(("fev", dc - 2)))
                for hc in range(HCN):
                    for half in range(2):
                        m = t.matmul(psum_f[:, p, half * 512:(half + 1) * 512],
                                     wp_t[:, p, hc, :],
                                     hn[:, hc, half * 512:(half + 1) * 512],
                                     start=(hc == 0), stop=(hc == HCN - 1))
                dd[("ftmm", dc)] = cx.inc("pe", m)

        def v_ft(cx, v):
            cx.sw(v, "vector", "ve")
            for dc in range(DCN):
                p = dc % 2
                cx.wait(v, "vector", "pe", dd[("ftmm", dc)])
                cx.wait(v, "vector", f"ld_pt{p}", dd.get(("ptld", dc)))
                if dc >= 2:
                    cx.wait(v, "vector", f"st_y{p}", 16 * (dc // 2))
                cx.inc("ve", v.tensor_tensor(yv_t[:, p, :], psum_f[:, p, :],
                                             rms_f[:, :], OP.mult))
                cx.sw(v, "vector", "ve")
                dd[("fev", dc)] = cx.inc("ve", v.tensor_tensor(
                    yv_t[:, p, :], yv_t[:, p, :], pt_t[:, p, :], OP.add))

        cx._tag = "ftail"
        cx.phase([("tensor", pe_ft), ("vector", v_ft),
                  ("sync", s_ftail)])
        cx.close_sems()
    return nc


def make_in_maps(inputs):
    X = np.asarray(inputs["L0_attn_x"], dtype=np.float32)
    ws = {k: np.asarray(inputs["L0_attn_" + k], dtype=np.float32)
          for k in ("Wq1", "Wq2", "Wk1", "Wk2", "Wv", "Wo", "norm_w")}
    Wgu = np.asarray(inputs["L0_ffn_w_gate_up"], dtype=np.float32)
    Wp = np.asarray(inputs["L0_ffn_w_proj"], dtype=np.float32)
    in_maps = []
    for c in range(NCORES):
        b, hf = c // 2, c % 2
        seq = X[b]
        main = seq[hf * T:(hf + 1) * T]
        warm = np.zeros((W, E), np.float32) if hf == 0 else seq[T - W:T]
        in_maps.append({
            "x_tm": np.ascontiguousarray(np.concatenate([warm, main], axis=0)),
            "x_fm": np.ascontiguousarray(main.T),
            "wq1": ws["Wq1"], "wq2": ws["Wq2"], "wk1": ws["Wk1"],
            "wk2": ws["Wk2"], "wv": ws["Wv"], "wo": ws["Wo"],
            "nrm": ws["norm_w"],
            "wgu_sl": np.ascontiguousarray(
                Wgu[c * (GU // NCORES):(c + 1) * (GU // NCORES)]),
            "wp_sl": np.ascontiguousarray(
                Wp[c * (D // NCORES):(c + 1) * (D // NCORES)]),
        })
    return in_maps


def kernel(**inputs):
    nc = build_program("full")
    in_maps = make_in_maps(inputs)
    res = run_bass_kernel_spmd(nc, in_maps, core_ids=list(range(NCORES)))
    out = np.empty((4, 2048, E), np.float32)
    for c in range(NCORES):
        b, hf = c // 2, c % 2
        out[b, hf * T:(hf + 1) * T, :] = res.results[c]["y"].T
    return out


# version 17
# speedup vs baseline: 1.1756x; 1.0234x over previous
import numpy as np
import ml_dtypes
import concourse.bass as bass
import concourse.mybir as mybir
from concourse.bass_utils import run_bass_kernel_spmd

F32 = mybir.dt.float32
BF16 = mybir.dt.bfloat16
AF = mybir.ActivationFunctionType
OP = mybir.AluOpType
AX = mybir.AxisListType

E = 2048
ECN = 16
C = 4096
CCN = 32
HD = 128
H = 8192
HCN = 64
GU = 16384
D = 2048
DCN = 16
T = 1024
W = 128
TW = T + W
NCORES = 8
TPS = [(0, W, False), (W, 512, True), (W + 512, 512, True)]

BF16_EPS8 = float(np.float32(ml_dtypes.bfloat16(1e-8)))
THR = 0.50048828125
LN_EPS = 1e-5
ATTN_EPS = 1e-6
FFN_EPS = 1.1920928955078125e-07

ENGS = ("sync", "scalar", "vector", "tensor", "gpsimd")


class _DummyInstr:
    def then_inc(self, *a, **k):
        return self


class _DryEng:
    def __getattr__(self, name):
        return lambda *a, **k: _DummyInstr()


class Ctx:

    def __init__(self, nc, block):
        self.nc = nc
        self.block = block
        self.sems = {}
        self.cnt = {}
        self.lastw = {}
        self._live = []
        self.dry = False

    def sem(self, name):
        cm = self.nc.semaphore(name)
        self._live.append(cm)
        self.sems[name] = cm.__enter__()
        self.cnt[name] = 0

    def close_sems(self):
        for cm in reversed(self._live):
            cm.__exit__(None, None, None)

    def wait(self, eng, engname, name, val):
        if val is None:
            return
        v = val
        if v <= 0:
            return
        key = (engname, name)
        if self.lastw.get(key, -1) >= v:
            return
        if not self.dry:
            eng.wait_ge(self.sems[name], v)
        self.lastw[key] = v

    def sw(self, eng, engname, name):
        self.wait(eng, engname, name, self.cnt[name])

    def inc(self, name, instr, k=1):
        if not self.dry:
            instr.then_inc(self.sems[name], k)
        self.cnt[name] += k
        return self.cnt[name]

    def phase(self, bodies):
        snap_cnt = dict(self.cnt)
        snap_w = dict(self.lastw)
        self.dry = True
        de = _DryEng()
        for name, fn in bodies:
            fn(self, de)
        dry_cnt = dict(self.cnt)
        self.cnt = snap_cnt
        self.lastw = snap_w
        self.dry = False
        tagn = getattr(self, "_tag", None)
        for name, fn in bodies:
            if tagn:
                with self.nc.named_scope(tagn):
                    getattr(self.block, name)(lambda eng, fn=fn: fn(self, eng))
            else:
                getattr(self.block, name)(lambda eng, fn=fn: fn(self, eng))
        assert self.cnt == dry_cnt, (self.cnt, dry_cnt)

    def barrier(self, tag, extra=None, engines=ENGS):
        tgt = self.cnt["ph"] + len(engines)
        extra = extra or {}
        def bfn(nm):
            def f(cx, e):
                for s in extra.get(nm, ()):
                    cx.wait(e, nm, s, cx.cnt[s])
                if not cx.dry:
                    if nm in ("scalar", "vector", "tensor"):
                        e.drain()
                    e.sem_inc(cx.sems["ph"], 1)
                    e.wait_ge(cx.sems["ph"], tgt)
                cx.lastw[(nm, "ph")] = tgt
            return f
        for nm in engines:
            getattr(self.block, nm)(lambda eng, nm=nm: bfn(nm)(self, eng))
            self.lastw[(nm, "ph")] = tgt
        self.cnt["ph"] = tgt


class Alloc:
    def __init__(self, nc):
        self.nc = nc
        self.cms = {}
        self.order = {}
        self.n = 0

    def sb(self, name, shape, dtype, side="left"):
        cm = self.nc.sbuf_tensor(name, shape, dtype, side=side)
        h = cm.__enter__()
        self.cms[name] = cm
        self.order[name] = self.n
        self.n += 1
        return h

    def ps(self, name, shape, dtype):
        cm = self.nc.psum_tensor(name, shape, dtype)
        h = cm.__enter__()
        self.cms[name] = cm
        self.order[name] = self.n
        self.n += 1
        return h

    def free(self, *names):
        for n in sorted(names, key=lambda x: -self.order[x]):
            self.cms.pop(n).__exit__(None, None, None)


def build_program(stage="full"):
    lvl = {"xn": 1, "attn": 2, "partial": 3, "full": 5}[stage]
    nc = bass.Bass("TRN2", target_bir_lowering=False, debug=False)

    x_tm = nc.dram_tensor("x_tm", [TW, E], F32, kind="ExternalInput").ap()
    x_fm = nc.dram_tensor("x_fm", [E, T], F32, kind="ExternalInput").ap()
    wq1 = nc.dram_tensor("wq1", [E, HD], F32, kind="ExternalInput").ap()
    wq2 = nc.dram_tensor("wq2", [HD, C], F32, kind="ExternalInput").ap()
    wk1 = nc.dram_tensor("wk1", [E, HD], F32, kind="ExternalInput").ap()
    wk2 = nc.dram_tensor("wk2", [HD, C], F32, kind="ExternalInput").ap()
    wv = nc.dram_tensor("wv", [E, C], F32, kind="ExternalInput").ap()
    wo = nc.dram_tensor("wo", [C, D], F32, kind="ExternalInput").ap()
    nrm = nc.dram_tensor("nrm", [C], F32, kind="ExternalInput").ap()
    wgu_sl = nc.dram_tensor("wgu_sl", [GU // NCORES, E], F32, kind="ExternalInput").ap()
    wp_sl = nc.dram_tensor("wp_sl", [D // NCORES, H], F32, kind="ExternalInput").ap()

    if lvl >= 5:
        y = nc.dram_tensor("y", [D, T], F32, kind="ExternalOutput").ap()
    if stage == "xn":
        dbg_xn = nc.dram_tensor("dbg_xn", [E, TW], BF16, kind="ExternalOutput").ap()
    if stage == "attn":
        dbg_ow = nc.dram_tensor("dbg_ow", [C, T], BF16, kind="ExternalOutput").ap()
        dbg_st = nc.dram_tensor("dbg_st", [1, T], F32, kind="ExternalOutput").ap()
    if stage == "partial":
        dbg_pt = nc.dram_tensor("dbg_pt", [D, T], F32, kind="ExternalOutput").ap()

    ag_gu_in = [nc.dram_tensor(f"ag_gu_in{i}", [512, E], BF16) for i in range(4)]
    ag_gu_out = [nc.dram_tensor(f"ag_gu_out{i}", [NCORES * 512, E], BF16,
                                addr_space="Shared")
                 for i in range(4)]
    ag_wp_in = nc.dram_tensor("ag_wp_in", [D // NCORES, H], BF16)
    ag_wp_out = nc.dram_tensor("ag_wp_out", [NCORES * (D // NCORES), H], BF16,
                               addr_space="Shared")
    partial = nc.dram_tensor("partial", [D, T], F32)
    wv_b = nc.dram_tensor("wv_b", [CCN * 128, E], BF16)
    wo_b = nc.dram_tensor("wo_b", [DCN * 128, C], BF16)

    al = Alloc(nc)
    nrm_sb = al.sb("nrm_sb", [128, CCN], F32, side="right")
    ones_c = al.sb("ones_c", [128, 1], BF16, side="right")
    ones_r = al.sb("ones_r", [1, 128], F32, side="right")
    epsc = al.sb("epsc", [128, 1], F32, side="right")
    cst = al.sb("cst", [1, 4], F32, side="right")
    xn_mn = al.sb("xn_mn", [128, ECN, T], BF16)
    xn_wm = al.sb("xn_wm", [128, ECN, W], BF16)
    wq1_sb = al.sb("wq1_sb", [128, ECN, HD], BF16)
    wk1_sb = al.sb("wk1_sb", [128, ECN, HD], BF16)
    wq2_sb = al.sb("wq2_sb", [128, C], BF16)
    wk2_sb = al.sb("wk2_sb", [128, C], BF16)
    t1_sb = al.sb("t1_sb", [128, T], BF16)
    t2_sb = al.sb("t2_sb", [128, TW], BF16)

    with nc.Block() as block:
        cx = Ctx(nc, block)
        for s in ("ph", "pe", "sc", "ve", "ldsm", "st_y", "cvb0", "cvb1", "cvb2", "cvb3", "cob",
                  "st_ag0", "st_ag1", "st_wp0", "st_wp1", "st_y0", "st_y1",
                  "ccg0", "ccg1", "ccg2", "ccg3", "ccwp",
                  "trs_q0", "trs_q1", "trs_w0", "trs_w1"):
            cx.sem(s)
        for s in ("ld_x", "cast", "cwp", "trs_ln", "ld_wv", "ld_wo",
                  "ld_wgu", "ld_wp", "ld_xf", "ld_pt", "st_py"):
            cx.sem(s + "0")
            cx.sem(s + "1")

        NGQ, NPQ = 8, 4
        wgu_buf = al.sb("wgu_buf", [128, 2, 2, E], BF16)
        wp_buf = al.sb("wp_buf", [128, 2, 4096], BF16)
        pos_sb = al.sb("pos_sb", [128, 4096], BF16)
        neg_sb = al.sb("neg_sb", [128, 4096], BF16)
        red_sb = al.sb("red_sb", [128, 64], F32)
        scl_sb = al.sb("scl_sb", [128, 64], BF16)
        thr_sb = al.sb("thr_sb", [128, 64], F32)
        nthr_sb = al.sb("nthr_sb", [128, 64], F32)
        tt_sb = al.sb("tt_sb", [128, 2, 16, 128], BF16)
        tt_wp = al.sb("tt_wp", [128, 2, 32, 128], BF16)
        xt_sb = al.sb("xt_sb", [128, 2, E], F32)
        xnt_sb = al.sb("xnt_sb", [128, 2, E], BF16)
        sum_sb = al.sb("sum_sb", [128, 2, 1], F32)
        ssq_sb = al.sb("ssq_sb", [128, 2, 1], F32)
        mean_sb = al.sb("mean_sb", [128, 2, 1], F32)
        var_sb = al.sb("var_sb", [128, 2, 1], F32)
        sd_sb = al.sb("sd_sb", [128, 2, 1], F32)
        inv_sb = al.sb("inv_sb", [128, 2, 1], F32)

        def xn_sl(ec, off, sz):
            if off < W:
                assert off + sz <= W
                return xn_wm[:, ec, off:off + sz]
            return xn_mn[:, ec, off - W:off - W + sz]

        dd = {}
        NTS = TW // 128

        def quant_ops(cx, v, wbuf, G):
            view = wbuf.rearrange("p (g k) -> p g k", k=64)
            rd = red_sb[:, 0:G]
            sc_ = scl_sb[:, 0:G]
            th = thr_sb[:, 0:G]
            nth = nthr_sb[:, 0:G]
            pv = pos_sb[:, 0:G * 64].rearrange("p (g k) -> p g k", k=64)
            nv = neg_sb[:, 0:G * 64].rearrange("p (g k) -> p g k", k=64)
            cx.inc("ve", v.tensor_reduce(rd, view, AX.X, OP.add,
                                          apply_absolute_value=True))
            cx.sw(v, "vector", "ve")
            cx.inc("ve", v.tensor_scalar(sc_, rd, 1.0 / 64.0, None, OP.mult))
            cx.sw(v, "vector", "ve")
            cx.inc("ve", v.tensor_scalar_max(sc_, sc_, BF16_EPS8))
            cx.sw(v, "vector", "ve")
            v.tensor_scalar(th, sc_, THR, None, OP.mult)
            cx.inc("ve", v.tensor_scalar(nth, sc_, -THR, None, OP.mult))
            cx.sw(v, "vector", "ve")
            v.tensor_tensor(pv, view, th.unsqueeze(2).broadcast_to([128, G, 64]),
                            OP.is_gt)
            cx.inc("ve", v.tensor_tensor(
                nv, view, nth.unsqueeze(2).broadcast_to([128, G, 64]), OP.is_lt))
            cx.sw(v, "vector", "ve")
            cx.inc("ve", v.tensor_tensor(pv, pv, nv, OP.subtract))
            cx.sw(v, "vector", "ve")
            return v.tensor_tensor(view, pv,
                                   sc_.unsqueeze(2).broadcast_to([128, G, 64]),
                                   OP.mult)

        def g_prep(cx, g):
            i = cx.inc
            i("ldsm", g.dma_start(out=wq1_sb[:, :, :],
                                  in_=wq1.rearrange("(a p) m -> p a m", p=128)), 16)
            i("ldsm", g.dma_start(out=wk1_sb[:, :, :],
                                  in_=wk1.rearrange("(a p) m -> p a m", p=128)), 16)
            i("ldsm", g.dma_start(out=wq2_sb[:, :], in_=wq2), 16)
            i("ldsm", g.dma_start(out=wk2_sb[:, :], in_=wk2), 16)
            i("ldsm", g.dma_start(out=nrm_sb[:, :],
                                  in_=nrm.rearrange("(a p) -> p a", p=128),
                                  allow_slow_non_contiguous=True), 16)
            dd["ldsm"] = cx.cnt["ldsm"]
            for q in range(NGQ):
                p = q % 2
                if q >= 2:
                    cx.wait(g, "gpsimd", "trs_q0", dd.get(("gu_trs0", q - 2)))
                    cx.wait(g, "gpsimd", "trs_q1", dd.get(("gu_trs1", q - 2)))
                i(f"cast{p}", g.dma_start(
                    out=wgu_buf[:, p, :, :],
                    in_=wgu_sl[q * 256:(q + 1) * 256, :]
                        .rearrange("(hc p) d -> p hc d", p=128)), 16)
            for q in range(4):
                p = q % 2
                if q >= 2:
                    cx.wait(g, "gpsimd", "trs_w0", dd.get(("wp_trs0", q - 2)))
                    cx.wait(g, "gpsimd", "trs_w1", dd.get(("wp_trs1", q - 2)))
                dcl, half = q // 2, q % 2
                i(f"cwp{p}", g.dma_start(
                    out=wp_buf[:, p, :],
                    in_=wp_sl[dcl * 128:(dcl + 1) * 128,
                              half * 4096:(half + 1) * 4096]), 16)


        def v_memset(cx, v):
            if not cx.dry:
                v.memset(ones_c[:, :], 1.0)
                v.memset(ones_r[:, :], 1.0)
                v.memset(epsc[:, :], LN_EPS)
                v.memset(cst[0:1, 0:1], ATTN_EPS)
                v.memset(cst[0:1, 1:2], 1.0 / C)
                v.memset(cst[0:1, 2:3], FFN_EPS)
                v.memset(cst[0:1, 3:4], 1.0 / H)

        def v_prep(cx, v):
            for q in range(NGQ):
                p = q % 2
                cx.wait(v, "vector", f"cast{p}", 16 * (q // 2 + 1))
                dd[("gu_q", q)] = cx.inc(
                    "ve", quant_ops(cx, v, wgu_buf[:, p, :, :]
                                    .rearrange("p a d -> p (a d)"), 64))
            for q in range(4):
                p = q % 2
                cx.wait(v, "vector", f"cwp{p}", 16 * (q // 2 + 1))
                dd[("wp_q", q)] = cx.inc("ve", quant_ops(cx, v, wp_buf[:, p, :], 64))

        def s_prep(cx, s):
            for q in range(NGQ):
                p = q % 2
                cx.wait(s, "sync", "ve", dd[("gu_q", q)])
                for i2 in range(2):
                    hcl = q * 2 + i2
                    P, col = hcl // 4, hcl % 4
                    pt = hcl % 2
                    if hcl >= 2:
                        cx.wait(s, "sync", f"st_ag{pt}", dd.get(("gu_st", hcl - 2)))
                    cx.inc(f"trs_q{pt}", s.dma_start_transpose(
                        tt_sb[:, pt, :, :],
                        wgu_buf[:, p, i2, :].rearrange("p (dc r) -> p dc r", r=128)),
                        16)
                    dd[("gu_trs" + str(pt), q)] = cx.cnt[f"trs_q{pt}"]
                    cx.wait(s, "sync", f"trs_q{pt}", cx.cnt[f"trs_q{pt}"])
                    dd[("gu_st", hcl)] = cx.inc(f"st_ag{pt}", s.dma_start(
                        out=ag_gu_in[P].ap()[col * 128:(col + 1) * 128, :]
                            .rearrange("p (ec h) -> p ec h", h=128),
                        in_=tt_sb[:, pt, :, :]), 16)
                    if hcl % 4 == 3:
                        dd[("gu_ag0", P)] = cx.cnt["st_ag0"]
                        dd[("gu_ag1", P)] = cx.cnt["st_ag1"]
            for q in range(4):
                p = q % 2
                dcl, half = q // 2, q % 2
                pt = q % 2
                cx.wait(s, "sync", "ve", dd[("wp_q", q)])
                if q >= 2:
                    cx.wait(s, "sync", f"st_wp{pt}", dd.get(("wp_st", q - 2)))
                cx.inc(f"trs_w{pt}", s.dma_start_transpose(
                    tt_wp[:, pt, :, :],
                    wp_buf[:, p, :].rearrange("p (hc r) -> p hc r", r=128)), 16)
                dd[("wp_trs" + str(pt), q)] = cx.cnt[f"trs_w{pt}"]
                cx.wait(s, "sync", f"trs_w{pt}", cx.cnt[f"trs_w{pt}"])
                dd[("wp_st", q)] = cx.inc(f"st_wp{pt}", s.dma_start(
                    out=ag_wp_in.ap()[dcl * 128:(dcl + 1) * 128,
                                      half * 4096:(half + 1) * 4096]
                        .rearrange("p (hc d) -> p hc d", d=128),
                    in_=tt_wp[:, pt, :, :]), 16)
                dd[("wp_ag0", q)] = cx.cnt["st_wp0"]
                dd[("wp_ag1", q)] = cx.cnt["st_wp1"]

        def sc_ln(cx, e):
            def process(ts):
                cx.wait(e, "scalar", "ve", dd.get(("ln_var", ts)))
                dd[("ln_sd", ts)] = cx.inc("sc", e.activation(
                    out=sd_sb[:, ts % 2, :], in_=var_sb[:, ts % 2, :],
                    func=AF.Sqrt, bias=epsc[:, 0:1]))
            for ts in range(NTS):
                p = ts % 2
                if ts >= 2:
                    cx.wait(e, "scalar", "ve", dd.get(("ln_xn", ts - 2)))
                dd[("ln_x", ts)] = cx.inc(f"ld_x{p}", e.dma_start(
                    out=xt_sb[:, p, :], in_=x_tm[ts * 128:(ts + 1) * 128, :]), 16)
                if ts >= 1:
                    process(ts - 1)
            process(NTS - 1)

        def v_ln(cx, v):
            for ts in range(NTS):
                p = ts % 2
                cx.wait(v, "vector", f"ld_x{p}", dd[("ln_x", ts)])
                if ts >= 2:
                    cx.wait(v, "vector", f"trs_ln{p}", dd.get(("ln_tr", ts - 2)))
                cx.inc("ve", v.scalar_tensor_tensor(
                    xnt_sb[:, p, :], xt_sb[:, p, :], 1.0, xt_sb[:, p, :],
                    OP.mult, OP.mult, accum_out=ssq_sb[:, p, :]))
                cx.inc("ve", v.reduce_sum(sum_sb[:, p, :], xt_sb[:, p, :], AX.X))
                cx.sw(v, "vector", "ve")
                cx.inc("ve", v.tensor_scalar(mean_sb[:, p, :], sum_sb[:, p, :],
                                             1.0 / E, None, OP.mult))
                v.tensor_scalar(var_sb[:, p, :], ssq_sb[:, p, :], 1.0 / E,
                                None, OP.mult)
                cx.sw(v, "vector", "ve")
                cx.inc("ve", v.tensor_tensor(inv_sb[:, p, :], mean_sb[:, p, :],
                                             mean_sb[:, p, :], OP.mult))
                cx.sw(v, "vector", "ve")
                dd[("ln_var", ts)] = cx.inc("ve", v.tensor_tensor(
                    var_sb[:, p, :], var_sb[:, p, :], inv_sb[:, p, :],
                    OP.subtract))
                cx.wait(v, "vector", "sc", dd[("ln_sd", ts)])
                cx.inc("ve", v.reciprocal(inv_sb[:, p, :], sd_sb[:, p, :]))
                cx.sw(v, "vector", "ve")
                dd[("ln_xn", ts)] = cx.inc("ve", v.tensor_scalar(
                    xnt_sb[:, p, :], xt_sb[:, p, :], mean_sb[:, p, :],
                    inv_sb[:, p, :], OP.subtract, OP.mult))

        def s_lntr(cx, s):
            for ts in range(NTS):
                p = ts % 2
                cx.wait(s, "sync", "ve", dd[("ln_xn", ts)])
                nw = W // 128
                if ts < nw:
                    dst = xn_wm[:, :, ts * 128:(ts + 1) * 128]
                else:
                    dst = xn_mn[:, :, (ts - nw) * 128:(ts - nw + 1) * 128]
                cx.inc(f"trs_ln{p}", s.dma_start_transpose(
                    dst,
                    xnt_sb[:, p, :].rearrange("p (ec r) -> p ec r", r=128)), 16)
                dd[("ln_tr", ts)] = cx.cnt[f"trs_ln{p}"]

        cx._tag = "p0ln"
        cx.phase([("vector", v_memset), ("scalar", sc_ln), ("vector", v_ln),
                  ("sync", s_lntr), ("vector", v_prep), ("sync", s_prep),
                  ("gpsimd", g_prep)])

        xn_rdy0 = cx.cnt["trs_ln0"]
        xn_rdy1 = cx.cnt["trs_ln1"]

        if stage == "xn":
            def s_dump(cx, s):
                cx.wait(s, "sync", "trs_ln0", xn_rdy0)
                cx.wait(s, "sync", "trs_ln1", xn_rdy1)
                for ec in range(ECN):
                    cx.inc("st_y", s.dma_start(
                        out=dbg_xn[ec * 128:(ec + 1) * 128, 0:W],
                        in_=xn_wm[:, ec, :]), 16)
                    cx.inc("st_y", s.dma_start(
                        out=dbg_xn[ec * 128:(ec + 1) * 128, W:TW],
                        in_=xn_mn[:, ec, :]), 16)
                if not cx.dry:
                    s.wait_ge(cx.sems["st_y"], cx.cnt["st_y"])
            cx.phase([("sync", s_dump)])
            cx.close_sems()
            return nc

        cx.barrier("b1", extra={
            "sync": ["trs_ln0", "trs_ln1", "st_ag0", "st_ag1", "st_wp0",
                     "st_wp1", "trs_q0", "trs_q1", "trs_w0", "trs_w1"],
            "gpsimd": ["ldsm", "cast0", "cast1", "cwp0", "cwp1"],
            "scalar": ["ld_x0", "ld_x1"]})
        al.free("wgu_buf", "wp_buf", "pos_sb", "neg_sb", "red_sb", "scl_sb",
                "thr_sb", "nthr_sb", "tt_sb", "tt_wp", "xt_sb", "xnt_sb",
                "sum_sb", "ssq_sb", "mean_sb", "var_sb", "sd_sb", "inv_sb")

        outw = al.sb("outw", [128, CCN, T], BF16)
        wv_t = al.sb("wv_t", [128, 2, ECN, 128], BF16)
        wo_t = al.sb("wo_t", [128, 2, CCN, 128], BF16)
        rowa = al.sb("rowa", [1, T], F32)
        rowb = al.sb("rowb", [1, T], F32)
        rms_a = al.sb("rms_a", [128, T], F32)
        f_sb = al.sb("f_sb", [128, 2, 512], F32)
        a_sb = al.sb("a_sb", [128, 2, 512], F32)
        k_sb = al.sb("k_sb", [128, 2, 512], F32)
        b_sb = al.sb("b_sb", [128, 2, 512], F32)
        h_sb = al.sb("h_sb", [128, 2, 512], F32)
        q_sb = al.sb("q_sb", [128, 2, 512], F32)
        o_sb = al.sb("o_sb", [128, 2, 512], F32)
        sq_sb = al.sb("sq_sb", [128, 2, 512], BF16)

        logf_ps = al.ps("logf_ps", [128, 2, 512], F32)
        v_ps = al.ps("v_ps", [128, 2, 512], F32)
        q_ps = al.ps("q_ps", [128, 2, 512], F32)
        stat_ps = al.ps("stat_ps", [1, T], F32)

        steps = [(cc, tpi) for cc in range(CCN) for tpi in range(3)]
        qsteps = set(s for s in steps if TPS[s[1]][2])

        def pe_t12(cx, t):
            cx.wait(t, "tensor", "ldsm", dd["ldsm"])
            cx.wait(t, "tensor", "trs_ln0", xn_rdy0)
            cx.wait(t, "tensor", "trs_ln1", xn_rdy1)
            gi = 0
            for (name, wsb, pieces) in (
                    ("t2", wk1_sb, TPS),
                    ("t1", wq1_sb, [TPS[1], TPS[2]])):
                for pi, (off, sz, _) in enumerate(pieces):
                    pg = gi % 2
                    if gi >= 2:
                        cx.wait(t, "tensor", "ve", dd.get(("t12ev", gi - 2)))
                    ps = logf_ps[:, pg, 0:sz]
                    for ec in range(ECN):
                        m = t.matmul(ps, wsb[:, ec, :], xn_sl(ec, off, sz),
                                     start=(ec == 0), stop=(ec == ECN - 1))
                    dd[("t12mm", gi)] = cx.inc("pe", m)
                    gi += 1

        def v_t12(cx, v):
            gi = 0
            for (name, dst, pieces) in (
                    ("t2", t2_sb, TPS),
                    ("t1", t1_sb, [TPS[1], TPS[2]])):
                for pi, (off, sz, _) in enumerate(pieces):
                    pg = gi % 2
                    cx.wait(v, "vector", "pe", dd[("t12mm", gi)])
                    doff = off if name == "t2" else off - W
                    dd[("t12ev", gi)] = cx.inc("ve", v.tensor_copy(
                        dst[:, doff:doff + sz], logf_ps[:, pg, 0:sz]))
                    dd[(name + "ev", pi)] = dd[("t12ev", gi)]
                    gi += 1

        def g_post(cx, g):
            i = cx.inc
            def ag(P):
                if P < NPQ:
                    cx.wait(g, "gpsimd", "st_ag0", dd[("gu_ag0", P)])
                    cx.wait(g, "gpsimd", "st_ag1", dd[("gu_ag1", P)])
                    i(f"ccg{P}", g.collective_compute(
                        "AllGather", OP.bypass,
                        replica_groups=[list(range(NCORES))],
                        ins=[ag_gu_in[P].ap().opt()],
                        outs=[ag_gu_out[P].ap().opt()]), 1)
                else:
                    cx.wait(g, "gpsimd", "st_wp0", dd[("wp_ag0", 3)])
                    cx.wait(g, "gpsimd", "st_wp1", dd[("wp_ag1", 3)])
                    i("ccwp", g.collective_compute(
                        "AllGather", OP.bypass,
                        replica_groups=[list(range(NCORES))],
                        ins=[ag_wp_in.ap().opt()],
                        outs=[ag_wp_out.ap().opt()]), 1)
            for cc in range(CCN):
                i(f"cvb{cc // 8}", g.dma_start(
                    out=wv_b.ap()[cc * 128:(cc + 1) * 128, :]
                        .rearrange("p (ec c) -> p ec c", c=128),
                    in_=wv[:, cc * 128:(cc + 1) * 128]
                        .rearrange("(ec p) c -> p ec c", p=128)), 16)
            for dc in range(DCN):
                i("cob", g.dma_start(
                    out=wo_b.ap()[dc * 128:(dc + 1) * 128, :]
                        .rearrange("p (cc d) -> p cc d", d=128),
                    in_=wo[:, dc * 128:(dc + 1) * 128]
                        .rearrange("(cc p) d -> p cc d", p=128)), 16)
            for P in range(NPQ + 1):
                ag(P)

        def s_wv(cx, s):
            for cc in range(CCN):
                p = cc % 2
                cx.wait(s, "sync", f"cvb{cc // 8}", 128)
                if cc >= 2:
                    cx.wait(s, "sync", "pe", dd.get(("vmm", cc - 2, 2)))
                dd[("wvld", cc)] = cx.inc(f"ld_wv{p}", s.dma_start(
                    out=wv_t[:, p, :, :],
                    in_=wv_b.ap()[cc * 128:(cc + 1) * 128, :]
                        .rearrange("p (ec c) -> p ec c", c=128)), 16)

        def pe_cc(cx, t):
            last_par = {}
            cx.wait(t, "tensor", "ve", dd.get(("t12ev", 4)))
            si = 0
            pend = []
            for cc in range(CCN):
                pwv = cc % 2
                cx.wait(t, "tensor", f"ld_wv{pwv}", dd[("wvld", cc)])
                for tpi, (off, sz, hasq) in enumerate(TPS):
                    s = (cc, tpi)
                    p = si % 2
                    if si >= 2:
                        cx.wait(t, "tensor", "sc", dd.get(("f", steps[si - 2])))
                    cx.wait(t, "tensor", "ve", dd[("t2ev", tpi)])
                    m = t.matmul(logf_ps[:, p, 0:sz],
                                 wk2_sb[:, cc * 128:(cc + 1) * 128],
                                 t2_sb[:, off:off + sz], start=True, stop=True)
                    dd[("logf", s)] = cx.inc("pe", m)
                    if si >= 2:
                        cx.wait(t, "tensor", "ve", dd.get(("b", steps[si - 2])))
                    for ec in range(ECN):
                        m = t.matmul(v_ps[:, p, 0:sz], wv_t[:, pwv, ec, :],
                                     xn_sl(ec, off, sz),
                                     start=(ec == 0), stop=(ec == ECN - 1))
                    dd[("vmm", cc, tpi)] = cx.inc("pe", m)
                    if hasq:
                        prevq = last_par.get(("qps", p))
                        if prevq is not None:
                            cx.wait(t, "tensor", "ve", dd.get(("o", prevq)))
                        last_par[("qps", p)] = s
                        cx.wait(t, "tensor", "ve", dd[("t1ev", tpi - 1)])
                        m = t.matmul(q_ps[:, p, 0:sz],
                                     wq2_sb[:, cc * 128:(cc + 1) * 128],
                                     t1_sb[:, off - W:off - W + sz],
                                     start=True, stop=True)
                        dd[("qmm", s)] = cx.inc("pe", m)
                        pend.append((s, p))
                    if len(pend) > 2:
                        stat_emit(cx, t, *pend.pop(0))
                    si += 1
            for it in pend:
                stat_emit(cx, t, *it)

        def stat_emit(cx, t, s, p):
            cc, tpi = s
            off = TPS[tpi][0] - W
            cx.wait(t, "tensor", "ve", dd.get(("sq", s)))
            m = t.matmul(stat_ps[0:1, off:off + 512], ones_c[:, :],
                         sq_sb[:, p, :], start=(cc == 0), stop=(cc == CCN - 1))
            dd[("stat", s)] = cx.inc("pe", m)

        def sc_cc(cx, e):
            last_par = {}
            si = 0
            pend = []
            for cc in range(CCN):
                for tpi, (off, sz, hasq) in enumerate(TPS):
                    s = (cc, tpi)
                    p = si % 2
                    cx.wait(e, "scalar", "pe", dd[("logf", s)])
                    if si >= 2 and steps[si - 2] in qsteps:
                        cx.wait(e, "scalar", "ve", dd.get(("o", steps[si - 2])))
                    dd[("f", s)] = cx.inc("sc", e.activation(
                        out=f_sb[:, p, 0:sz], in_=logf_ps[:, p, 0:sz],
                        func=AF.Sigmoid))
                    cx.sw(e, "scalar", "sc")
                    if si >= 2:
                        cx.wait(e, "scalar", "ve", dd.get(("scan", steps[si - 2])))
                    dd[("a", s)] = cx.inc("sc", e.activation(
                        out=a_sb[:, p, 0:sz], in_=f_sb[:, p, 0:sz],
                        func=AF.Sigmoid))
                    if hasq:
                        cx.wait(e, "scalar", "pe", dd[("qmm", s)])
                        prevq = last_par.get(("qsb", p))
                        if prevq is not None:
                            cx.wait(e, "scalar", "ve", dd.get(("o1", prevq)))
                        last_par[("qsb", p)] = s
                        dd[("sig", s)] = cx.inc("sc", e.activation(
                            out=q_sb[:, p, 0:sz], in_=q_ps[:, p, 0:sz],
                            func=AF.Sigmoid))
                    si += 1

        def v_cc(cx, v):
            last_par = {}
            si = 0
            for cc in range(CCN):
                for tpi, (off, sz, hasq) in enumerate(TPS):
                    s = (cc, tpi)
                    p = si % 2
                    cx.wait(v, "vector", "sc", dd[("a", s)])
                    cx.inc("ve", v.tensor_scalar(k_sb[:, p, 0:sz],
                                                 a_sb[:, p, 0:sz],
                                                 -1.0, 1.0, OP.mult, OP.add))
                    cx.sw(v, "vector", "ve")
                    cx.wait(v, "vector", "pe", dd[("vmm", cc, tpi)])
                    dd[("b", s)] = cx.inc("ve", v.tensor_tensor(
                        b_sb[:, p, 0:sz], k_sb[:, p, 0:sz], v_ps[:, p, 0:sz],
                        OP.mult))
                    cx.sw(v, "vector", "ve")
                    if tpi == 0:
                        init = 0.0
                    else:
                        psz = TPS[tpi - 1][1]
                        init = h_sb[:, 1 - p, psz - 1:psz]
                    dd[("scan", s)] = cx.inc("ve", v.tensor_tensor_scan(
                        h_sb[:, p, 0:sz], a_sb[:, p, 0:sz], b_sb[:, p, 0:sz],
                        init, OP.mult, OP.add))
                    cx.sw(v, "vector", "ve")
                    if hasq:
                        cx.wait(v, "vector", "sc", dd[("sig", s)])
                        dd[("o1", s)] = cx.inc("ve", v.tensor_tensor(
                            f_sb[:, p, 0:sz], q_sb[:, p, 0:sz],
                            h_sb[:, p, 0:sz], OP.mult))
                        cx.sw(v, "vector", "ve")
                        dd[("o", s)] = cx.inc("ve", v.tensor_tensor(
                            o_sb[:, p, 0:sz], f_sb[:, p, 0:sz],
                            q_ps[:, p, 0:sz], OP.mult))
                        cx.sw(v, "vector", "ve")
                        prevq = last_par.get(("sqb", p))
                        if prevq is not None:
                            cx.wait(v, "vector", "pe", dd.get(("stat", prevq)))
                        last_par[("sqb", p)] = s
                        dd[("sq", s)] = cx.inc("ve", v.tensor_tensor(
                            sq_sb[:, p, :], o_sb[:, p, 0:sz], o_sb[:, p, 0:sz],
                            OP.mult))
                        dd[("outw", s)] = cx.inc("ve", v.tensor_scalar(
                            outw[:, cc, off - W:off - W + sz], o_sb[:, p, 0:sz],
                            nrm_sb[:, cc:cc + 1], None, OP.mult))
                    si += 1

        cx._tag = "attn"
        cx.phase([("gpsimd", g_post),
                  ("tensor", pe_t12), ("vector", v_t12), ("sync", s_wv),
                  ("tensor", pe_cc), ("scalar", sc_cc), ("vector", v_cc)])

        if stage == "attn":
            def v_dump(cx, v):
                cx.wait(v, "vector", "pe", cx.cnt["pe"])
                dd["strow"] = cx.inc("ve", v.tensor_copy(rowa[0:1, :],
                                                         stat_ps[0:1, :]))
            def s_dumpa(cx, s):
                cx.wait(s, "sync", "ve", cx.cnt["ve"])
                for cc in range(CCN):
                    cx.inc("st_y", s.dma_start(
                        out=dbg_ow[cc * 128:(cc + 1) * 128, :],
                        in_=outw[:, cc, :]), 16)
                cx.inc("st_y", s.dma_start(out=dbg_st, in_=rowa[0:1, :]), 16)
                if not cx.dry:
                    s.wait_ge(cx.sems["st_y"], cx.cnt["st_y"])
            cx.phase([("vector", v_dump), ("sync", s_dumpa)])
            cx.close_sems()
            return nc

        def v_rmsa1(cx, v):
            cx.wait(v, "vector", "pe", cx.cnt["pe"])
            dd["rowa"] = cx.inc("ve", v.tensor_copy(rowa[0:1, :],
                                                    stat_ps[0:1, :]))
        def sc_rmsa(cx, e):
            cx.wait(e, "scalar", "ve", dd["rowa"])
            dd["rowsd"] = cx.inc("sc", e.activation(
                out=rowb[0:1, :], in_=rowa[0:1, :], func=AF.Sqrt,
                bias=cst[0:1, 0:1], scale=cst[0:1, 1:2]))
        def v_rmsa2(cx, v):
            cx.wait(v, "vector", "sc", dd["rowsd"])
            dd["rowinv"] = cx.inc("ve", v.reciprocal(rowa[0:1, :], rowb[0:1, :]))
        cx._tag = "rmsa"
        cx.phase([("vector", v_rmsa1), ("scalar", sc_rmsa),
                  ("vector", v_rmsa2)])

        al.free("logf_ps", "v_ps", "q_ps", "stat_ps")
        al.free("f_sb", "a_sb", "k_sb", "b_sb", "h_sb", "q_sb", "o_sb",
                "sq_sb")
        xf_t = al.sb("xf_t", [128, 2, T], F32)
        py_t = al.sb("py_t", [128, 2, T], F32)
        psum_a = al.ps("psum_a", [128, 2, T], F32)
        bc_ps = al.ps("bc_ps", [128, T], F32)

        def pe_bca(cx, t):
            cx.wait(t, "tensor", "ve", cx.cnt["ve"])
            cx.wait(t, "tensor", "sc", cx.cnt["sc"])
            for half in range(2):
                m = t.matmul(bc_ps[:, half * 512:(half + 1) * 512], ones_r[:, :],
                             rowa[0:1, half * 512:(half + 1) * 512],
                             start=True, stop=True)
            dd["bca"] = cx.inc("pe", m)
        def v_bca(cx, v):
            cx.wait(v, "vector", "pe", dd["bca"])
            dd["rmsabc"] = cx.inc("ve", v.tensor_copy(rms_a[:, :], bc_ps[:, :]))
        cx._tag = "rmsa"
        cx.phase([("tensor", pe_bca), ("vector", v_bca)])

        def wo_load(cx, s, dc):
            p = dc % 2
            if dc >= 2:
                cx.wait(s, "sync", "pe", dd.get(("atmm", dc - 2)))
            dd[("wold", dc)] = cx.inc(f"ld_wo{p}", s.dma_start(
                out=wo_t[:, p, :, :],
                in_=wo_b.ap()[dc * 128:(dc + 1) * 128, :]
                    .rearrange("p (cc d) -> p cc d", d=128)), 16)

        def s_atail(cx, s):
            cx.wait(s, "sync", "cob", 16 * DCN)
            wo_load(cx, s, 0)
            wo_load(cx, s, 1)
            cx.wait(s, "sync", "ve", dd.get("rowinv"))
            def load(dc):
                p = dc % 2
                if dc >= 2:
                    cx.wait(s, "sync", "ve", dd.get(("aev", dc - 2)))
                dd[("xfld", dc)] = cx.inc(f"ld_xf{p}", s.dma_start(
                    out=xf_t[:, p, :], in_=x_fm[dc * 128:(dc + 1) * 128, :]), 16)
            load(0)
            load(1)
            for dc in range(DCN):
                cx.wait(s, "sync", "ve", dd.get(("aev", dc)))
                cx.inc(f"st_py{dc % 2}", s.dma_start(
                    out=partial.ap()[dc * 128:(dc + 1) * 128, :],
                    in_=py_t[:, dc % 2, :]), 16)
                if dc + 2 < DCN:
                    wo_load(cx, s, dc + 2)
                    load(dc + 2)

        def pe_at(cx, t):
            for dc in range(DCN):
                p = dc % 2
                cx.wait(t, "tensor", f"ld_wo{p}", dd.get(("wold", dc)))
                if dc >= 2:
                    cx.wait(t, "tensor", "ve", dd.get(("aev", dc - 2)))
                for cc in range(CCN):
                    for half in range(2):
                        m = t.matmul(psum_a[:, p, half * 512:(half + 1) * 512],
                                     wo_t[:, p, cc, :],
                                     outw[:, cc, half * 512:(half + 1) * 512],
                                     start=(cc == 0), stop=(cc == CCN - 1))
                dd[("atmm", dc)] = cx.inc("pe", m)

        def v_at(cx, v):
            cx.wait(v, "vector", "ve", dd.get("rmsabc"))
            for dc in range(DCN):
                p = dc % 2
                cx.wait(v, "vector", "pe", dd[("atmm", dc)])
                cx.wait(v, "vector", f"ld_xf{p}", dd.get(("xfld", dc)))
                if dc >= 2:
                    cx.wait(v, "vector", f"st_py{p}", 16 * (dc // 2))
                cx.inc("ve", v.tensor_tensor(py_t[:, p, :], psum_a[:, p, :],
                                             rms_a[:, :], OP.mult))
                cx.sw(v, "vector", "ve")
                dd[("aev", dc)] = cx.inc("ve", v.tensor_tensor(
                    py_t[:, p, :], py_t[:, p, :], xf_t[:, p, :], OP.add))

        cx._tag = "atail"
        cx.phase([("tensor", pe_at), ("vector", v_at), ("sync", s_atail)])
        pt_done0 = cx.cnt["st_py0"]
        pt_done1 = cx.cnt["st_py1"]

        if stage == "partial":
            def s_dumpp(cx, s):
                cx.wait(s, "sync", "st_py0", pt_done0)
                cx.wait(s, "sync", "st_py1", pt_done1)
                cx.inc("st_y", s.dma_start(out=dbg_pt, in_=partial.ap()), 16)
                if not cx.dry:
                    s.wait_ge(cx.sems["st_y"], cx.cnt["st_y"])
            cx.phase([("sync", s_dumpp)])
            cx.close_sems()
            return nc

        cx.barrier("b2", extra={
            "sync": ["st_py0", "st_py1", "ld_xf0", "ld_xf1", "ld_wv0",
                     "ld_wv1", "ld_wo0", "ld_wo1"]})
        al.free("outw", "wv_t", "wo_t", "xf_t", "py_t",
                "psum_a", "bc_ps", "rowa", "rowb", "rms_a",
                "wq1_sb", "wk1_sb", "wq2_sb", "wk2_sb", "t1_sb", "t2_sb",
                "xn_wm")

        hn = al.sb("hn", [128, HCN, T], BF16, side="right")
        sqtmp = al.sb("sqtmp", [128, T], F32, side="right")
        sqacc = al.sb("sqacc", [128, T], F32, side="right")
        wgu_t = al.sb("wgu_t", [128, 2, 2, ECN, 128], BF16)
        g_sb = al.sb("g_sb", [128, T], F32)
        gate_ps = al.ps("gate_ps", [128, 2, T], F32)
        up_ps = al.ps("up_ps", [128, 2, T], F32)

        pairs = [(P, r, k) for P in range(4) for r in range(4) for k in range(4)]

        def s_gul(cx, s):
            for j, (P, r, k) in enumerate(pairs):
                p = j % 2
                cx.wait(s, "sync", f"ccg{P}", 1)
                if j >= 2:
                    cx.wait(s, "sync", "pe", dd.get(("gumm", j - 2)))
                cx.inc(f"ld_wgu{p}", s.dma_start(
                    out=wgu_t[:, p, 0, :, :],
                    in_=ag_gu_out[P].ap()[r * 512 + k * 128:r * 512 + (k + 1) * 128, :]
                        .rearrange("p (ec h) -> p ec h", h=128)), 16)
                dd[("guld", j)] = cx.inc(f"ld_wgu{p}", s.dma_start(
                    out=wgu_t[:, p, 1, :, :],
                    in_=ag_gu_out[P].ap()[(r + 4) * 512 + k * 128:
                                          (r + 4) * 512 + (k + 1) * 128, :]
                        .rearrange("p (ec h) -> p ec h", h=128)), 16)

        def pe_gu(cx, t):
            for j in range(len(pairs)):
                p = j % 2
                cx.wait(t, "tensor", f"ld_wgu{p}", dd[("guld", j)])
                if j >= 2:
                    cx.wait(t, "tensor", "ve", dd.get(("hn", j - 2)))
                    cx.wait(t, "tensor", "sc", dd.get(("gsilu", j - 2)))
                for ec in range(ECN):
                    for g in range(2):
                        ps = gate_ps if g == 0 else up_ps
                        for half in range(2):
                            m = t.matmul(ps[:, p, half * 512:(half + 1) * 512],
                                         wgu_t[:, p, g, ec, :],
                                         xn_mn[:, ec,
                                               half * 512:(half + 1) * 512],
                                         start=(ec == 0), stop=(ec == ECN - 1))
                dd[("gumm", j)] = cx.inc("pe", m)

        def sc_gu(cx, e):
            for j in range(len(pairs)):
                p = j % 2
                cx.wait(e, "scalar", "pe", dd[("gumm", j)])
                if j >= 1:
                    cx.wait(e, "scalar", "ve", dd.get(("hn", j - 1)))
                dd[("gsilu", j)] = cx.inc("sc", e.activation(
                    out=g_sb[:, :], in_=gate_ps[:, p, :], func=AF.Silu))

        def v_gu(cx, v):
            for j, (P, r, k) in enumerate(pairs):
                p = j % 2
                hc = r * 16 + P * 4 + k
                cx.wait(v, "vector", "sc", dd[("gsilu", j)])
                cx.wait(v, "vector", "pe", dd[("gumm", j)])
                dd[("hn", j)] = cx.inc("ve", v.tensor_tensor(
                    hn[:, hc, :], g_sb[:, :], up_ps[:, p, :], OP.mult))
                cx.sw(v, "vector", "ve")
                if j == 0:
                    cx.inc("ve", v.tensor_tensor(
                        sqacc[:, :], hn[:, hc, :], hn[:, hc, :], OP.mult))
                else:
                    cx.inc("ve", v.tensor_tensor(
                        sqtmp[:, :], hn[:, hc, :], hn[:, hc, :], OP.mult))
                    cx.sw(v, "vector", "ve")
                    cx.inc("ve", v.tensor_tensor(
                        sqacc[:, :], sqacc[:, :], sqtmp[:, :], OP.add))

        cx._tag = "gu"
        cx.phase([("sync", s_gul), ("tensor", pe_gu), ("scalar", sc_gu),
                  ("vector", v_gu)])

        cx.barrier("b3", extra={"sync": ["ld_wgu0", "ld_wgu1"]})
        al.free("wgu_t", "g_sb", "gate_ps", "up_ps", "xn_mn")

        rowa = al.sb("rowa2", [1, T], F32)
        rowb = al.sb("rowb2", [1, T], F32)
        rms_f = al.sb("rms_f", [128, T], F32)
        sqf_sb = al.sb("sqf_sb", [128, T], BF16)
        statf_ps = al.ps("statf_ps", [1, T], F32)

        def v_sqf(cx, v):
            cx.sw(v, "vector", "ve")
            dd["sqfb"] = cx.inc("ve", v.tensor_copy(sqf_sb[:, :], sqacc[:, :]))

        def pe_sqf(cx, t):
            cx.wait(t, "tensor", "ve", dd["sqfb"])
            for half in range(2):
                m = t.matmul(statf_ps[0:1, half * 512:(half + 1) * 512],
                             ones_c[:, :],
                             sqf_sb[:, half * 512:(half + 1) * 512],
                             start=True, stop=True)
            dd[("sfmm", 0)] = cx.inc("pe", m)

        def v_rmsf1(cx, v):
            cx.wait(v, "vector", "pe", cx.cnt["pe"])
            dd["rowf"] = cx.inc("ve", v.tensor_copy(rowa[0:1, :],
                                                    statf_ps[0:1, :]))
        def sc_rmsf(cx, e):
            cx.wait(e, "scalar", "ve", dd["rowf"])
            dd["rowfsd"] = cx.inc("sc", e.activation(
                out=rowb[0:1, :], in_=rowa[0:1, :], func=AF.Sqrt,
                bias=cst[0:1, 2:3], scale=cst[0:1, 3:4]))
        def v_rmsf2(cx, v):
            cx.wait(v, "vector", "sc", dd["rowfsd"])
            cx.inc("ve", v.reciprocal(rowa[0:1, :], rowb[0:1, :]))
        cx._tag = "statf"
        cx.phase([("vector", v_sqf), ("tensor", pe_sqf),
                  ("vector", v_rmsf1), ("scalar", sc_rmsf),
                  ("vector", v_rmsf2)])

        al.free("statf_ps", "sqf_sb")
        psum_f = al.ps("psum_f", [128, 2, T], F32)
        bcf_ps = al.ps("bcf_ps", [128, T], F32)

        def pe_bcf(cx, t):
            cx.wait(t, "tensor", "ve", cx.cnt["ve"])
            for half in range(2):
                m = t.matmul(bcf_ps[:, half * 512:(half + 1) * 512], ones_r[:, :],
                             rowa[0:1, half * 512:(half + 1) * 512],
                             start=True, stop=True)
            dd["bcf"] = cx.inc("pe", m)
        def v_bcf(cx, v):
            cx.wait(v, "vector", "pe", dd["bcf"])
            cx.inc("ve", v.tensor_copy(rms_f[:, :], bcf_ps[:, :]))
        cx._tag = "statf"
        cx.phase([("tensor", pe_bcf), ("vector", v_bcf)])
        rmsf_rdy = cx.cnt["ve"]

        wp_t = al.sb("wp_t", [128, 2, HCN, 128], BF16)
        pt_t = al.sb("pt_t", [128, 2, T], F32)
        yv_t = al.sb("yv_t", [128, 2, T], F32)

        def s_ftail(cx, s):
            cx.wait(s, "sync", "ccwp", 1)
            cx.wait(s, "sync", "st_py0", pt_done0)
            cx.wait(s, "sync", "st_py1", pt_done1)
            cx.wait(s, "sync", "pe", dd.get("bcf"))
            cx.wait(s, "sync", "ve", rmsf_rdy)
            def load(dc):
                p = dc % 2
                if dc >= 2:
                    cx.wait(s, "sync", "pe", dd.get(("ftmm", dc - 2)))
                rr, dcl = dc // 2, dc % 2
                dd[("wpld", dc)] = cx.inc(f"ld_wp{p}", s.dma_start(
                    out=wp_t[:, p, :, :],
                    in_=ag_wp_out.ap()[rr * 256 + dcl * 128:
                                       rr * 256 + (dcl + 1) * 128, :]
                        .rearrange("p (hc d) -> p hc d", d=128)), 16)
                if dc >= 2:
                    cx.wait(s, "sync", "ve", dd.get(("fev", dc - 2)))
                dd[("ptld", dc)] = cx.inc(f"ld_pt{p}", s.dma_start(
                    out=pt_t[:, p, :],
                    in_=partial.ap()[dc * 128:(dc + 1) * 128, :]), 16)
            load(0)
            load(1)
            for dc in range(DCN):
                cx.wait(s, "sync", "ve", dd.get(("fev", dc)))
                cx.inc(f"st_y{dc % 2}", s.dma_start(
                    out=y[dc * 128:(dc + 1) * 128, :],
                    in_=yv_t[:, dc % 2, :]), 16)
                if dc + 2 < DCN:
                    load(dc + 2)
            if not cx.dry:
                s.wait_ge(cx.sems["st_y0"], cx.cnt["st_y0"])
                s.wait_ge(cx.sems["st_y1"], cx.cnt["st_y1"])

        def pe_ft(cx, t):
            for dc in range(DCN):
                p = dc % 2
                cx.wait(t, "tensor", f"ld_wp{p}", dd.get(("wpld", dc)))
                if dc >= 2:
                    cx.wait(t, "tensor", "ve", dd.get(("fev", dc - 2)))
                for hc in range(HCN):
                    for half in range(2):
                        m = t.matmul(psum_f[:, p, half * 512:(half + 1) * 512],
                                     wp_t[:, p, hc, :],
                                     hn[:, hc, half * 512:(half + 1) * 512],
                                     start=(hc == 0), stop=(hc == HCN - 1))
                dd[("ftmm", dc)] = cx.inc("pe", m)

        def v_ft(cx, v):
            cx.sw(v, "vector", "ve")
            for dc in range(DCN):
                p = dc % 2
                cx.wait(v, "vector", "pe", dd[("ftmm", dc)])
                cx.wait(v, "vector", f"ld_pt{p}", dd.get(("ptld", dc)))
                if dc >= 2:
                    cx.wait(v, "vector", f"st_y{p}", 16 * (dc // 2))
                cx.inc("ve", v.tensor_tensor(yv_t[:, p, :], psum_f[:, p, :],
                                             rms_f[:, :], OP.mult))
                cx.sw(v, "vector", "ve")
                dd[("fev", dc)] = cx.inc("ve", v.tensor_tensor(
                    yv_t[:, p, :], yv_t[:, p, :], pt_t[:, p, :], OP.add))

        cx._tag = "ftail"
        cx.phase([("tensor", pe_ft), ("vector", v_ft),
                  ("sync", s_ftail)])
        cx.close_sems()
    return nc


def make_in_maps(inputs):
    X = np.asarray(inputs["L0_attn_x"], dtype=np.float32)
    ws = {k: np.asarray(inputs["L0_attn_" + k], dtype=np.float32)
          for k in ("Wq1", "Wq2", "Wk1", "Wk2", "Wv", "Wo", "norm_w")}
    Wgu = np.asarray(inputs["L0_ffn_w_gate_up"], dtype=np.float32)
    Wp = np.asarray(inputs["L0_ffn_w_proj"], dtype=np.float32)
    in_maps = []
    for c in range(NCORES):
        b, hf = c // 2, c % 2
        seq = X[b]
        main = seq[hf * T:(hf + 1) * T]
        warm = np.zeros((W, E), np.float32) if hf == 0 else seq[T - W:T]
        in_maps.append({
            "x_tm": np.ascontiguousarray(np.concatenate([warm, main], axis=0)),
            "x_fm": np.ascontiguousarray(main.T),
            "wq1": ws["Wq1"], "wq2": ws["Wq2"], "wk1": ws["Wk1"],
            "wk2": ws["Wk2"], "wv": ws["Wv"], "wo": ws["Wo"],
            "nrm": ws["norm_w"],
            "wgu_sl": np.ascontiguousarray(
                Wgu[c * (GU // NCORES):(c + 1) * (GU // NCORES)]),
            "wp_sl": np.ascontiguousarray(
                Wp[c * (D // NCORES):(c + 1) * (D // NCORES)]),
        })
    return in_maps


def kernel(**inputs):
    nc = build_program("full")
    in_maps = make_in_maps(inputs)
    res = run_bass_kernel_spmd(nc, in_maps, core_ids=list(range(NCORES)))
    out = np.empty((4, 2048, E), np.float32)
    for c in range(NCORES):
        b, hf = c // 2, c % 2
        out[b, hf * T:(hf + 1) * T, :] = res.results[c]["y"].T
    return out


# version 18
# speedup vs baseline: 1.1822x; 1.0056x over previous
import numpy as np
import ml_dtypes
import concourse.bass as bass
import concourse.mybir as mybir
from concourse.bass_utils import run_bass_kernel_spmd

F32 = mybir.dt.float32
BF16 = mybir.dt.bfloat16
AF = mybir.ActivationFunctionType
OP = mybir.AluOpType
AX = mybir.AxisListType

E = 2048
ECN = 16
C = 4096
CCN = 32
HD = 128
H = 8192
HCN = 64
GU = 16384
D = 2048
DCN = 16
T = 1024
W = 128
TW = T + W
NCORES = 8
TPS = [(0, W, False), (W, 512, True), (W + 512, 512, True)]

BF16_EPS8 = float(np.float32(ml_dtypes.bfloat16(1e-8)))
THR = 0.50048828125
LN_EPS = 1e-5
ATTN_EPS = 1e-6
FFN_EPS = 1.1920928955078125e-07

ENGS = ("sync", "scalar", "vector", "tensor", "gpsimd")


class _DummyInstr:
    def then_inc(self, *a, **k):
        return self


class _DryEng:
    def __getattr__(self, name):
        return lambda *a, **k: _DummyInstr()


class Ctx:

    def __init__(self, nc, block):
        self.nc = nc
        self.block = block
        self.sems = {}
        self.cnt = {}
        self.lastw = {}
        self._live = []
        self.dry = False

    def sem(self, name):
        cm = self.nc.semaphore(name)
        self._live.append(cm)
        self.sems[name] = cm.__enter__()
        self.cnt[name] = 0

    def close_sems(self):
        for cm in reversed(self._live):
            cm.__exit__(None, None, None)

    def wait(self, eng, engname, name, val):
        if val is None:
            return
        v = val
        if v <= 0:
            return
        key = (engname, name)
        if self.lastw.get(key, -1) >= v:
            return
        if not self.dry:
            eng.wait_ge(self.sems[name], v)
        self.lastw[key] = v

    def sw(self, eng, engname, name):
        self.wait(eng, engname, name, self.cnt[name])

    def inc(self, name, instr, k=1):
        if not self.dry:
            instr.then_inc(self.sems[name], k)
        self.cnt[name] += k
        return self.cnt[name]

    def phase(self, bodies):
        snap_cnt = dict(self.cnt)
        snap_w = dict(self.lastw)
        self.dry = True
        de = _DryEng()
        for name, fn in bodies:
            fn(self, de)
        dry_cnt = dict(self.cnt)
        self.cnt = snap_cnt
        self.lastw = snap_w
        self.dry = False
        tagn = getattr(self, "_tag", None)
        for name, fn in bodies:
            if tagn:
                with self.nc.named_scope(tagn):
                    getattr(self.block, name)(lambda eng, fn=fn: fn(self, eng))
            else:
                getattr(self.block, name)(lambda eng, fn=fn: fn(self, eng))
        assert self.cnt == dry_cnt, (self.cnt, dry_cnt)

    def barrier(self, tag, extra=None, engines=ENGS):
        tgt = self.cnt["ph"] + len(engines)
        extra = extra or {}
        def bfn(nm):
            def f(cx, e):
                for s in extra.get(nm, ()):
                    cx.wait(e, nm, s, cx.cnt[s])
                if not cx.dry:
                    if nm in ("scalar", "vector", "tensor"):
                        e.drain()
                    e.sem_inc(cx.sems["ph"], 1)
                    e.wait_ge(cx.sems["ph"], tgt)
                cx.lastw[(nm, "ph")] = tgt
            return f
        for nm in engines:
            getattr(self.block, nm)(lambda eng, nm=nm: bfn(nm)(self, eng))
            self.lastw[(nm, "ph")] = tgt
        self.cnt["ph"] = tgt


class Alloc:
    def __init__(self, nc):
        self.nc = nc
        self.cms = {}
        self.order = {}
        self.n = 0

    def sb(self, name, shape, dtype, side="left"):
        cm = self.nc.sbuf_tensor(name, shape, dtype, side=side)
        h = cm.__enter__()
        self.cms[name] = cm
        self.order[name] = self.n
        self.n += 1
        return h

    def ps(self, name, shape, dtype):
        cm = self.nc.psum_tensor(name, shape, dtype)
        h = cm.__enter__()
        self.cms[name] = cm
        self.order[name] = self.n
        self.n += 1
        return h

    def free(self, *names):
        for n in sorted(names, key=lambda x: -self.order[x]):
            self.cms.pop(n).__exit__(None, None, None)


def build_program(stage="full"):
    lvl = {"xn": 1, "attn": 2, "partial": 3, "full": 5}[stage]
    nc = bass.Bass("TRN2", target_bir_lowering=False, debug=False)

    x_tm = nc.dram_tensor("x_tm", [TW, E], F32, kind="ExternalInput").ap()
    x_fm = nc.dram_tensor("x_fm", [E, T], F32, kind="ExternalInput").ap()
    wq1 = nc.dram_tensor("wq1", [E, HD], F32, kind="ExternalInput").ap()
    wq2 = nc.dram_tensor("wq2", [HD, C], F32, kind="ExternalInput").ap()
    wk1 = nc.dram_tensor("wk1", [E, HD], F32, kind="ExternalInput").ap()
    wk2 = nc.dram_tensor("wk2", [HD, C], F32, kind="ExternalInput").ap()
    wv = nc.dram_tensor("wv", [E, C], F32, kind="ExternalInput").ap()
    wo = nc.dram_tensor("wo", [C, D], F32, kind="ExternalInput").ap()
    nrm = nc.dram_tensor("nrm", [C], F32, kind="ExternalInput").ap()
    wgu_sl = nc.dram_tensor("wgu_sl", [GU // NCORES, E], F32, kind="ExternalInput").ap()
    wp_sl = nc.dram_tensor("wp_sl", [D // NCORES, H], F32, kind="ExternalInput").ap()

    if lvl >= 5:
        y = nc.dram_tensor("y", [D, T], F32, kind="ExternalOutput").ap()
    if stage == "xn":
        dbg_xn = nc.dram_tensor("dbg_xn", [E, TW], BF16, kind="ExternalOutput").ap()
    if stage == "attn":
        dbg_ow = nc.dram_tensor("dbg_ow", [C, T], BF16, kind="ExternalOutput").ap()
        dbg_st = nc.dram_tensor("dbg_st", [1, T], F32, kind="ExternalOutput").ap()
    if stage == "partial":
        dbg_pt = nc.dram_tensor("dbg_pt", [D, T], F32, kind="ExternalOutput").ap()

    ag_gu_in = [nc.dram_tensor(f"ag_gu_in{i}", [512, E], BF16) for i in range(4)]
    ag_gu_out = [nc.dram_tensor(f"ag_gu_out{i}", [NCORES * 512, E], BF16,
                                addr_space="Shared")
                 for i in range(4)]
    ag_wp_in = nc.dram_tensor("ag_wp_in", [D // NCORES, H], BF16)
    ag_wp_out = nc.dram_tensor("ag_wp_out", [NCORES * (D // NCORES), H], BF16,
                               addr_space="Shared")
    partial = nc.dram_tensor("partial", [D, T], F32)
    wv_b = nc.dram_tensor("wv_b", [CCN * 128, E], BF16)
    wo_b = nc.dram_tensor("wo_b", [DCN * 128, C], BF16)

    al = Alloc(nc)
    nrm_sb = al.sb("nrm_sb", [128, CCN], F32, side="right")
    ones_c = al.sb("ones_c", [128, 1], BF16, side="right")
    ones_r = al.sb("ones_r", [1, 128], F32, side="right")
    epsc = al.sb("epsc", [128, 1], F32, side="right")
    cst = al.sb("cst", [1, 4], F32, side="right")
    xn_mn = al.sb("xn_mn", [128, ECN, T], BF16)
    xn_wm = al.sb("xn_wm", [128, ECN, W], BF16)
    wq1_sb = al.sb("wq1_sb", [128, ECN, HD], BF16)
    wk1_sb = al.sb("wk1_sb", [128, ECN, HD], BF16)
    wq2_sb = al.sb("wq2_sb", [128, C], BF16)
    wk2_sb = al.sb("wk2_sb", [128, C], BF16)
    t1_sb = al.sb("t1_sb", [128, T], BF16)
    t2_sb = al.sb("t2_sb", [128, TW], BF16)

    with nc.Block() as block:
        cx = Ctx(nc, block)
        for s in ("ph", "pe", "sc", "ve", "ldsm", "st_y", "cvb0", "cvb1", "cvb2", "cvb3", "cob",
                  "st_ag0", "st_ag1", "st_wp0", "st_wp1", "st_y0", "st_y1",
                  "ccg0", "ccg1", "ccg2", "ccg3", "ccwp",
                  "trs_q0", "trs_q1", "trs_w0", "trs_w1", "ld_wo2"):
            cx.sem(s)
        for s in ("ld_x", "cast", "cwp", "trs_ln", "ld_wv", "ld_wo",
                  "ld_wgu", "ld_wp", "ld_xf", "ld_pt", "st_py"):
            cx.sem(s + "0")
            cx.sem(s + "1")

        NGQ, NPQ = 8, 4
        wgu_buf = al.sb("wgu_buf", [128, 2, 2, E], BF16)
        wp_buf = al.sb("wp_buf", [128, 2, 4096], BF16)
        pos_sb = al.sb("pos_sb", [128, 4096], BF16)
        neg_sb = al.sb("neg_sb", [128, 4096], BF16)
        red_sb = al.sb("red_sb", [128, 64], F32)
        scl_sb = al.sb("scl_sb", [128, 64], BF16)
        thr_sb = al.sb("thr_sb", [128, 64], F32)
        nthr_sb = al.sb("nthr_sb", [128, 64], F32)
        tt_sb = al.sb("tt_sb", [128, 2, 16, 128], BF16)
        tt_wp = al.sb("tt_wp", [128, 2, 32, 128], BF16)
        xt_sb = al.sb("xt_sb", [128, 2, E], F32)
        xnt_sb = al.sb("xnt_sb", [128, 2, E], BF16)
        sum_sb = al.sb("sum_sb", [128, 2, 1], F32)
        ssq_sb = al.sb("ssq_sb", [128, 2, 1], F32)
        mean_sb = al.sb("mean_sb", [128, 2, 1], F32)
        var_sb = al.sb("var_sb", [128, 2, 1], F32)
        sd_sb = al.sb("sd_sb", [128, 2, 1], F32)
        inv_sb = al.sb("inv_sb", [128, 2, 1], F32)

        def xn_sl(ec, off, sz):
            if off < W:
                assert off + sz <= W
                return xn_wm[:, ec, off:off + sz]
            return xn_mn[:, ec, off - W:off - W + sz]

        dd = {}
        NTS = TW // 128

        def quant_ops(cx, v, wbuf, G):
            view = wbuf.rearrange("p (g k) -> p g k", k=64)
            rd = red_sb[:, 0:G]
            sc_ = scl_sb[:, 0:G]
            th = thr_sb[:, 0:G]
            nth = nthr_sb[:, 0:G]
            pv = pos_sb[:, 0:G * 64].rearrange("p (g k) -> p g k", k=64)
            nv = neg_sb[:, 0:G * 64].rearrange("p (g k) -> p g k", k=64)
            cx.inc("ve", v.tensor_reduce(rd, view, AX.X, OP.add,
                                          apply_absolute_value=True))
            cx.sw(v, "vector", "ve")
            cx.inc("ve", v.tensor_scalar(sc_, rd, 1.0 / 64.0, None, OP.mult))
            cx.sw(v, "vector", "ve")
            cx.inc("ve", v.tensor_scalar_max(sc_, sc_, BF16_EPS8))
            cx.sw(v, "vector", "ve")
            v.tensor_scalar(th, sc_, THR, None, OP.mult)
            cx.inc("ve", v.tensor_scalar(nth, sc_, -THR, None, OP.mult))
            cx.sw(v, "vector", "ve")
            v.tensor_tensor(pv, view, th.unsqueeze(2).broadcast_to([128, G, 64]),
                            OP.is_gt)
            cx.inc("ve", v.tensor_tensor(
                nv, view, nth.unsqueeze(2).broadcast_to([128, G, 64]), OP.is_lt))
            cx.sw(v, "vector", "ve")
            cx.inc("ve", v.tensor_tensor(pv, pv, nv, OP.subtract))
            cx.sw(v, "vector", "ve")
            return v.tensor_tensor(view, pv,
                                   sc_.unsqueeze(2).broadcast_to([128, G, 64]),
                                   OP.mult)

        def g_prep(cx, g):
            i = cx.inc
            i("ldsm", g.dma_start(out=wq1_sb[:, :, :],
                                  in_=wq1.rearrange("(a p) m -> p a m", p=128)), 16)
            i("ldsm", g.dma_start(out=wk1_sb[:, :, :],
                                  in_=wk1.rearrange("(a p) m -> p a m", p=128)), 16)
            i("ldsm", g.dma_start(out=wq2_sb[:, :], in_=wq2), 16)
            i("ldsm", g.dma_start(out=wk2_sb[:, :], in_=wk2), 16)
            i("ldsm", g.dma_start(out=nrm_sb[:, :],
                                  in_=nrm.rearrange("(a p) -> p a", p=128),
                                  allow_slow_non_contiguous=True), 16)
            dd["ldsm"] = cx.cnt["ldsm"]
            for q in range(NGQ):
                p = q % 2
                if q >= 2:
                    cx.wait(g, "gpsimd", "trs_q0", dd.get(("gu_trs0", q - 2)))
                    cx.wait(g, "gpsimd", "trs_q1", dd.get(("gu_trs1", q - 2)))
                i(f"cast{p}", g.dma_start(
                    out=wgu_buf[:, p, :, :],
                    in_=wgu_sl[q * 256:(q + 1) * 256, :]
                        .rearrange("(hc p) d -> p hc d", p=128)), 16)
            for q in range(4):
                p = q % 2
                if q >= 2:
                    cx.wait(g, "gpsimd", "trs_w0", dd.get(("wp_trs0", q - 2)))
                    cx.wait(g, "gpsimd", "trs_w1", dd.get(("wp_trs1", q - 2)))
                dcl, half = q // 2, q % 2
                i(f"cwp{p}", g.dma_start(
                    out=wp_buf[:, p, :],
                    in_=wp_sl[dcl * 128:(dcl + 1) * 128,
                              half * 4096:(half + 1) * 4096]), 16)


        def v_memset(cx, v):
            if not cx.dry:
                v.memset(ones_c[:, :], 1.0)
                v.memset(ones_r[:, :], 1.0)
                v.memset(epsc[:, :], LN_EPS)
                v.memset(cst[0:1, 0:1], ATTN_EPS)
                v.memset(cst[0:1, 1:2], 1.0 / C)
                v.memset(cst[0:1, 2:3], FFN_EPS)
                v.memset(cst[0:1, 3:4], 1.0 / H)

        def v_prep(cx, v):
            for q in range(NGQ):
                p = q % 2
                cx.wait(v, "vector", f"cast{p}", 16 * (q // 2 + 1))
                dd[("gu_q", q)] = cx.inc(
                    "ve", quant_ops(cx, v, wgu_buf[:, p, :, :]
                                    .rearrange("p a d -> p (a d)"), 64))
            for q in range(4):
                p = q % 2
                cx.wait(v, "vector", f"cwp{p}", 16 * (q // 2 + 1))
                dd[("wp_q", q)] = cx.inc("ve", quant_ops(cx, v, wp_buf[:, p, :], 64))

        def s_prep(cx, s):
            for q in range(NGQ):
                p = q % 2
                cx.wait(s, "sync", "ve", dd[("gu_q", q)])
                for i2 in range(2):
                    hcl = q * 2 + i2
                    P, col = hcl // 4, hcl % 4
                    pt = hcl % 2
                    if hcl >= 2:
                        cx.wait(s, "sync", f"st_ag{pt}", dd.get(("gu_st", hcl - 2)))
                    cx.inc(f"trs_q{pt}", s.dma_start_transpose(
                        tt_sb[:, pt, :, :],
                        wgu_buf[:, p, i2, :].rearrange("p (dc r) -> p dc r", r=128)),
                        16)
                    dd[("gu_trs" + str(pt), q)] = cx.cnt[f"trs_q{pt}"]
                    cx.wait(s, "sync", f"trs_q{pt}", cx.cnt[f"trs_q{pt}"])
                    dd[("gu_st", hcl)] = cx.inc(f"st_ag{pt}", s.dma_start(
                        out=ag_gu_in[P].ap()[col * 128:(col + 1) * 128, :]
                            .rearrange("p (ec h) -> p ec h", h=128),
                        in_=tt_sb[:, pt, :, :]), 16)
                    if hcl % 4 == 3:
                        dd[("gu_ag0", P)] = cx.cnt["st_ag0"]
                        dd[("gu_ag1", P)] = cx.cnt["st_ag1"]
            for q in range(4):
                p = q % 2
                dcl, half = q // 2, q % 2
                pt = q % 2
                cx.wait(s, "sync", "ve", dd[("wp_q", q)])
                if q >= 2:
                    cx.wait(s, "sync", f"st_wp{pt}", dd.get(("wp_st", q - 2)))
                cx.inc(f"trs_w{pt}", s.dma_start_transpose(
                    tt_wp[:, pt, :, :],
                    wp_buf[:, p, :].rearrange("p (hc r) -> p hc r", r=128)), 16)
                dd[("wp_trs" + str(pt), q)] = cx.cnt[f"trs_w{pt}"]
                cx.wait(s, "sync", f"trs_w{pt}", cx.cnt[f"trs_w{pt}"])
                dd[("wp_st", q)] = cx.inc(f"st_wp{pt}", s.dma_start(
                    out=ag_wp_in.ap()[dcl * 128:(dcl + 1) * 128,
                                      half * 4096:(half + 1) * 4096]
                        .rearrange("p (hc d) -> p hc d", d=128),
                    in_=tt_wp[:, pt, :, :]), 16)
                dd[("wp_ag0", q)] = cx.cnt["st_wp0"]
                dd[("wp_ag1", q)] = cx.cnt["st_wp1"]

        def sc_ln(cx, e):
            def process(ts):
                cx.wait(e, "scalar", "ve", dd.get(("ln_var", ts)))
                dd[("ln_sd", ts)] = cx.inc("sc", e.activation(
                    out=sd_sb[:, ts % 2, :], in_=var_sb[:, ts % 2, :],
                    func=AF.Sqrt, bias=epsc[:, 0:1]))
            for ts in range(NTS):
                p = ts % 2
                if ts >= 2:
                    cx.wait(e, "scalar", "ve", dd.get(("ln_xn", ts - 2)))
                dd[("ln_x", ts)] = cx.inc(f"ld_x{p}", e.dma_start(
                    out=xt_sb[:, p, :], in_=x_tm[ts * 128:(ts + 1) * 128, :]), 16)
                if ts >= 1:
                    process(ts - 1)
            process(NTS - 1)

        def v_ln(cx, v):
            for ts in range(NTS):
                p = ts % 2
                cx.wait(v, "vector", f"ld_x{p}", dd[("ln_x", ts)])
                if ts >= 2:
                    cx.wait(v, "vector", f"trs_ln{p}", dd.get(("ln_tr", ts - 2)))
                cx.inc("ve", v.scalar_tensor_tensor(
                    xnt_sb[:, p, :], xt_sb[:, p, :], 1.0, xt_sb[:, p, :],
                    OP.mult, OP.mult, accum_out=ssq_sb[:, p, :]))
                cx.inc("ve", v.reduce_sum(sum_sb[:, p, :], xt_sb[:, p, :], AX.X))
                cx.sw(v, "vector", "ve")
                cx.inc("ve", v.tensor_scalar(mean_sb[:, p, :], sum_sb[:, p, :],
                                             1.0 / E, None, OP.mult))
                v.tensor_scalar(var_sb[:, p, :], ssq_sb[:, p, :], 1.0 / E,
                                None, OP.mult)
                cx.sw(v, "vector", "ve")
                cx.inc("ve", v.tensor_tensor(inv_sb[:, p, :], mean_sb[:, p, :],
                                             mean_sb[:, p, :], OP.mult))
                cx.sw(v, "vector", "ve")
                dd[("ln_var", ts)] = cx.inc("ve", v.tensor_tensor(
                    var_sb[:, p, :], var_sb[:, p, :], inv_sb[:, p, :],
                    OP.subtract))
                cx.wait(v, "vector", "sc", dd[("ln_sd", ts)])
                cx.inc("ve", v.reciprocal(inv_sb[:, p, :], sd_sb[:, p, :]))
                cx.sw(v, "vector", "ve")
                dd[("ln_xn", ts)] = cx.inc("ve", v.tensor_scalar(
                    xnt_sb[:, p, :], xt_sb[:, p, :], mean_sb[:, p, :],
                    inv_sb[:, p, :], OP.subtract, OP.mult))

        def s_lntr(cx, s):
            for ts in range(NTS):
                p = ts % 2
                cx.wait(s, "sync", "ve", dd[("ln_xn", ts)])
                nw = W // 128
                if ts < nw:
                    dst = xn_wm[:, :, ts * 128:(ts + 1) * 128]
                else:
                    dst = xn_mn[:, :, (ts - nw) * 128:(ts - nw + 1) * 128]
                cx.inc(f"trs_ln{p}", s.dma_start_transpose(
                    dst,
                    xnt_sb[:, p, :].rearrange("p (ec r) -> p ec r", r=128)), 16)
                dd[("ln_tr", ts)] = cx.cnt[f"trs_ln{p}"]

        cx._tag = "p0ln"
        cx.phase([("vector", v_memset), ("scalar", sc_ln), ("vector", v_ln),
                  ("sync", s_lntr), ("vector", v_prep), ("sync", s_prep),
                  ("gpsimd", g_prep)])

        xn_rdy0 = cx.cnt["trs_ln0"]
        xn_rdy1 = cx.cnt["trs_ln1"]

        if stage == "xn":
            def s_dump(cx, s):
                cx.wait(s, "sync", "trs_ln0", xn_rdy0)
                cx.wait(s, "sync", "trs_ln1", xn_rdy1)
                for ec in range(ECN):
                    cx.inc("st_y", s.dma_start(
                        out=dbg_xn[ec * 128:(ec + 1) * 128, 0:W],
                        in_=xn_wm[:, ec, :]), 16)
                    cx.inc("st_y", s.dma_start(
                        out=dbg_xn[ec * 128:(ec + 1) * 128, W:TW],
                        in_=xn_mn[:, ec, :]), 16)
                if not cx.dry:
                    s.wait_ge(cx.sems["st_y"], cx.cnt["st_y"])
            cx.phase([("sync", s_dump)])
            cx.close_sems()
            return nc

        cx.barrier("b1", extra={
            "sync": ["trs_ln0", "trs_ln1", "st_ag0", "st_ag1", "st_wp0",
                     "st_wp1", "trs_q0", "trs_q1", "trs_w0", "trs_w1"],
            "gpsimd": ["ldsm", "cast0", "cast1", "cwp0", "cwp1"],
            "scalar": ["ld_x0", "ld_x1"]})
        al.free("wgu_buf", "wp_buf", "pos_sb", "neg_sb", "red_sb", "scl_sb",
                "thr_sb", "nthr_sb", "tt_sb", "tt_wp", "xt_sb", "xnt_sb",
                "sum_sb", "ssq_sb", "mean_sb", "var_sb", "sd_sb", "inv_sb")

        outw = al.sb("outw", [128, CCN, T], BF16)
        wv_t = al.sb("wv_t", [128, 2, ECN, 128], BF16)
        wo_t = al.sb("wo_t", [128, 3, CCN, 128], BF16)
        rowa = al.sb("rowa", [1, T], F32)
        rowb = al.sb("rowb", [1, T], F32)
        rms_a = al.sb("rms_a", [128, T], F32)
        f_sb = al.sb("f_sb", [128, 2, 512], F32)
        a_sb = al.sb("a_sb", [128, 2, 512], F32)
        k_sb = al.sb("k_sb", [128, 2, 512], F32)
        b_sb = al.sb("b_sb", [128, 2, 512], F32)
        h_sb = al.sb("h_sb", [128, 2, 512], F32)
        q_sb = al.sb("q_sb", [128, 2, 512], F32)
        o_sb = al.sb("o_sb", [128, 2, 512], F32)
        sq_sb = al.sb("sq_sb", [128, 2, 512], BF16)

        logf_ps = al.ps("logf_ps", [128, 2, 512], F32)
        v_ps = al.ps("v_ps", [128, 2, 512], F32)
        q_ps = al.ps("q_ps", [128, 2, 512], F32)
        stat_ps = al.ps("stat_ps", [1, T], F32)

        steps = [(cc, tpi) for cc in range(CCN) for tpi in range(3)]
        qsteps = set(s for s in steps if TPS[s[1]][2])

        def pe_t12(cx, t):
            cx.wait(t, "tensor", "ldsm", dd["ldsm"])
            cx.wait(t, "tensor", "trs_ln0", xn_rdy0)
            cx.wait(t, "tensor", "trs_ln1", xn_rdy1)
            gi = 0
            for (name, wsb, pieces) in (
                    ("t2", wk1_sb, TPS),
                    ("t1", wq1_sb, [TPS[1], TPS[2]])):
                for pi, (off, sz, _) in enumerate(pieces):
                    pg = gi % 2
                    if gi >= 2:
                        cx.wait(t, "tensor", "ve", dd.get(("t12ev", gi - 2)))
                    ps = logf_ps[:, pg, 0:sz]
                    for ec in range(ECN):
                        m = t.matmul(ps, wsb[:, ec, :], xn_sl(ec, off, sz),
                                     start=(ec == 0), stop=(ec == ECN - 1))
                    dd[("t12mm", gi)] = cx.inc("pe", m)
                    gi += 1

        def v_t12(cx, v):
            gi = 0
            for (name, dst, pieces) in (
                    ("t2", t2_sb, TPS),
                    ("t1", t1_sb, [TPS[1], TPS[2]])):
                for pi, (off, sz, _) in enumerate(pieces):
                    pg = gi % 2
                    cx.wait(v, "vector", "pe", dd[("t12mm", gi)])
                    doff = off if name == "t2" else off - W
                    dd[("t12ev", gi)] = cx.inc("ve", v.tensor_copy(
                        dst[:, doff:doff + sz], logf_ps[:, pg, 0:sz]))
                    dd[(name + "ev", pi)] = dd[("t12ev", gi)]
                    gi += 1

        def g_post(cx, g):
            i = cx.inc
            def ag(P):
                if P < NPQ:
                    cx.wait(g, "gpsimd", "st_ag0", dd[("gu_ag0", P)])
                    cx.wait(g, "gpsimd", "st_ag1", dd[("gu_ag1", P)])
                    i(f"ccg{P}", g.collective_compute(
                        "AllGather", OP.bypass,
                        replica_groups=[list(range(NCORES))],
                        ins=[ag_gu_in[P].ap().opt()],
                        outs=[ag_gu_out[P].ap().opt()]), 1)
                else:
                    cx.wait(g, "gpsimd", "st_wp0", dd[("wp_ag0", 3)])
                    cx.wait(g, "gpsimd", "st_wp1", dd[("wp_ag1", 3)])
                    i("ccwp", g.collective_compute(
                        "AllGather", OP.bypass,
                        replica_groups=[list(range(NCORES))],
                        ins=[ag_wp_in.ap().opt()],
                        outs=[ag_wp_out.ap().opt()]), 1)
            for cc in range(CCN):
                i(f"cvb{cc // 8}", g.dma_start(
                    out=wv_b.ap()[cc * 128:(cc + 1) * 128, :]
                        .rearrange("p (ec c) -> p ec c", c=128),
                    in_=wv[:, cc * 128:(cc + 1) * 128]
                        .rearrange("(ec p) c -> p ec c", p=128)), 16)
            for dc in range(DCN):
                i("cob", g.dma_start(
                    out=wo_b.ap()[dc * 128:(dc + 1) * 128, :]
                        .rearrange("p (cc d) -> p cc d", d=128),
                    in_=wo[:, dc * 128:(dc + 1) * 128]
                        .rearrange("(cc p) d -> p cc d", p=128)), 16)
            for P in range(NPQ + 1):
                ag(P)

        def s_wv(cx, s):
            for cc in range(CCN):
                p = cc % 2
                cx.wait(s, "sync", f"cvb{cc // 8}", 128)
                if cc >= 2:
                    cx.wait(s, "sync", "pe", dd.get(("vmm", cc - 2, 2)))
                dd[("wvld", cc)] = cx.inc(f"ld_wv{p}", s.dma_start(
                    out=wv_t[:, p, :, :],
                    in_=wv_b.ap()[cc * 128:(cc + 1) * 128, :]
                        .rearrange("p (ec c) -> p ec c", c=128)), 16)

        def pe_cc(cx, t):
            last_par = {}
            cx.wait(t, "tensor", "ve", dd.get(("t12ev", 4)))
            si = 0
            pend = []
            for cc in range(CCN):
                pwv = cc % 2
                cx.wait(t, "tensor", f"ld_wv{pwv}", dd[("wvld", cc)])
                for tpi, (off, sz, hasq) in enumerate(TPS):
                    s = (cc, tpi)
                    p = si % 2
                    if si >= 2:
                        cx.wait(t, "tensor", "sc", dd.get(("f", steps[si - 2])))
                    cx.wait(t, "tensor", "ve", dd[("t2ev", tpi)])
                    m = t.matmul(logf_ps[:, p, 0:sz],
                                 wk2_sb[:, cc * 128:(cc + 1) * 128],
                                 t2_sb[:, off:off + sz], start=True, stop=True)
                    dd[("logf", s)] = cx.inc("pe", m)
                    if si >= 2:
                        cx.wait(t, "tensor", "ve", dd.get(("b", steps[si - 2])))
                    for ec in range(ECN):
                        m = t.matmul(v_ps[:, p, 0:sz], wv_t[:, pwv, ec, :],
                                     xn_sl(ec, off, sz),
                                     start=(ec == 0), stop=(ec == ECN - 1))
                    dd[("vmm", cc, tpi)] = cx.inc("pe", m)
                    if hasq:
                        prevq = last_par.get(("qps", p))
                        if prevq is not None:
                            cx.wait(t, "tensor", "ve", dd.get(("o", prevq)))
                        last_par[("qps", p)] = s
                        cx.wait(t, "tensor", "ve", dd[("t1ev", tpi - 1)])
                        m = t.matmul(q_ps[:, p, 0:sz],
                                     wq2_sb[:, cc * 128:(cc + 1) * 128],
                                     t1_sb[:, off - W:off - W + sz],
                                     start=True, stop=True)
                        dd[("qmm", s)] = cx.inc("pe", m)
                        pend.append((s, p))
                    if len(pend) > 2:
                        stat_emit(cx, t, *pend.pop(0))
                    si += 1
            for it in pend:
                stat_emit(cx, t, *it)

        def stat_emit(cx, t, s, p):
            cc, tpi = s
            off = TPS[tpi][0] - W
            cx.wait(t, "tensor", "ve", dd.get(("sq", s)))
            m = t.matmul(stat_ps[0:1, off:off + 512], ones_c[:, :],
                         sq_sb[:, p, :], start=(cc == 0), stop=(cc == CCN - 1))
            dd[("stat", s)] = cx.inc("pe", m)

        def sc_cc(cx, e):
            last_par = {}
            si = 0
            pend = []
            for cc in range(CCN):
                for tpi, (off, sz, hasq) in enumerate(TPS):
                    s = (cc, tpi)
                    p = si % 2
                    cx.wait(e, "scalar", "pe", dd[("logf", s)])
                    if si >= 2 and steps[si - 2] in qsteps:
                        cx.wait(e, "scalar", "ve", dd.get(("o", steps[si - 2])))
                    dd[("f", s)] = cx.inc("sc", e.activation(
                        out=f_sb[:, p, 0:sz], in_=logf_ps[:, p, 0:sz],
                        func=AF.Sigmoid))
                    cx.sw(e, "scalar", "sc")
                    if si >= 2:
                        cx.wait(e, "scalar", "ve", dd.get(("scan", steps[si - 2])))
                    dd[("a", s)] = cx.inc("sc", e.activation(
                        out=a_sb[:, p, 0:sz], in_=f_sb[:, p, 0:sz],
                        func=AF.Sigmoid))
                    if hasq:
                        cx.wait(e, "scalar", "pe", dd[("qmm", s)])
                        prevq = last_par.get(("qsb", p))
                        if prevq is not None:
                            cx.wait(e, "scalar", "ve", dd.get(("o1", prevq)))
                        last_par[("qsb", p)] = s
                        dd[("sig", s)] = cx.inc("sc", e.activation(
                            out=q_sb[:, p, 0:sz], in_=q_ps[:, p, 0:sz],
                            func=AF.Sigmoid))
                    si += 1

        def v_cc(cx, v):
            last_par = {}
            si = 0
            for cc in range(CCN):
                for tpi, (off, sz, hasq) in enumerate(TPS):
                    s = (cc, tpi)
                    p = si % 2
                    cx.wait(v, "vector", "sc", dd[("a", s)])
                    cx.inc("ve", v.tensor_scalar(k_sb[:, p, 0:sz],
                                                 a_sb[:, p, 0:sz],
                                                 -1.0, 1.0, OP.mult, OP.add))
                    cx.sw(v, "vector", "ve")
                    cx.wait(v, "vector", "pe", dd[("vmm", cc, tpi)])
                    dd[("b", s)] = cx.inc("ve", v.tensor_tensor(
                        b_sb[:, p, 0:sz], k_sb[:, p, 0:sz], v_ps[:, p, 0:sz],
                        OP.mult))
                    cx.sw(v, "vector", "ve")
                    if tpi == 0:
                        init = 0.0
                    else:
                        psz = TPS[tpi - 1][1]
                        init = h_sb[:, 1 - p, psz - 1:psz]
                    dd[("scan", s)] = cx.inc("ve", v.tensor_tensor_scan(
                        h_sb[:, p, 0:sz], a_sb[:, p, 0:sz], b_sb[:, p, 0:sz],
                        init, OP.mult, OP.add))
                    cx.sw(v, "vector", "ve")
                    if hasq:
                        cx.wait(v, "vector", "sc", dd[("sig", s)])
                        dd[("o1", s)] = cx.inc("ve", v.tensor_tensor(
                            f_sb[:, p, 0:sz], q_sb[:, p, 0:sz],
                            h_sb[:, p, 0:sz], OP.mult))
                        cx.sw(v, "vector", "ve")
                        dd[("o", s)] = cx.inc("ve", v.tensor_tensor(
                            o_sb[:, p, 0:sz], f_sb[:, p, 0:sz],
                            q_ps[:, p, 0:sz], OP.mult))
                        cx.sw(v, "vector", "ve")
                        prevq = last_par.get(("sqb", p))
                        if prevq is not None:
                            cx.wait(v, "vector", "pe", dd.get(("stat", prevq)))
                        last_par[("sqb", p)] = s
                        dd[("sq", s)] = cx.inc("ve", v.tensor_tensor(
                            sq_sb[:, p, :], o_sb[:, p, 0:sz], o_sb[:, p, 0:sz],
                            OP.mult))
                        dd[("outw", s)] = cx.inc("ve", v.tensor_scalar(
                            outw[:, cc, off - W:off - W + sz], o_sb[:, p, 0:sz],
                            nrm_sb[:, cc:cc + 1], None, OP.mult))
                    si += 1

        cx._tag = "attn"
        cx.phase([("gpsimd", g_post),
                  ("tensor", pe_t12), ("vector", v_t12), ("sync", s_wv),
                  ("tensor", pe_cc), ("scalar", sc_cc), ("vector", v_cc)])

        if stage == "attn":
            def v_dump(cx, v):
                cx.wait(v, "vector", "pe", cx.cnt["pe"])
                dd["strow"] = cx.inc("ve", v.tensor_copy(rowa[0:1, :],
                                                         stat_ps[0:1, :]))
            def s_dumpa(cx, s):
                cx.wait(s, "sync", "ve", cx.cnt["ve"])
                for cc in range(CCN):
                    cx.inc("st_y", s.dma_start(
                        out=dbg_ow[cc * 128:(cc + 1) * 128, :],
                        in_=outw[:, cc, :]), 16)
                cx.inc("st_y", s.dma_start(out=dbg_st, in_=rowa[0:1, :]), 16)
                if not cx.dry:
                    s.wait_ge(cx.sems["st_y"], cx.cnt["st_y"])
            cx.phase([("vector", v_dump), ("sync", s_dumpa)])
            cx.close_sems()
            return nc

        def v_rmsa1(cx, v):
            cx.wait(v, "vector", "pe", cx.cnt["pe"])
            dd["rowa"] = cx.inc("ve", v.tensor_copy(rowa[0:1, :],
                                                    stat_ps[0:1, :]))
        def sc_rmsa(cx, e):
            cx.wait(e, "scalar", "ve", dd["rowa"])
            dd["rowsd"] = cx.inc("sc", e.activation(
                out=rowb[0:1, :], in_=rowa[0:1, :], func=AF.Sqrt,
                bias=cst[0:1, 0:1], scale=cst[0:1, 1:2]))
        def v_rmsa2(cx, v):
            cx.wait(v, "vector", "sc", dd["rowsd"])
            dd["rowinv"] = cx.inc("ve", v.reciprocal(rowa[0:1, :], rowb[0:1, :]))
        cx._tag = "rmsa"
        cx.phase([("vector", v_rmsa1), ("scalar", sc_rmsa),
                  ("vector", v_rmsa2)])

        al.free("logf_ps", "v_ps", "q_ps", "stat_ps")
        al.free("f_sb", "a_sb", "k_sb", "b_sb", "h_sb", "q_sb", "o_sb",
                "sq_sb")
        xf_t = al.sb("xf_t", [128, 2, T], F32)
        py_t = al.sb("py_t", [128, 2, T], F32)
        psum_a = al.ps("psum_a", [128, 3, T], F32)
        bc_ps = al.ps("bc_ps", [128, T], F32)

        def pe_bca(cx, t):
            cx.wait(t, "tensor", "ve", cx.cnt["ve"])
            cx.wait(t, "tensor", "sc", cx.cnt["sc"])
            for half in range(2):
                m = t.matmul(bc_ps[:, half * 512:(half + 1) * 512], ones_r[:, :],
                             rowa[0:1, half * 512:(half + 1) * 512],
                             start=True, stop=True)
            dd["bca"] = cx.inc("pe", m)
        def v_bca(cx, v):
            cx.wait(v, "vector", "pe", dd["bca"])
            dd["rmsabc"] = cx.inc("ve", v.tensor_copy(rms_a[:, :], bc_ps[:, :]))
        cx._tag = "rmsa"
        cx.phase([("tensor", pe_bca), ("vector", v_bca)])

        def wo_load(cx, s, dc):
            p = dc % 3
            if dc >= 3:
                cx.wait(s, "sync", "pe", dd.get(("atmm", dc - 3)))
            dd[("wold", dc)] = cx.inc(f"ld_wo{p}", s.dma_start(
                out=wo_t[:, p, :, :],
                in_=wo_b.ap()[dc * 128:(dc + 1) * 128, :]
                    .rearrange("p (cc d) -> p cc d", d=128)), 16)

        def s_atail(cx, s):
            cx.wait(s, "sync", "cob", 16 * DCN)
            wo_load(cx, s, 0)
            wo_load(cx, s, 1)
            wo_load(cx, s, 2)
            cx.wait(s, "sync", "ve", dd.get("rowinv"))
            def load(dc):
                p = dc % 2
                if dc >= 2:
                    cx.wait(s, "sync", "ve", dd.get(("aev", dc - 2)))
                dd[("xfld", dc)] = cx.inc(f"ld_xf{p}", s.dma_start(
                    out=xf_t[:, p, :], in_=x_fm[dc * 128:(dc + 1) * 128, :]), 16)
            load(0)
            load(1)
            for dc in range(DCN):
                cx.wait(s, "sync", "ve", dd.get(("aev", dc)))
                cx.inc(f"st_py{dc % 2}", s.dma_start(
                    out=partial.ap()[dc * 128:(dc + 1) * 128, :],
                    in_=py_t[:, dc % 2, :]), 16)
                if dc + 3 < DCN:
                    wo_load(cx, s, dc + 3)
                if dc + 2 < DCN:
                    load(dc + 2)

        def pe_at(cx, t):
            for dc in range(DCN):
                p = dc % 3
                cx.wait(t, "tensor", f"ld_wo{p}", dd.get(("wold", dc)))
                if dc >= 3:
                    cx.wait(t, "tensor", "ve", dd.get(("aev", dc - 3)))
                for cc in range(CCN):
                    for half in range(2):
                        m = t.matmul(psum_a[:, p, half * 512:(half + 1) * 512],
                                     wo_t[:, p, cc, :],
                                     outw[:, cc, half * 512:(half + 1) * 512],
                                     start=(cc == 0), stop=(cc == CCN - 1))
                dd[("atmm", dc)] = cx.inc("pe", m)

        def v_at(cx, v):
            cx.wait(v, "vector", "ve", dd.get("rmsabc"))
            for dc in range(DCN):
                p = dc % 2
                pp = dc % 3
                cx.wait(v, "vector", "pe", dd[("atmm", dc)])
                cx.wait(v, "vector", f"ld_xf{p}", dd.get(("xfld", dc)))
                if dc >= 2:
                    cx.wait(v, "vector", f"st_py{p}", 16 * (dc // 2))
                cx.inc("ve", v.tensor_tensor(py_t[:, p, :], psum_a[:, pp, :],
                                             rms_a[:, :], OP.mult))
                cx.sw(v, "vector", "ve")
                dd[("aev", dc)] = cx.inc("ve", v.tensor_tensor(
                    py_t[:, p, :], py_t[:, p, :], xf_t[:, p, :], OP.add))

        cx._tag = "atail"
        cx.phase([("tensor", pe_at), ("vector", v_at), ("sync", s_atail)])
        pt_done0 = cx.cnt["st_py0"]
        pt_done1 = cx.cnt["st_py1"]

        if stage == "partial":
            def s_dumpp(cx, s):
                cx.wait(s, "sync", "st_py0", pt_done0)
                cx.wait(s, "sync", "st_py1", pt_done1)
                cx.inc("st_y", s.dma_start(out=dbg_pt, in_=partial.ap()), 16)
                if not cx.dry:
                    s.wait_ge(cx.sems["st_y"], cx.cnt["st_y"])
            cx.phase([("sync", s_dumpp)])
            cx.close_sems()
            return nc

        cx.barrier("b2", extra={
            "sync": ["st_py0", "st_py1", "ld_xf0", "ld_xf1", "ld_wv0",
                     "ld_wv1", "ld_wo0", "ld_wo1", "ld_wo2"]})
        al.free("outw", "wv_t", "wo_t", "xf_t", "py_t",
                "psum_a", "bc_ps", "rowa", "rowb", "rms_a",
                "wq1_sb", "wk1_sb", "wq2_sb", "wk2_sb", "t1_sb", "t2_sb",
                "xn_wm")

        hn = al.sb("hn", [128, HCN, T], BF16, side="right")
        sqtmp = al.sb("sqtmp", [128, T], F32, side="right")
        sqacc = al.sb("sqacc", [128, T], F32, side="right")
        wgu_t = al.sb("wgu_t", [128, 2, 2, ECN, 128], BF16)
        g_sb = al.sb("g_sb", [128, T], F32)
        gate_ps = al.ps("gate_ps", [128, 2, T], F32)
        up_ps = al.ps("up_ps", [128, 2, T], F32)

        pairs = [(P, r, k) for P in range(4) for r in range(4) for k in range(4)]

        def s_gul(cx, s):
            for j, (P, r, k) in enumerate(pairs):
                p = j % 2
                cx.wait(s, "sync", f"ccg{P}", 1)
                if j >= 2:
                    cx.wait(s, "sync", "pe", dd.get(("gumm", j - 2)))
                cx.inc(f"ld_wgu{p}", s.dma_start(
                    out=wgu_t[:, p, 0, :, :],
                    in_=ag_gu_out[P].ap()[r * 512 + k * 128:r * 512 + (k + 1) * 128, :]
                        .rearrange("p (ec h) -> p ec h", h=128)), 16)
                dd[("guld", j)] = cx.inc(f"ld_wgu{p}", s.dma_start(
                    out=wgu_t[:, p, 1, :, :],
                    in_=ag_gu_out[P].ap()[(r + 4) * 512 + k * 128:
                                          (r + 4) * 512 + (k + 1) * 128, :]
                        .rearrange("p (ec h) -> p ec h", h=128)), 16)

        def pe_gu(cx, t):
            for j in range(len(pairs)):
                p = j % 2
                cx.wait(t, "tensor", f"ld_wgu{p}", dd[("guld", j)])
                if j >= 2:
                    cx.wait(t, "tensor", "ve", dd.get(("hn", j - 2)))
                    cx.wait(t, "tensor", "sc", dd.get(("gsilu", j - 2)))
                for ec in range(ECN):
                    for g in range(2):
                        ps = gate_ps if g == 0 else up_ps
                        for half in range(2):
                            m = t.matmul(ps[:, p, half * 512:(half + 1) * 512],
                                         wgu_t[:, p, g, ec, :],
                                         xn_mn[:, ec,
                                               half * 512:(half + 1) * 512],
                                         start=(ec == 0), stop=(ec == ECN - 1))
                dd[("gumm", j)] = cx.inc("pe", m)

        def sc_gu(cx, e):
            for j in range(len(pairs)):
                p = j % 2
                cx.wait(e, "scalar", "pe", dd[("gumm", j)])
                if j >= 1:
                    cx.wait(e, "scalar", "ve", dd.get(("hn", j - 1)))
                dd[("gsilu", j)] = cx.inc("sc", e.activation(
                    out=g_sb[:, :], in_=gate_ps[:, p, :], func=AF.Silu))

        def v_gu(cx, v):
            for j, (P, r, k) in enumerate(pairs):
                p = j % 2
                hc = r * 16 + P * 4 + k
                cx.wait(v, "vector", "sc", dd[("gsilu", j)])
                cx.wait(v, "vector", "pe", dd[("gumm", j)])
                dd[("hn", j)] = cx.inc("ve", v.tensor_tensor(
                    hn[:, hc, :], g_sb[:, :], up_ps[:, p, :], OP.mult))
                cx.sw(v, "vector", "ve")
                if j == 0:
                    cx.inc("ve", v.tensor_tensor(
                        sqacc[:, :], hn[:, hc, :], hn[:, hc, :], OP.mult))
                else:
                    cx.inc("ve", v.tensor_tensor(
                        sqtmp[:, :], hn[:, hc, :], hn[:, hc, :], OP.mult))
                    cx.sw(v, "vector", "ve")
                    cx.inc("ve", v.tensor_tensor(
                        sqacc[:, :], sqacc[:, :], sqtmp[:, :], OP.add))

        cx._tag = "gu"
        cx.phase([("sync", s_gul), ("tensor", pe_gu), ("scalar", sc_gu),
                  ("vector", v_gu)])

        cx.barrier("b3", extra={"sync": ["ld_wgu0", "ld_wgu1"]})
        al.free("wgu_t", "g_sb", "gate_ps", "up_ps", "xn_mn")

        rowa = al.sb("rowa2", [1, T], F32)
        rowb = al.sb("rowb2", [1, T], F32)
        rms_f = al.sb("rms_f", [128, T], F32)
        sqf_sb = al.sb("sqf_sb", [128, T], BF16)
        statf_ps = al.ps("statf_ps", [1, T], F32)

        def v_sqf(cx, v):
            cx.sw(v, "vector", "ve")
            dd["sqfb"] = cx.inc("ve", v.tensor_copy(sqf_sb[:, :], sqacc[:, :]))

        def pe_sqf(cx, t):
            cx.wait(t, "tensor", "ve", dd["sqfb"])
            for half in range(2):
                m = t.matmul(statf_ps[0:1, half * 512:(half + 1) * 512],
                             ones_c[:, :],
                             sqf_sb[:, half * 512:(half + 1) * 512],
                             start=True, stop=True)
            dd[("sfmm", 0)] = cx.inc("pe", m)

        def v_rmsf1(cx, v):
            cx.wait(v, "vector", "pe", cx.cnt["pe"])
            dd["rowf"] = cx.inc("ve", v.tensor_copy(rowa[0:1, :],
                                                    statf_ps[0:1, :]))
        def sc_rmsf(cx, e):
            cx.wait(e, "scalar", "ve", dd["rowf"])
            dd["rowfsd"] = cx.inc("sc", e.activation(
                out=rowb[0:1, :], in_=rowa[0:1, :], func=AF.Sqrt,
                bias=cst[0:1, 2:3], scale=cst[0:1, 3:4]))
        def v_rmsf2(cx, v):
            cx.wait(v, "vector", "sc", dd["rowfsd"])
            cx.inc("ve", v.reciprocal(rowa[0:1, :], rowb[0:1, :]))
        cx._tag = "statf"
        cx.phase([("vector", v_sqf), ("tensor", pe_sqf),
                  ("vector", v_rmsf1), ("scalar", sc_rmsf),
                  ("vector", v_rmsf2)])

        al.free("statf_ps", "sqf_sb")
        psum_f = al.ps("psum_f", [128, 2, T], F32)
        bcf_ps = al.ps("bcf_ps", [128, T], F32)

        def pe_bcf(cx, t):
            cx.wait(t, "tensor", "ve", cx.cnt["ve"])
            for half in range(2):
                m = t.matmul(bcf_ps[:, half * 512:(half + 1) * 512], ones_r[:, :],
                             rowa[0:1, half * 512:(half + 1) * 512],
                             start=True, stop=True)
            dd["bcf"] = cx.inc("pe", m)
        def v_bcf(cx, v):
            cx.wait(v, "vector", "pe", dd["bcf"])
            cx.inc("ve", v.tensor_copy(rms_f[:, :], bcf_ps[:, :]))
        cx._tag = "statf"
        cx.phase([("tensor", pe_bcf), ("vector", v_bcf)])
        rmsf_rdy = cx.cnt["ve"]

        wp_t = al.sb("wp_t", [128, 2, HCN, 128], BF16)
        pt_t = al.sb("pt_t", [128, 2, T], F32)
        yv_t = al.sb("yv_t", [128, 2, T], F32)

        def s_ftail(cx, s):
            cx.wait(s, "sync", "ccwp", 1)
            cx.wait(s, "sync", "st_py0", pt_done0)
            cx.wait(s, "sync", "st_py1", pt_done1)
            cx.wait(s, "sync", "pe", dd.get("bcf"))
            cx.wait(s, "sync", "ve", rmsf_rdy)
            def load(dc):
                p = dc % 2
                if dc >= 2:
                    cx.wait(s, "sync", "pe", dd.get(("ftmm", dc - 2)))
                rr, dcl = dc // 2, dc % 2
                dd[("wpld", dc)] = cx.inc(f"ld_wp{p}", s.dma_start(
                    out=wp_t[:, p, :, :],
                    in_=ag_wp_out.ap()[rr * 256 + dcl * 128:
                                       rr * 256 + (dcl + 1) * 128, :]
                        .rearrange("p (hc d) -> p hc d", d=128)), 16)
                if dc >= 2:
                    cx.wait(s, "sync", "ve", dd.get(("fev", dc - 2)))
                dd[("ptld", dc)] = cx.inc(f"ld_pt{p}", s.dma_start(
                    out=pt_t[:, p, :],
                    in_=partial.ap()[dc * 128:(dc + 1) * 128, :]), 16)
            load(0)
            load(1)
            for dc in range(DCN):
                cx.wait(s, "sync", "ve", dd.get(("fev", dc)))
                cx.inc(f"st_y{dc % 2}", s.dma_start(
                    out=y[dc * 128:(dc + 1) * 128, :],
                    in_=yv_t[:, dc % 2, :]), 16)
                if dc + 2 < DCN:
                    load(dc + 2)
            if not cx.dry:
                s.wait_ge(cx.sems["st_y0"], cx.cnt["st_y0"])
                s.wait_ge(cx.sems["st_y1"], cx.cnt["st_y1"])

        def pe_ft(cx, t):
            for dc in range(DCN):
                p = dc % 2
                cx.wait(t, "tensor", f"ld_wp{p}", dd.get(("wpld", dc)))
                if dc >= 2:
                    cx.wait(t, "tensor", "ve", dd.get(("fev", dc - 2)))
                for hc in range(HCN):
                    for half in range(2):
                        m = t.matmul(psum_f[:, p, half * 512:(half + 1) * 512],
                                     wp_t[:, p, hc, :],
                                     hn[:, hc, half * 512:(half + 1) * 512],
                                     start=(hc == 0), stop=(hc == HCN - 1))
                dd[("ftmm", dc)] = cx.inc("pe", m)

        def v_ft(cx, v):
            cx.sw(v, "vector", "ve")
            for dc in range(DCN):
                p = dc % 2
                cx.wait(v, "vector", "pe", dd[("ftmm", dc)])
                cx.wait(v, "vector", f"ld_pt{p}", dd.get(("ptld", dc)))
                if dc >= 2:
                    cx.wait(v, "vector", f"st_y{p}", 16 * (dc // 2))
                cx.inc("ve", v.tensor_tensor(yv_t[:, p, :], psum_f[:, p, :],
                                             rms_f[:, :], OP.mult))
                cx.sw(v, "vector", "ve")
                dd[("fev", dc)] = cx.inc("ve", v.tensor_tensor(
                    yv_t[:, p, :], yv_t[:, p, :], pt_t[:, p, :], OP.add))

        cx._tag = "ftail"
        cx.phase([("tensor", pe_ft), ("vector", v_ft),
                  ("sync", s_ftail)])
        cx.close_sems()
    return nc


def make_in_maps(inputs):
    X = np.asarray(inputs["L0_attn_x"], dtype=np.float32)
    ws = {k: np.asarray(inputs["L0_attn_" + k], dtype=np.float32)
          for k in ("Wq1", "Wq2", "Wk1", "Wk2", "Wv", "Wo", "norm_w")}
    Wgu = np.asarray(inputs["L0_ffn_w_gate_up"], dtype=np.float32)
    Wp = np.asarray(inputs["L0_ffn_w_proj"], dtype=np.float32)
    in_maps = []
    for c in range(NCORES):
        b, hf = c // 2, c % 2
        seq = X[b]
        main = seq[hf * T:(hf + 1) * T]
        warm = np.zeros((W, E), np.float32) if hf == 0 else seq[T - W:T]
        in_maps.append({
            "x_tm": np.ascontiguousarray(np.concatenate([warm, main], axis=0)),
            "x_fm": np.ascontiguousarray(main.T),
            "wq1": ws["Wq1"], "wq2": ws["Wq2"], "wk1": ws["Wk1"],
            "wk2": ws["Wk2"], "wv": ws["Wv"], "wo": ws["Wo"],
            "nrm": ws["norm_w"],
            "wgu_sl": np.ascontiguousarray(
                Wgu[c * (GU // NCORES):(c + 1) * (GU // NCORES)]),
            "wp_sl": np.ascontiguousarray(
                Wp[c * (D // NCORES):(c + 1) * (D // NCORES)]),
        })
    return in_maps


def kernel(**inputs):
    nc = build_program("full")
    in_maps = make_in_maps(inputs)
    res = run_bass_kernel_spmd(nc, in_maps, core_ids=list(range(NCORES)))
    out = np.empty((4, 2048, E), np.float32)
    for c in range(NCORES):
        b, hf = c // 2, c % 2
        out[b, hf * T:(hf + 1) * T, :] = res.results[c]["y"].T
    return out


# version 19
# speedup vs baseline: 1.1873x; 1.0043x over previous
import numpy as np
import ml_dtypes
import concourse.bass as bass
import concourse.mybir as mybir
from concourse.bass_utils import run_bass_kernel_spmd

F32 = mybir.dt.float32
BF16 = mybir.dt.bfloat16
AF = mybir.ActivationFunctionType
OP = mybir.AluOpType
AX = mybir.AxisListType

E = 2048
ECN = 16
C = 4096
CCN = 32
HD = 128
H = 8192
HCN = 64
GU = 16384
D = 2048
DCN = 16
T = 1024
W = 128
TW = T + W
NCORES = 8
TPS = [(0, W, False), (W, 512, True), (W + 512, 512, True)]

BF16_EPS8 = float(np.float32(ml_dtypes.bfloat16(1e-8)))
THR = 0.50048828125
LN_EPS = 1e-5
ATTN_EPS = 1e-6
FFN_EPS = 1.1920928955078125e-07

ENGS = ("sync", "scalar", "vector", "tensor", "gpsimd")


class _DummyInstr:
    def then_inc(self, *a, **k):
        return self


class _DryEng:
    def __getattr__(self, name):
        return lambda *a, **k: _DummyInstr()


class Ctx:

    def __init__(self, nc, block):
        self.nc = nc
        self.block = block
        self.sems = {}
        self.cnt = {}
        self.lastw = {}
        self._live = []
        self.dry = False

    def sem(self, name):
        cm = self.nc.semaphore(name)
        self._live.append(cm)
        self.sems[name] = cm.__enter__()
        self.cnt[name] = 0

    def close_sems(self):
        for cm in reversed(self._live):
            cm.__exit__(None, None, None)

    def wait(self, eng, engname, name, val):
        if val is None:
            return
        v = val
        if v <= 0:
            return
        key = (engname, name)
        if self.lastw.get(key, -1) >= v:
            return
        if not self.dry:
            eng.wait_ge(self.sems[name], v)
        self.lastw[key] = v

    def sw(self, eng, engname, name):
        self.wait(eng, engname, name, self.cnt[name])

    def inc(self, name, instr, k=1):
        if not self.dry:
            instr.then_inc(self.sems[name], k)
        self.cnt[name] += k
        return self.cnt[name]

    def phase(self, bodies):
        snap_cnt = dict(self.cnt)
        snap_w = dict(self.lastw)
        self.dry = True
        de = _DryEng()
        for name, fn in bodies:
            fn(self, de)
        dry_cnt = dict(self.cnt)
        self.cnt = snap_cnt
        self.lastw = snap_w
        self.dry = False
        tagn = getattr(self, "_tag", None)
        for name, fn in bodies:
            if tagn:
                with self.nc.named_scope(tagn):
                    getattr(self.block, name)(lambda eng, fn=fn: fn(self, eng))
            else:
                getattr(self.block, name)(lambda eng, fn=fn: fn(self, eng))
        assert self.cnt == dry_cnt, (self.cnt, dry_cnt)

    def barrier(self, tag, extra=None, engines=ENGS):
        tgt = self.cnt["ph"] + len(engines)
        extra = extra or {}
        def bfn(nm):
            def f(cx, e):
                for s in extra.get(nm, ()):
                    cx.wait(e, nm, s, cx.cnt[s])
                if not cx.dry:
                    if nm in ("scalar", "vector", "tensor"):
                        e.drain()
                    e.sem_inc(cx.sems["ph"], 1)
                    e.wait_ge(cx.sems["ph"], tgt)
                cx.lastw[(nm, "ph")] = tgt
            return f
        for nm in engines:
            getattr(self.block, nm)(lambda eng, nm=nm: bfn(nm)(self, eng))
            self.lastw[(nm, "ph")] = tgt
        self.cnt["ph"] = tgt


class Alloc:
    def __init__(self, nc):
        self.nc = nc
        self.cms = {}
        self.order = {}
        self.n = 0

    def sb(self, name, shape, dtype, side="left"):
        cm = self.nc.sbuf_tensor(name, shape, dtype, side=side)
        h = cm.__enter__()
        self.cms[name] = cm
        self.order[name] = self.n
        self.n += 1
        return h

    def ps(self, name, shape, dtype):
        cm = self.nc.psum_tensor(name, shape, dtype)
        h = cm.__enter__()
        self.cms[name] = cm
        self.order[name] = self.n
        self.n += 1
        return h

    def free(self, *names):
        for n in sorted(names, key=lambda x: -self.order[x]):
            self.cms.pop(n).__exit__(None, None, None)


def build_program(stage="full"):
    lvl = {"xn": 1, "attn": 2, "partial": 3, "full": 5}[stage]
    nc = bass.Bass("TRN2", target_bir_lowering=False, debug=False)

    x_tm = nc.dram_tensor("x_tm", [TW, E], F32, kind="ExternalInput").ap()
    x_fm = nc.dram_tensor("x_fm", [E, T], F32, kind="ExternalInput").ap()
    wq1 = nc.dram_tensor("wq1", [E, HD], F32, kind="ExternalInput").ap()
    wq2 = nc.dram_tensor("wq2", [HD, C], F32, kind="ExternalInput").ap()
    wk1 = nc.dram_tensor("wk1", [E, HD], F32, kind="ExternalInput").ap()
    wk2 = nc.dram_tensor("wk2", [HD, C], F32, kind="ExternalInput").ap()
    wv = nc.dram_tensor("wv", [E, C], F32, kind="ExternalInput").ap()
    wo = nc.dram_tensor("wo", [C, D], F32, kind="ExternalInput").ap()
    nrm = nc.dram_tensor("nrm", [C], F32, kind="ExternalInput").ap()
    wgu_sl = nc.dram_tensor("wgu_sl", [GU // NCORES, E], F32, kind="ExternalInput").ap()
    wp_sl = nc.dram_tensor("wp_sl", [D // NCORES, H], F32, kind="ExternalInput").ap()

    if lvl >= 5:
        y = nc.dram_tensor("y", [D, T], F32, kind="ExternalOutput").ap()
    if stage == "xn":
        dbg_xn = nc.dram_tensor("dbg_xn", [E, TW], BF16, kind="ExternalOutput").ap()
    if stage == "attn":
        dbg_ow = nc.dram_tensor("dbg_ow", [C, T], BF16, kind="ExternalOutput").ap()
        dbg_st = nc.dram_tensor("dbg_st", [1, T], F32, kind="ExternalOutput").ap()
    if stage == "partial":
        dbg_pt = nc.dram_tensor("dbg_pt", [D, T], F32, kind="ExternalOutput").ap()

    ag_gu_in = [nc.dram_tensor(f"ag_gu_in{i}", [512, E], BF16) for i in range(4)]
    ag_gu_out = [nc.dram_tensor(f"ag_gu_out{i}", [NCORES * 512, E], BF16,
                                addr_space="Shared")
                 for i in range(4)]
    ag_wp_in = nc.dram_tensor("ag_wp_in", [D // NCORES, H], BF16)
    ag_wp_out = nc.dram_tensor("ag_wp_out", [NCORES * (D // NCORES), H], BF16,
                               addr_space="Shared")
    partial = nc.dram_tensor("partial", [D, T], F32)
    wv_b = nc.dram_tensor("wv_b", [CCN * 128, E], BF16)
    wo_b = nc.dram_tensor("wo_b", [DCN * 128, C], BF16)

    al = Alloc(nc)
    nrm_sb = al.sb("nrm_sb", [128, CCN], F32, side="right")
    ones_c = al.sb("ones_c", [128, 1], BF16, side="right")
    ones_r = al.sb("ones_r", [1, 128], F32, side="right")
    epsc = al.sb("epsc", [128, 1], F32, side="right")
    cst = al.sb("cst", [1, 4], F32, side="right")
    xn_mn = al.sb("xn_mn", [128, ECN, T], BF16)
    xn_wm = al.sb("xn_wm", [128, ECN, W], BF16)
    wq1_sb = al.sb("wq1_sb", [128, ECN, HD], BF16)
    wk1_sb = al.sb("wk1_sb", [128, ECN, HD], BF16)
    wq2_sb = al.sb("wq2_sb", [128, C], BF16)
    wk2_sb = al.sb("wk2_sb", [128, C], BF16)
    t1_sb = al.sb("t1_sb", [128, T], BF16)
    t2_sb = al.sb("t2_sb", [128, TW], BF16)

    with nc.Block() as block:
        cx = Ctx(nc, block)
        for s in ("ph", "pe", "sc", "ve", "ldsm", "st_y", "cvb0", "cvb1", "cvb2", "cvb3", "cob",
                  "st_ag0", "st_ag1", "st_wp0", "st_wp1", "st_y0", "st_y1",
                  "ccg0", "ccg1", "ccg2", "ccg3", "ccwp",
                  "trs_q0", "trs_q1", "trs_w0", "trs_w1", "ld_wo2"):
            cx.sem(s)
        for s in ("ld_x", "cast", "cwp", "trs_ln", "ld_wv", "ld_wo",
                  "ld_wgu", "ld_wp", "ld_xf", "ld_pt", "st_py"):
            cx.sem(s + "0")
            cx.sem(s + "1")

        NGQ, NPQ = 8, 4
        wgu_buf = al.sb("wgu_buf", [128, 2, 2, E], BF16)
        wp_buf = al.sb("wp_buf", [128, 2, 4096], BF16)
        pos_sb = al.sb("pos_sb", [128, 4096], BF16)
        neg_sb = al.sb("neg_sb", [128, 4096], BF16)
        red_sb = al.sb("red_sb", [128, 64], F32)
        scl_sb = al.sb("scl_sb", [128, 64], BF16)
        thr_sb = al.sb("thr_sb", [128, 64], F32)
        nthr_sb = al.sb("nthr_sb", [128, 64], F32)
        tt_sb = al.sb("tt_sb", [128, 2, 16, 128], BF16)
        tt_wp = al.sb("tt_wp", [128, 2, 32, 128], BF16)
        xt_sb = al.sb("xt_sb", [128, 2, E], F32)
        xnt_sb = al.sb("xnt_sb", [128, 2, E], BF16)
        sum_sb = al.sb("sum_sb", [128, 2, 1], F32)
        ssq_sb = al.sb("ssq_sb", [128, 2, 1], F32)
        mean_sb = al.sb("mean_sb", [128, 2, 1], F32)
        var_sb = al.sb("var_sb", [128, 2, 1], F32)
        sd_sb = al.sb("sd_sb", [128, 2, 1], F32)
        inv_sb = al.sb("inv_sb", [128, 2, 1], F32)
        nm_sb = al.sb("nm_sb", [128, 2, 1], F32)

        def xn_sl(ec, off, sz):
            if off < W:
                assert off + sz <= W
                return xn_wm[:, ec, off:off + sz]
            return xn_mn[:, ec, off - W:off - W + sz]

        dd = {}
        NTS = TW // 128

        def quant_ops(cx, v, wbuf, G):
            view = wbuf.rearrange("p (g k) -> p g k", k=64)
            rd = red_sb[:, 0:G]
            sc_ = scl_sb[:, 0:G]
            th = thr_sb[:, 0:G]
            nth = nthr_sb[:, 0:G]
            pv = pos_sb[:, 0:G * 64].rearrange("p (g k) -> p g k", k=64)
            nv = neg_sb[:, 0:G * 64].rearrange("p (g k) -> p g k", k=64)
            cx.inc("ve", v.tensor_reduce(rd, view, AX.X, OP.add,
                                          apply_absolute_value=True))
            cx.sw(v, "vector", "ve")
            cx.inc("ve", v.tensor_scalar(sc_, rd, 1.0 / 64.0, None, OP.mult))
            cx.sw(v, "vector", "ve")
            cx.inc("ve", v.tensor_scalar_max(sc_, sc_, BF16_EPS8))
            cx.sw(v, "vector", "ve")
            v.tensor_scalar(th, sc_, THR, None, OP.mult)
            cx.inc("ve", v.tensor_scalar(nth, sc_, -THR, None, OP.mult))
            cx.sw(v, "vector", "ve")
            v.tensor_tensor(pv, view, th.unsqueeze(2).broadcast_to([128, G, 64]),
                            OP.is_gt)
            cx.inc("ve", v.tensor_tensor(
                nv, view, nth.unsqueeze(2).broadcast_to([128, G, 64]), OP.is_lt))
            cx.sw(v, "vector", "ve")
            cx.inc("ve", v.tensor_tensor(pv, pv, nv, OP.subtract))
            cx.sw(v, "vector", "ve")
            return v.tensor_tensor(view, pv,
                                   sc_.unsqueeze(2).broadcast_to([128, G, 64]),
                                   OP.mult)

        def g_prep(cx, g):
            i = cx.inc
            i("ldsm", g.dma_start(out=wq1_sb[:, :, :],
                                  in_=wq1.rearrange("(a p) m -> p a m", p=128)), 16)
            i("ldsm", g.dma_start(out=wk1_sb[:, :, :],
                                  in_=wk1.rearrange("(a p) m -> p a m", p=128)), 16)
            i("ldsm", g.dma_start(out=wq2_sb[:, :], in_=wq2), 16)
            i("ldsm", g.dma_start(out=wk2_sb[:, :], in_=wk2), 16)
            i("ldsm", g.dma_start(out=nrm_sb[:, :],
                                  in_=nrm.rearrange("(a p) -> p a", p=128),
                                  allow_slow_non_contiguous=True), 16)
            dd["ldsm"] = cx.cnt["ldsm"]
            for q in range(NGQ):
                p = q % 2
                if q >= 2:
                    cx.wait(g, "gpsimd", "trs_q0", dd.get(("gu_trs0", q - 2)))
                    cx.wait(g, "gpsimd", "trs_q1", dd.get(("gu_trs1", q - 2)))
                i(f"cast{p}", g.dma_start(
                    out=wgu_buf[:, p, :, :],
                    in_=wgu_sl[q * 256:(q + 1) * 256, :]
                        .rearrange("(hc p) d -> p hc d", p=128)), 16)
            for q in range(4):
                p = q % 2
                if q >= 2:
                    cx.wait(g, "gpsimd", "trs_w0", dd.get(("wp_trs0", q - 2)))
                    cx.wait(g, "gpsimd", "trs_w1", dd.get(("wp_trs1", q - 2)))
                dcl, half = q // 2, q % 2
                i(f"cwp{p}", g.dma_start(
                    out=wp_buf[:, p, :],
                    in_=wp_sl[dcl * 128:(dcl + 1) * 128,
                              half * 4096:(half + 1) * 4096]), 16)


        def v_memset(cx, v):
            if not cx.dry:
                v.memset(ones_c[:, :], 1.0)
                v.memset(ones_r[:, :], 1.0)
                v.memset(epsc[:, :], LN_EPS)
                v.memset(cst[0:1, 0:1], ATTN_EPS)
                v.memset(cst[0:1, 1:2], 1.0 / C)
                v.memset(cst[0:1, 2:3], FFN_EPS)
            dd["msets"] = cx.inc("ve", v.memset(cst[0:1, 3:4], 1.0 / H))

        def v_prep(cx, v):
            def ln_recip(ts):
                if ts >= NTS:
                    return
                p = ts % 2
                cx.wait(v, "vector", "sc", dd.get(("ln_sd", ts)))
                if ts >= 2:
                    cx.wait(v, "vector", "sc", dd.get(("ln_xn", ts - 2)))
                dd[("ln_inv", ts)] = cx.inc(
                    "ve", v.reciprocal(inv_sb[:, p, :], sd_sb[:, p, :]))
            for q in range(NGQ):
                p = q % 2
                cx.wait(v, "vector", f"cast{p}", 16 * (q // 2 + 1))
                dd[("gu_q", q)] = cx.inc(
                    "ve", quant_ops(cx, v, wgu_buf[:, p, :, :]
                                    .rearrange("p a d -> p (a d)"), 64))
                ln_recip(q)
            for q in range(4):
                p = q % 2
                cx.wait(v, "vector", f"cwp{p}", 16 * (q // 2 + 1))
                dd[("wp_q", q)] = cx.inc("ve", quant_ops(cx, v, wp_buf[:, p, :], 64))
                ln_recip(NGQ + q)

        def s_prep(cx, s):
            def lntr(ts):
                pl = ts % 2
                cx.wait(s, "sync", "sc", dd[("ln_xn", ts)])
                nw = W // 128
                if ts < nw:
                    dst = xn_wm[:, :, ts * 128:(ts + 1) * 128]
                else:
                    dst = xn_mn[:, :, (ts - nw) * 128:(ts - nw + 1) * 128]
                cx.inc(f"trs_ln{pl}", s.dma_start_transpose(
                    dst,
                    xnt_sb[:, pl, :].rearrange("p (ec r) -> p ec r", r=128)), 16)
                dd[("ln_tr", ts)] = cx.cnt[f"trs_ln{pl}"]
            for q in range(NGQ):
                p = q % 2
                lntr(q)
                cx.wait(s, "sync", "ve", dd[("gu_q", q)])
                for i2 in range(2):
                    hcl = q * 2 + i2
                    P, col = hcl // 4, hcl % 4
                    pt = hcl % 2
                    if hcl >= 2:
                        cx.wait(s, "sync", f"st_ag{pt}", dd.get(("gu_st", hcl - 2)))
                    cx.inc(f"trs_q{pt}", s.dma_start_transpose(
                        tt_sb[:, pt, :, :],
                        wgu_buf[:, p, i2, :].rearrange("p (dc r) -> p dc r", r=128)),
                        16)
                    dd[("gu_trs" + str(pt), q)] = cx.cnt[f"trs_q{pt}"]
                    cx.wait(s, "sync", f"trs_q{pt}", cx.cnt[f"trs_q{pt}"])
                    dd[("gu_st", hcl)] = cx.inc(f"st_ag{pt}", s.dma_start(
                        out=ag_gu_in[P].ap()[col * 128:(col + 1) * 128, :]
                            .rearrange("p (ec h) -> p ec h", h=128),
                        in_=tt_sb[:, pt, :, :]), 16)
                    if hcl % 4 == 3:
                        dd[("gu_ag0", P)] = cx.cnt["st_ag0"]
                        dd[("gu_ag1", P)] = cx.cnt["st_ag1"]
            lntr(NGQ)
            for q in range(4):
                p = q % 2
                dcl, half = q // 2, q % 2
                pt = q % 2
                cx.wait(s, "sync", "ve", dd[("wp_q", q)])
                if q >= 2:
                    cx.wait(s, "sync", f"st_wp{pt}", dd.get(("wp_st", q - 2)))
                cx.inc(f"trs_w{pt}", s.dma_start_transpose(
                    tt_wp[:, pt, :, :],
                    wp_buf[:, p, :].rearrange("p (hc r) -> p hc r", r=128)), 16)
                dd[("wp_trs" + str(pt), q)] = cx.cnt[f"trs_w{pt}"]
                cx.wait(s, "sync", f"trs_w{pt}", cx.cnt[f"trs_w{pt}"])
                dd[("wp_st", q)] = cx.inc(f"st_wp{pt}", s.dma_start(
                    out=ag_wp_in.ap()[dcl * 128:(dcl + 1) * 128,
                                      half * 4096:(half + 1) * 4096]
                        .rearrange("p (hc d) -> p hc d", d=128),
                    in_=tt_wp[:, pt, :, :]), 16)
                dd[("wp_ag0", q)] = cx.cnt["st_wp0"]
                dd[("wp_ag1", q)] = cx.cnt["st_wp1"]

        def sc_ln(cx, e):
            def ln_a(ts):
                p = ts % 2
                cx.wait(e, "scalar", f"ld_x{p}", dd[("ln_x", ts)])
                if ts >= 2:
                    cx.wait(e, "scalar", f"trs_ln{p}", dd.get(("ln_tr", ts - 2)))
                    cx.wait(e, "scalar", "ve", dd.get(("ln_inv", ts - 2)))
                cx.inc("sc", e.activation(
                    out=xnt_sb[:, p, :], in_=xt_sb[:, p, :], func=AF.Square,
                    accum_out=ssq_sb[:, p, :]))
                cx.sw(e, "scalar", "sc")
                cx.inc("sc", e.activation(
                    out=xnt_sb[:, p, :], in_=xt_sb[:, p, :], func=AF.Copy,
                    accum_out=sum_sb[:, p, :]))
                cx.sw(e, "scalar", "sc")
                cx.inc("sc", e.mul(mean_sb[:, p, :], sum_sb[:, p, :], 1.0 / E))
                cx.inc("sc", e.mul(var_sb[:, p, :], ssq_sb[:, p, :], 1.0 / E))
                cx.sw(e, "scalar", "sc")
                cx.inc("sc", e.activation(out=sd_sb[:, p, :],
                                          in_=mean_sb[:, p, :], func=AF.Square))
                cx.sw(e, "scalar", "sc")
                cx.inc("sc", e.mul(sd_sb[:, p, :], sd_sb[:, p, :], -1.0))
                cx.sw(e, "scalar", "sc")
                cx.inc("sc", e.add(var_sb[:, p, :], var_sb[:, p, :],
                                   sd_sb[:, p, :]))
                cx.sw(e, "scalar", "sc")
                dd[("ln_sd", ts)] = cx.inc("sc", e.activation(
                    out=sd_sb[:, p, :], in_=var_sb[:, p, :], func=AF.Sqrt,
                    bias=epsc[:, 0:1]))
            def ln_b(ts):
                p = ts % 2
                cx.wait(e, "scalar", "ve", dd.get(("ln_inv", ts)))
                cx.inc("sc", e.mul(nm_sb[:, p, :], mean_sb[:, p, :],
                                   inv_sb[:, p, :]))
                cx.sw(e, "scalar", "sc")
                cx.inc("sc", e.mul(nm_sb[:, p, :], nm_sb[:, p, :], -1.0))
                cx.sw(e, "scalar", "sc")
                dd[("ln_xn", ts)] = cx.inc("sc", e.activation(
                    out=xnt_sb[:, p, :], in_=xt_sb[:, p, :], func=AF.Identity,
                    bias=nm_sb[:, p, :], scale=inv_sb[:, p, :]))
            cx.wait(e, "scalar", "ve", dd["msets"])
            for ts in range(NTS):
                p = ts % 2
                if ts >= 2:
                    ln_b(ts - 2)
                    cx.wait(e, "scalar", "sc", dd.get(("ln_xn", ts - 2)))
                dd[("ln_x", ts)] = cx.inc(f"ld_x{p}", e.dma_start(
                    out=xt_sb[:, p, :], in_=x_tm[ts * 128:(ts + 1) * 128, :]), 16)
                if ts >= 1:
                    ln_a(ts - 1)
            ln_a(NTS - 1)
            ln_b(NTS - 2)
            ln_b(NTS - 1)

        cx._tag = "p0ln"
        cx.phase([("vector", v_memset), ("scalar", sc_ln),
                  ("vector", v_prep), ("sync", s_prep),
                  ("gpsimd", g_prep)])

        xn_rdy0 = cx.cnt["trs_ln0"]
        xn_rdy1 = cx.cnt["trs_ln1"]

        if stage == "xn":
            def s_dump(cx, s):
                cx.wait(s, "sync", "trs_ln0", xn_rdy0)
                cx.wait(s, "sync", "trs_ln1", xn_rdy1)
                for ec in range(ECN):
                    cx.inc("st_y", s.dma_start(
                        out=dbg_xn[ec * 128:(ec + 1) * 128, 0:W],
                        in_=xn_wm[:, ec, :]), 16)
                    cx.inc("st_y", s.dma_start(
                        out=dbg_xn[ec * 128:(ec + 1) * 128, W:TW],
                        in_=xn_mn[:, ec, :]), 16)
                if not cx.dry:
                    s.wait_ge(cx.sems["st_y"], cx.cnt["st_y"])
            cx.phase([("sync", s_dump)])
            cx.close_sems()
            return nc

        cx.barrier("b1", extra={
            "sync": ["trs_ln0", "trs_ln1", "st_ag0", "st_ag1", "st_wp0",
                     "st_wp1", "trs_q0", "trs_q1", "trs_w0", "trs_w1"],
            "gpsimd": ["ldsm", "cast0", "cast1", "cwp0", "cwp1"],
            "scalar": ["ld_x0", "ld_x1"]})
        al.free("wgu_buf", "wp_buf", "pos_sb", "neg_sb", "red_sb", "scl_sb",
                "thr_sb", "nthr_sb", "tt_sb", "tt_wp", "xt_sb", "xnt_sb",
                "sum_sb", "ssq_sb", "mean_sb", "var_sb", "sd_sb", "inv_sb",
                "nm_sb")

        outw = al.sb("outw", [128, CCN, T], BF16)
        wv_t = al.sb("wv_t", [128, 2, ECN, 128], BF16)
        wo_t = al.sb("wo_t", [128, 3, CCN, 128], BF16)
        rowa = al.sb("rowa", [1, T], F32)
        rowb = al.sb("rowb", [1, T], F32)
        rms_a = al.sb("rms_a", [128, T], F32)
        f_sb = al.sb("f_sb", [128, 2, 512], F32)
        a_sb = al.sb("a_sb", [128, 2, 512], F32)
        k_sb = al.sb("k_sb", [128, 2, 512], F32)
        b_sb = al.sb("b_sb", [128, 2, 512], F32)
        h_sb = al.sb("h_sb", [128, 2, 512], F32)
        q_sb = al.sb("q_sb", [128, 2, 512], F32)
        o_sb = al.sb("o_sb", [128, 2, 512], F32)
        sq_sb = al.sb("sq_sb", [128, 2, 512], BF16)

        logf_ps = al.ps("logf_ps", [128, 2, 512], F32)
        v_ps = al.ps("v_ps", [128, 2, 512], F32)
        q_ps = al.ps("q_ps", [128, 2, 512], F32)
        stat_ps = al.ps("stat_ps", [1, T], F32)

        steps = [(cc, tpi) for cc in range(CCN) for tpi in range(3)]
        qsteps = set(s for s in steps if TPS[s[1]][2])

        def pe_t12(cx, t):
            cx.wait(t, "tensor", "ldsm", dd["ldsm"])
            cx.wait(t, "tensor", "trs_ln0", xn_rdy0)
            cx.wait(t, "tensor", "trs_ln1", xn_rdy1)
            gi = 0
            for (name, wsb, pieces) in (
                    ("t2", wk1_sb, TPS),
                    ("t1", wq1_sb, [TPS[1], TPS[2]])):
                for pi, (off, sz, _) in enumerate(pieces):
                    pg = gi % 2
                    if gi >= 2:
                        cx.wait(t, "tensor", "ve", dd.get(("t12ev", gi - 2)))
                    ps = logf_ps[:, pg, 0:sz]
                    for ec in range(ECN):
                        m = t.matmul(ps, wsb[:, ec, :], xn_sl(ec, off, sz),
                                     start=(ec == 0), stop=(ec == ECN - 1))
                    dd[("t12mm", gi)] = cx.inc("pe", m)
                    gi += 1

        def v_t12(cx, v):
            gi = 0
            for (name, dst, pieces) in (
                    ("t2", t2_sb, TPS),
                    ("t1", t1_sb, [TPS[1], TPS[2]])):
                for pi, (off, sz, _) in enumerate(pieces):
                    pg = gi % 2
                    cx.wait(v, "vector", "pe", dd[("t12mm", gi)])
                    doff = off if name == "t2" else off - W
                    dd[("t12ev", gi)] = cx.inc("ve", v.tensor_copy(
                        dst[:, doff:doff + sz], logf_ps[:, pg, 0:sz]))
                    dd[(name + "ev", pi)] = dd[("t12ev", gi)]
                    gi += 1

        def g_post(cx, g):
            i = cx.inc
            def ag(P):
                if P < NPQ:
                    cx.wait(g, "gpsimd", "st_ag0", dd[("gu_ag0", P)])
                    cx.wait(g, "gpsimd", "st_ag1", dd[("gu_ag1", P)])
                    i(f"ccg{P}", g.collective_compute(
                        "AllGather", OP.bypass,
                        replica_groups=[list(range(NCORES))],
                        ins=[ag_gu_in[P].ap().opt()],
                        outs=[ag_gu_out[P].ap().opt()]), 1)
                else:
                    cx.wait(g, "gpsimd", "st_wp0", dd[("wp_ag0", 3)])
                    cx.wait(g, "gpsimd", "st_wp1", dd[("wp_ag1", 3)])
                    i("ccwp", g.collective_compute(
                        "AllGather", OP.bypass,
                        replica_groups=[list(range(NCORES))],
                        ins=[ag_wp_in.ap().opt()],
                        outs=[ag_wp_out.ap().opt()]), 1)
            for cc in range(CCN):
                i(f"cvb{cc // 8}", g.dma_start(
                    out=wv_b.ap()[cc * 128:(cc + 1) * 128, :]
                        .rearrange("p (ec c) -> p ec c", c=128),
                    in_=wv[:, cc * 128:(cc + 1) * 128]
                        .rearrange("(ec p) c -> p ec c", p=128)), 16)
            for dc in range(DCN):
                i("cob", g.dma_start(
                    out=wo_b.ap()[dc * 128:(dc + 1) * 128, :]
                        .rearrange("p (cc d) -> p cc d", d=128),
                    in_=wo[:, dc * 128:(dc + 1) * 128]
                        .rearrange("(cc p) d -> p cc d", p=128)), 16)
            for P in range(NPQ + 1):
                ag(P)

        def s_wv(cx, s):
            for cc in range(CCN):
                p = cc % 2
                cx.wait(s, "sync", f"cvb{cc // 8}", 128)
                if cc >= 2:
                    cx.wait(s, "sync", "pe", dd.get(("vmm", cc - 2, 2)))
                dd[("wvld", cc)] = cx.inc(f"ld_wv{p}", s.dma_start(
                    out=wv_t[:, p, :, :],
                    in_=wv_b.ap()[cc * 128:(cc + 1) * 128, :]
                        .rearrange("p (ec c) -> p ec c", c=128)), 16)

        def pe_cc(cx, t):
            last_par = {}
            cx.wait(t, "tensor", "ve", dd.get(("t12ev", 4)))
            si = 0
            pend = []
            for cc in range(CCN):
                pwv = cc % 2
                cx.wait(t, "tensor", f"ld_wv{pwv}", dd[("wvld", cc)])
                for tpi, (off, sz, hasq) in enumerate(TPS):
                    s = (cc, tpi)
                    p = si % 2
                    if si >= 2:
                        cx.wait(t, "tensor", "sc", dd.get(("f", steps[si - 2])))
                    cx.wait(t, "tensor", "ve", dd[("t2ev", tpi)])
                    m = t.matmul(logf_ps[:, p, 0:sz],
                                 wk2_sb[:, cc * 128:(cc + 1) * 128],
                                 t2_sb[:, off:off + sz], start=True, stop=True)
                    dd[("logf", s)] = cx.inc("pe", m)
                    if si >= 2:
                        cx.wait(t, "tensor", "ve", dd.get(("b", steps[si - 2])))
                    for ec in range(ECN):
                        m = t.matmul(v_ps[:, p, 0:sz], wv_t[:, pwv, ec, :],
                                     xn_sl(ec, off, sz),
                                     start=(ec == 0), stop=(ec == ECN - 1))
                    dd[("vmm", cc, tpi)] = cx.inc("pe", m)
                    if hasq:
                        prevq = last_par.get(("qps", p))
                        if prevq is not None:
                            cx.wait(t, "tensor", "ve", dd.get(("o", prevq)))
                        last_par[("qps", p)] = s
                        cx.wait(t, "tensor", "ve", dd[("t1ev", tpi - 1)])
                        m = t.matmul(q_ps[:, p, 0:sz],
                                     wq2_sb[:, cc * 128:(cc + 1) * 128],
                                     t1_sb[:, off - W:off - W + sz],
                                     start=True, stop=True)
                        dd[("qmm", s)] = cx.inc("pe", m)
                        pend.append((s, p))
                    if len(pend) > 2:
                        stat_emit(cx, t, *pend.pop(0))
                    si += 1
            for it in pend:
                stat_emit(cx, t, *it)

        def stat_emit(cx, t, s, p):
            cc, tpi = s
            off = TPS[tpi][0] - W
            cx.wait(t, "tensor", "ve", dd.get(("sq", s)))
            m = t.matmul(stat_ps[0:1, off:off + 512], ones_c[:, :],
                         sq_sb[:, p, :], start=(cc == 0), stop=(cc == CCN - 1))
            dd[("stat", s)] = cx.inc("pe", m)

        def sc_cc(cx, e):
            last_par = {}
            si = 0
            pend = []
            for cc in range(CCN):
                for tpi, (off, sz, hasq) in enumerate(TPS):
                    s = (cc, tpi)
                    p = si % 2
                    cx.wait(e, "scalar", "pe", dd[("logf", s)])
                    if si >= 2 and steps[si - 2] in qsteps:
                        cx.wait(e, "scalar", "ve", dd.get(("o", steps[si - 2])))
                    dd[("f", s)] = cx.inc("sc", e.activation(
                        out=f_sb[:, p, 0:sz], in_=logf_ps[:, p, 0:sz],
                        func=AF.Sigmoid))
                    cx.sw(e, "scalar", "sc")
                    if si >= 2:
                        cx.wait(e, "scalar", "ve", dd.get(("scan", steps[si - 2])))
                    dd[("a", s)] = cx.inc("sc", e.activation(
                        out=a_sb[:, p, 0:sz], in_=f_sb[:, p, 0:sz],
                        func=AF.Sigmoid))
                    if hasq:
                        cx.wait(e, "scalar", "pe", dd[("qmm", s)])
                        prevq = last_par.get(("qsb", p))
                        if prevq is not None:
                            cx.wait(e, "scalar", "ve", dd.get(("o1", prevq)))
                        last_par[("qsb", p)] = s
                        dd[("sig", s)] = cx.inc("sc", e.activation(
                            out=q_sb[:, p, 0:sz], in_=q_ps[:, p, 0:sz],
                            func=AF.Sigmoid))
                    si += 1

        def v_cc(cx, v):
            last_par = {}
            si = 0
            for cc in range(CCN):
                for tpi, (off, sz, hasq) in enumerate(TPS):
                    s = (cc, tpi)
                    p = si % 2
                    cx.wait(v, "vector", "sc", dd[("a", s)])
                    cx.inc("ve", v.tensor_scalar(k_sb[:, p, 0:sz],
                                                 a_sb[:, p, 0:sz],
                                                 -1.0, 1.0, OP.mult, OP.add))
                    cx.sw(v, "vector", "ve")
                    cx.wait(v, "vector", "pe", dd[("vmm", cc, tpi)])
                    dd[("b", s)] = cx.inc("ve", v.tensor_tensor(
                        b_sb[:, p, 0:sz], k_sb[:, p, 0:sz], v_ps[:, p, 0:sz],
                        OP.mult))
                    cx.sw(v, "vector", "ve")
                    if tpi == 0:
                        init = 0.0
                    else:
                        psz = TPS[tpi - 1][1]
                        init = h_sb[:, 1 - p, psz - 1:psz]
                    dd[("scan", s)] = cx.inc("ve", v.tensor_tensor_scan(
                        h_sb[:, p, 0:sz], a_sb[:, p, 0:sz], b_sb[:, p, 0:sz],
                        init, OP.mult, OP.add))
                    cx.sw(v, "vector", "ve")
                    if hasq:
                        cx.wait(v, "vector", "sc", dd[("sig", s)])
                        dd[("o1", s)] = cx.inc("ve", v.tensor_tensor(
                            f_sb[:, p, 0:sz], q_sb[:, p, 0:sz],
                            h_sb[:, p, 0:sz], OP.mult))
                        cx.sw(v, "vector", "ve")
                        dd[("o", s)] = cx.inc("ve", v.tensor_tensor(
                            o_sb[:, p, 0:sz], f_sb[:, p, 0:sz],
                            q_ps[:, p, 0:sz], OP.mult))
                        cx.sw(v, "vector", "ve")
                        prevq = last_par.get(("sqb", p))
                        if prevq is not None:
                            cx.wait(v, "vector", "pe", dd.get(("stat", prevq)))
                        last_par[("sqb", p)] = s
                        dd[("sq", s)] = cx.inc("ve", v.tensor_tensor(
                            sq_sb[:, p, :], o_sb[:, p, 0:sz], o_sb[:, p, 0:sz],
                            OP.mult))
                        dd[("outw", s)] = cx.inc("ve", v.tensor_scalar(
                            outw[:, cc, off - W:off - W + sz], o_sb[:, p, 0:sz],
                            nrm_sb[:, cc:cc + 1], None, OP.mult))
                    si += 1

        cx._tag = "attn"
        cx.phase([("gpsimd", g_post),
                  ("tensor", pe_t12), ("vector", v_t12), ("sync", s_wv),
                  ("tensor", pe_cc), ("scalar", sc_cc), ("vector", v_cc)])

        if stage == "attn":
            def v_dump(cx, v):
                cx.wait(v, "vector", "pe", cx.cnt["pe"])
                dd["strow"] = cx.inc("ve", v.tensor_copy(rowa[0:1, :],
                                                         stat_ps[0:1, :]))
            def s_dumpa(cx, s):
                cx.wait(s, "sync", "ve", cx.cnt["ve"])
                for cc in range(CCN):
                    cx.inc("st_y", s.dma_start(
                        out=dbg_ow[cc * 128:(cc + 1) * 128, :],
                        in_=outw[:, cc, :]), 16)
                cx.inc("st_y", s.dma_start(out=dbg_st, in_=rowa[0:1, :]), 16)
                if not cx.dry:
                    s.wait_ge(cx.sems["st_y"], cx.cnt["st_y"])
            cx.phase([("vector", v_dump), ("sync", s_dumpa)])
            cx.close_sems()
            return nc

        def v_rmsa1(cx, v):
            cx.wait(v, "vector", "pe", cx.cnt["pe"])
            dd["rowa"] = cx.inc("ve", v.tensor_copy(rowa[0:1, :],
                                                    stat_ps[0:1, :]))
        def sc_rmsa(cx, e):
            cx.wait(e, "scalar", "ve", dd["rowa"])
            dd["rowsd"] = cx.inc("sc", e.activation(
                out=rowb[0:1, :], in_=rowa[0:1, :], func=AF.Sqrt,
                bias=cst[0:1, 0:1], scale=cst[0:1, 1:2]))
        def v_rmsa2(cx, v):
            cx.wait(v, "vector", "sc", dd["rowsd"])
            dd["rowinv"] = cx.inc("ve", v.reciprocal(rowa[0:1, :], rowb[0:1, :]))
        cx._tag = "rmsa"
        cx.phase([("vector", v_rmsa1), ("scalar", sc_rmsa),
                  ("vector", v_rmsa2)])

        al.free("logf_ps", "v_ps", "q_ps", "stat_ps")
        al.free("f_sb", "a_sb", "k_sb", "b_sb", "h_sb", "q_sb", "o_sb",
                "sq_sb")
        xf_t = al.sb("xf_t", [128, 2, T], F32)
        py_t = al.sb("py_t", [128, 2, T], F32)
        psum_a = al.ps("psum_a", [128, 3, T], F32)
        bc_ps = al.ps("bc_ps", [128, T], F32)

        def pe_bca(cx, t):
            cx.wait(t, "tensor", "ve", cx.cnt["ve"])
            cx.wait(t, "tensor", "sc", cx.cnt["sc"])
            for half in range(2):
                m = t.matmul(bc_ps[:, half * 512:(half + 1) * 512], ones_r[:, :],
                             rowa[0:1, half * 512:(half + 1) * 512],
                             start=True, stop=True)
            dd["bca"] = cx.inc("pe", m)
        def v_bca(cx, v):
            cx.wait(v, "vector", "pe", dd["bca"])
            dd["rmsabc"] = cx.inc("ve", v.tensor_copy(rms_a[:, :], bc_ps[:, :]))
        cx._tag = "rmsa"
        cx.phase([("tensor", pe_bca), ("vector", v_bca)])

        def wo_load(cx, s, dc):
            p = dc % 3
            if dc >= 3:
                cx.wait(s, "sync", "pe", dd.get(("atmm", dc - 3)))
            dd[("wold", dc)] = cx.inc(f"ld_wo{p}", s.dma_start(
                out=wo_t[:, p, :, :],
                in_=wo_b.ap()[dc * 128:(dc + 1) * 128, :]
                    .rearrange("p (cc d) -> p cc d", d=128)), 16)

        def s_atail(cx, s):
            cx.wait(s, "sync", "cob", 16 * DCN)
            wo_load(cx, s, 0)
            wo_load(cx, s, 1)
            wo_load(cx, s, 2)
            cx.wait(s, "sync", "ve", dd.get("rowinv"))
            def load(dc):
                p = dc % 2
                if dc >= 2:
                    cx.wait(s, "sync", "ve", dd.get(("aev", dc - 2)))
                dd[("xfld", dc)] = cx.inc(f"ld_xf{p}", s.dma_start(
                    out=xf_t[:, p, :], in_=x_fm[dc * 128:(dc + 1) * 128, :]), 16)
            load(0)
            load(1)
            for dc in range(DCN):
                cx.wait(s, "sync", "ve", dd.get(("aev", dc)))
                cx.inc(f"st_py{dc % 2}", s.dma_start(
                    out=partial.ap()[dc * 128:(dc + 1) * 128, :],
                    in_=py_t[:, dc % 2, :]), 16)
                if dc + 3 < DCN:
                    wo_load(cx, s, dc + 3)
                if dc + 2 < DCN:
                    load(dc + 2)

        def pe_at(cx, t):
            for dc in range(DCN):
                p = dc % 3
                cx.wait(t, "tensor", f"ld_wo{p}", dd.get(("wold", dc)))
                if dc >= 3:
                    cx.wait(t, "tensor", "ve", dd.get(("aev", dc - 3)))
                for cc in range(CCN):
                    for half in range(2):
                        m = t.matmul(psum_a[:, p, half * 512:(half + 1) * 512],
                                     wo_t[:, p, cc, :],
                                     outw[:, cc, half * 512:(half + 1) * 512],
                                     start=(cc == 0), stop=(cc == CCN - 1))
                dd[("atmm", dc)] = cx.inc("pe", m)

        def v_at(cx, v):
            cx.wait(v, "vector", "ve", dd.get("rmsabc"))
            for dc in range(DCN):
                p = dc % 2
                pp = dc % 3
                cx.wait(v, "vector", "pe", dd[("atmm", dc)])
                cx.wait(v, "vector", f"ld_xf{p}", dd.get(("xfld", dc)))
                if dc >= 2:
                    cx.wait(v, "vector", f"st_py{p}", 16 * (dc // 2))
                cx.inc("ve", v.tensor_tensor(py_t[:, p, :], psum_a[:, pp, :],
                                             rms_a[:, :], OP.mult))
                cx.sw(v, "vector", "ve")
                dd[("aev", dc)] = cx.inc("ve", v.tensor_tensor(
                    py_t[:, p, :], py_t[:, p, :], xf_t[:, p, :], OP.add))

        cx._tag = "atail"
        cx.phase([("tensor", pe_at), ("vector", v_at), ("sync", s_atail)])
        pt_done0 = cx.cnt["st_py0"]
        pt_done1 = cx.cnt["st_py1"]

        if stage == "partial":
            def s_dumpp(cx, s):
                cx.wait(s, "sync", "st_py0", pt_done0)
                cx.wait(s, "sync", "st_py1", pt_done1)
                cx.inc("st_y", s.dma_start(out=dbg_pt, in_=partial.ap()), 16)
                if not cx.dry:
                    s.wait_ge(cx.sems["st_y"], cx.cnt["st_y"])
            cx.phase([("sync", s_dumpp)])
            cx.close_sems()
            return nc

        cx.barrier("b2", extra={
            "sync": ["st_py0", "st_py1", "ld_xf0", "ld_xf1", "ld_wv0",
                     "ld_wv1", "ld_wo0", "ld_wo1", "ld_wo2"]})
        al.free("outw", "wv_t", "wo_t", "xf_t", "py_t",
                "psum_a", "bc_ps", "rowa", "rowb", "rms_a",
                "wq1_sb", "wk1_sb", "wq2_sb", "wk2_sb", "t1_sb", "t2_sb",
                "xn_wm")

        hn = al.sb("hn", [128, HCN, T], BF16, side="right")
        sqtmp = al.sb("sqtmp", [128, T], F32, side="right")
        sqacc = al.sb("sqacc", [128, T], F32, side="right")
        wgu_t = al.sb("wgu_t", [128, 2, 2, ECN, 128], BF16)
        g_sb = al.sb("g_sb", [128, T], F32)
        gate_ps = al.ps("gate_ps", [128, 2, T], F32)
        up_ps = al.ps("up_ps", [128, 2, T], F32)

        pairs = [(P, r, k) for P in range(4) for r in range(4) for k in range(4)]

        def s_gul(cx, s):
            for j, (P, r, k) in enumerate(pairs):
                p = j % 2
                cx.wait(s, "sync", f"ccg{P}", 1)
                if j >= 2:
                    cx.wait(s, "sync", "pe", dd.get(("gumm", j - 2)))
                cx.inc(f"ld_wgu{p}", s.dma_start(
                    out=wgu_t[:, p, 0, :, :],
                    in_=ag_gu_out[P].ap()[r * 512 + k * 128:r * 512 + (k + 1) * 128, :]
                        .rearrange("p (ec h) -> p ec h", h=128)), 16)
                dd[("guld", j)] = cx.inc(f"ld_wgu{p}", s.dma_start(
                    out=wgu_t[:, p, 1, :, :],
                    in_=ag_gu_out[P].ap()[(r + 4) * 512 + k * 128:
                                          (r + 4) * 512 + (k + 1) * 128, :]
                        .rearrange("p (ec h) -> p ec h", h=128)), 16)

        def pe_gu(cx, t):
            for j in range(len(pairs)):
                p = j % 2
                cx.wait(t, "tensor", f"ld_wgu{p}", dd[("guld", j)])
                if j >= 2:
                    cx.wait(t, "tensor", "ve", dd.get(("hn", j - 2)))
                    cx.wait(t, "tensor", "sc", dd.get(("gsilu", j - 2)))
                for ec in range(ECN):
                    for g in range(2):
                        ps = gate_ps if g == 0 else up_ps
                        for half in range(2):
                            m = t.matmul(ps[:, p, half * 512:(half + 1) * 512],
                                         wgu_t[:, p, g, ec, :],
                                         xn_mn[:, ec,
                                               half * 512:(half + 1) * 512],
                                         start=(ec == 0), stop=(ec == ECN - 1))
                dd[("gumm", j)] = cx.inc("pe", m)

        def sc_gu(cx, e):
            for j in range(len(pairs)):
                p = j % 2
                cx.wait(e, "scalar", "pe", dd[("gumm", j)])
                if j >= 1:
                    cx.wait(e, "scalar", "ve", dd.get(("hn", j - 1)))
                dd[("gsilu", j)] = cx.inc("sc", e.activation(
                    out=g_sb[:, :], in_=gate_ps[:, p, :], func=AF.Silu))

        def v_gu(cx, v):
            for j, (P, r, k) in enumerate(pairs):
                p = j % 2
                hc = r * 16 + P * 4 + k
                cx.wait(v, "vector", "sc", dd[("gsilu", j)])
                cx.wait(v, "vector", "pe", dd[("gumm", j)])
                dd[("hn", j)] = cx.inc("ve", v.tensor_tensor(
                    hn[:, hc, :], g_sb[:, :], up_ps[:, p, :], OP.mult))
                cx.sw(v, "vector", "ve")
                if j == 0:
                    cx.inc("ve", v.tensor_tensor(
                        sqacc[:, :], hn[:, hc, :], hn[:, hc, :], OP.mult))
                else:
                    cx.inc("ve", v.tensor_tensor(
                        sqtmp[:, :], hn[:, hc, :], hn[:, hc, :], OP.mult))
                    cx.sw(v, "vector", "ve")
                    cx.inc("ve", v.tensor_tensor(
                        sqacc[:, :], sqacc[:, :], sqtmp[:, :], OP.add))

        cx._tag = "gu"
        cx.phase([("sync", s_gul), ("tensor", pe_gu), ("scalar", sc_gu),
                  ("vector", v_gu)])

        cx.barrier("b3", extra={"sync": ["ld_wgu0", "ld_wgu1"]})
        al.free("wgu_t", "g_sb", "gate_ps", "up_ps", "xn_mn")

        rowa = al.sb("rowa2", [1, T], F32)
        rowb = al.sb("rowb2", [1, T], F32)
        rms_f = al.sb("rms_f", [128, T], F32)
        sqf_sb = al.sb("sqf_sb", [128, T], BF16)
        statf_ps = al.ps("statf_ps", [1, T], F32)

        def v_sqf(cx, v):
            cx.sw(v, "vector", "ve")
            dd["sqfb"] = cx.inc("ve", v.tensor_copy(sqf_sb[:, :], sqacc[:, :]))

        def pe_sqf(cx, t):
            cx.wait(t, "tensor", "ve", dd["sqfb"])
            for half in range(2):
                m = t.matmul(statf_ps[0:1, half * 512:(half + 1) * 512],
                             ones_c[:, :],
                             sqf_sb[:, half * 512:(half + 1) * 512],
                             start=True, stop=True)
            dd[("sfmm", 0)] = cx.inc("pe", m)

        def v_rmsf1(cx, v):
            cx.wait(v, "vector", "pe", cx.cnt["pe"])
            dd["rowf"] = cx.inc("ve", v.tensor_copy(rowa[0:1, :],
                                                    statf_ps[0:1, :]))
        def sc_rmsf(cx, e):
            cx.wait(e, "scalar", "ve", dd["rowf"])
            dd["rowfsd"] = cx.inc("sc", e.activation(
                out=rowb[0:1, :], in_=rowa[0:1, :], func=AF.Sqrt,
                bias=cst[0:1, 2:3], scale=cst[0:1, 3:4]))
        def v_rmsf2(cx, v):
            cx.wait(v, "vector", "sc", dd["rowfsd"])
            cx.inc("ve", v.reciprocal(rowa[0:1, :], rowb[0:1, :]))
        cx._tag = "statf"
        cx.phase([("vector", v_sqf), ("tensor", pe_sqf),
                  ("vector", v_rmsf1), ("scalar", sc_rmsf),
                  ("vector", v_rmsf2)])

        al.free("statf_ps", "sqf_sb")
        psum_f = al.ps("psum_f", [128, 2, T], F32)
        bcf_ps = al.ps("bcf_ps", [128, T], F32)

        def pe_bcf(cx, t):
            cx.wait(t, "tensor", "ve", cx.cnt["ve"])
            for half in range(2):
                m = t.matmul(bcf_ps[:, half * 512:(half + 1) * 512], ones_r[:, :],
                             rowa[0:1, half * 512:(half + 1) * 512],
                             start=True, stop=True)
            dd["bcf"] = cx.inc("pe", m)
        def v_bcf(cx, v):
            cx.wait(v, "vector", "pe", dd["bcf"])
            cx.inc("ve", v.tensor_copy(rms_f[:, :], bcf_ps[:, :]))
        cx._tag = "statf"
        cx.phase([("tensor", pe_bcf), ("vector", v_bcf)])
        rmsf_rdy = cx.cnt["ve"]

        wp_t = al.sb("wp_t", [128, 2, HCN, 128], BF16)
        pt_t = al.sb("pt_t", [128, 2, T], F32)
        yv_t = al.sb("yv_t", [128, 2, T], F32)

        def s_ftail(cx, s):
            cx.wait(s, "sync", "ccwp", 1)
            cx.wait(s, "sync", "st_py0", pt_done0)
            cx.wait(s, "sync", "st_py1", pt_done1)
            cx.wait(s, "sync", "pe", dd.get("bcf"))
            cx.wait(s, "sync", "ve", rmsf_rdy)
            def load(dc):
                p = dc % 2
                if dc >= 2:
                    cx.wait(s, "sync", "pe", dd.get(("ftmm", dc - 2)))
                rr, dcl = dc // 2, dc % 2
                dd[("wpld", dc)] = cx.inc(f"ld_wp{p}", s.dma_start(
                    out=wp_t[:, p, :, :],
                    in_=ag_wp_out.ap()[rr * 256 + dcl * 128:
                                       rr * 256 + (dcl + 1) * 128, :]
                        .rearrange("p (hc d) -> p hc d", d=128)), 16)
                if dc >= 2:
                    cx.wait(s, "sync", "ve", dd.get(("fev", dc - 2)))
                dd[("ptld", dc)] = cx.inc(f"ld_pt{p}", s.dma_start(
                    out=pt_t[:, p, :],
                    in_=partial.ap()[dc * 128:(dc + 1) * 128, :]), 16)
            load(0)
            load(1)
            for dc in range(DCN):
                cx.wait(s, "sync", "ve", dd.get(("fev", dc)))
                cx.inc(f"st_y{dc % 2}", s.dma_start(
                    out=y[dc * 128:(dc + 1) * 128, :],
                    in_=yv_t[:, dc % 2, :]), 16)
                if dc + 2 < DCN:
                    load(dc + 2)
            if not cx.dry:
                s.wait_ge(cx.sems["st_y0"], cx.cnt["st_y0"])
                s.wait_ge(cx.sems["st_y1"], cx.cnt["st_y1"])

        def pe_ft(cx, t):
            for dc in range(DCN):
                p = dc % 2
                cx.wait(t, "tensor", f"ld_wp{p}", dd.get(("wpld", dc)))
                if dc >= 2:
                    cx.wait(t, "tensor", "ve", dd.get(("fev", dc - 2)))
                for hc in range(HCN):
                    for half in range(2):
                        m = t.matmul(psum_f[:, p, half * 512:(half + 1) * 512],
                                     wp_t[:, p, hc, :],
                                     hn[:, hc, half * 512:(half + 1) * 512],
                                     start=(hc == 0), stop=(hc == HCN - 1))
                dd[("ftmm", dc)] = cx.inc("pe", m)

        def v_ft(cx, v):
            cx.sw(v, "vector", "ve")
            for dc in range(DCN):
                p = dc % 2
                cx.wait(v, "vector", "pe", dd[("ftmm", dc)])
                cx.wait(v, "vector", f"ld_pt{p}", dd.get(("ptld", dc)))
                if dc >= 2:
                    cx.wait(v, "vector", f"st_y{p}", 16 * (dc // 2))
                cx.inc("ve", v.tensor_tensor(yv_t[:, p, :], psum_f[:, p, :],
                                             rms_f[:, :], OP.mult))
                cx.sw(v, "vector", "ve")
                dd[("fev", dc)] = cx.inc("ve", v.tensor_tensor(
                    yv_t[:, p, :], yv_t[:, p, :], pt_t[:, p, :], OP.add))

        cx._tag = "ftail"
        cx.phase([("tensor", pe_ft), ("vector", v_ft),
                  ("sync", s_ftail)])
        cx.close_sems()
    return nc


def make_in_maps(inputs):
    X = np.asarray(inputs["L0_attn_x"], dtype=np.float32)
    ws = {k: np.asarray(inputs["L0_attn_" + k], dtype=np.float32)
          for k in ("Wq1", "Wq2", "Wk1", "Wk2", "Wv", "Wo", "norm_w")}
    Wgu = np.asarray(inputs["L0_ffn_w_gate_up"], dtype=np.float32)
    Wp = np.asarray(inputs["L0_ffn_w_proj"], dtype=np.float32)
    in_maps = []
    for c in range(NCORES):
        b, hf = c // 2, c % 2
        seq = X[b]
        main = seq[hf * T:(hf + 1) * T]
        warm = np.zeros((W, E), np.float32) if hf == 0 else seq[T - W:T]
        in_maps.append({
            "x_tm": np.ascontiguousarray(np.concatenate([warm, main], axis=0)),
            "x_fm": np.ascontiguousarray(main.T),
            "wq1": ws["Wq1"], "wq2": ws["Wq2"], "wk1": ws["Wk1"],
            "wk2": ws["Wk2"], "wv": ws["Wv"], "wo": ws["Wo"],
            "nrm": ws["norm_w"],
            "wgu_sl": np.ascontiguousarray(
                Wgu[c * (GU // NCORES):(c + 1) * (GU // NCORES)]),
            "wp_sl": np.ascontiguousarray(
                Wp[c * (D // NCORES):(c + 1) * (D // NCORES)]),
        })
    return in_maps


def kernel(**inputs):
    nc = build_program("full")
    in_maps = make_in_maps(inputs)
    res = run_bass_kernel_spmd(nc, in_maps, core_ids=list(range(NCORES)))
    out = np.empty((4, 2048, E), np.float32)
    for c in range(NCORES):
        b, hf = c // 2, c % 2
        out[b, hf * T:(hf + 1) * T, :] = res.results[c]["y"].T
    return out
